# version 01
# baseline (speedup 1.0000x reference)
import numpy as np
import ml_dtypes
import concourse.bass as bass
import concourse.mybir as mybir
from concourse.bass_utils import run_bass_kernel_spmd
from contextlib import ExitStack

F32 = mybir.dt.float32
BF16 = mybir.dt.bfloat16
U8 = mybir.dt.uint8
AF = mybir.ActivationFunctionType
OP = mybir.AluOpType

E = 2048
T = 2048
HT = 1024
ET = 16
TC = 4
CH = 512
H = 16
D = 128
MA = 1024
KV = 4
NI = 16384
NT = 128
NCORE = 8
ISD = float(D) ** -0.5
C1P = 2.0 * 0.7978845608028654
C2P = 2.0 * 0.7978845608028654 * 0.044715

STAGE = 4


def perm256(n):
    return np.arange(n).reshape(-1, 128, 2).transpose(0, 2, 1).reshape(-1)


def build_program():
    nc = bass.Bass("TRN2", target_bir_lowering=False)

    def din(name, shape, dt=F32):
        return nc.dram_tensor(name, shape, dt, kind="ExternalInput").ap()

    xT = din("xT", [E, T])
    winT = din("winT", [E, E])
    wqa = din("wqa", [D, D])
    wka = din("wka", [D, D])
    wva = din("wva", [D, D])
    patT = din("patT", [H * D, MA])
    nqw_c = din("nqw_c", [D, ET])
    nrw_c = din("nrw_c", [D, ET])
    wqb = din("wqb", [E, E])
    wkb = din("wkb", [E, KV * D])
    wvb = din("wvb", [E, KV * D])
    wob = din("wob", [E, E])
    gqv = din("gqv", [D, 1])
    gkv = din("gkv", [D, 1])
    gcodes = din("gcodes", [NI // 16, T], U8)
    ucodes = din("ucodes", [NI // 16, T], U8)
    dcodes = din("dcodes", [NI // 16, T], U8)
    gscl = din("gscl", [NI // 16, T])
    uscl = din("uscl", [NI // 16, T])
    dscl = din("dscl", [NI // 16, T])
    kc = din("kc", [D, 8])
    beta_in = din("beta_in", [1, 1])
    pmask = din("pmask", [1, 1])
    mtri = din("mtri", [D, 4 * CH])
    onesf_r = din("onesf_r", [D, D])
    onesf_c = din("onesf_c", [D, 1])
    onesb_c = din("onesb_c", [D, 1], BF16)

    yT = nc.dram_tensor("yT", [E, T], F32, kind="ExternalOutput").ap()

    def dint(name, shape, dt):
        return nc.dram_tensor(name, shape, dt, kind="Internal").ap()

    xparkA = dint("xparkA", [E, T], F32)
    xparkB = dint("xparkB", [E, T], F32)
    ypark = dint("ypark", [E, T], F32)
    interp = dint("interp", [NT, D, T], BF16)
    wg_sh = dint("wg_sh", [16 * ET * D, D], BF16)
    wu_sh = dint("wu_sh", [16 * ET * D, D], BF16)
    wd_sh = dint("wd_sh", [16 * D, T], BF16)
    wg_all = dint("wg_all", [NT * ET * D, D], BF16)
    wu_all = dint("wu_all", [NT * ET * D, D], BF16)
    wd_all = dint("wd_all", [NT * D, T], BF16)
    kag_in = dint("kag_in", [KV * D, T], BF16)
    kag_out = dint("kag_out", [2 * KV * D, T], BF16)
    vag_in = dint("vag_in", [T, KV * D], BF16)
    vag_out = dint("vag_out", [2 * T, KV * D], BF16)

    es = ExitStack()
    SB = es.enter_context

    R1 = SB(nc.sbuf_tensor("R1", [D, T], F32))
    R2 = SB(nc.sbuf_tensor("R2", [D, T], F32))
    R3 = SB(nc.sbuf_tensor("R3", [D, T], F32))
    R4 = SB(nc.sbuf_tensor("R4", [D, T], F32))
    kc_s = SB(nc.sbuf_tensor("kc_s", [D, 8], F32))
    nqc_s = SB(nc.sbuf_tensor("nqc_s", [D, ET], F32))
    nrc_s = SB(nc.sbuf_tensor("nrc_s", [D, ET], F32))
    gq_s = SB(nc.sbuf_tensor("gq_s", [D, 1], F32))
    gqsc_s = SB(nc.sbuf_tensor("gqsc_s", [D, 1], F32))
    gk_s = SB(nc.sbuf_tensor("gk_s", [D, 1], F32))
    beta_s = SB(nc.sbuf_tensor("beta_s", [1, 4], F32))
    pm_s = SB(nc.sbuf_tensor("pm_s", [1, 1], F32))
    km_s = SB(nc.sbuf_tensor("km_s", [1, 8], F32))
    onfr_s = SB(nc.sbuf_tensor("onfr_s", [D, D], F32))
    onfc_s = SB(nc.sbuf_tensor("onfc_s", [D, 1], F32))
    onbc_s = SB(nc.sbuf_tensor("onbc_s", [D, 1], BF16))
    sigb_s = SB(nc.sbuf_tensor("sigb_s", [D, 1], F32))
    wqa_s = SB(nc.sbuf_tensor("wqa_s", [D, D], BF16))
    wka_s = SB(nc.sbuf_tensor("wka_s", [D, D], BF16))
    wva_s = SB(nc.sbuf_tensor("wva_s", [D, D], BF16))

    psA = SB(nc.psum_tensor("psA", [D, T], F32))
    psB = SB(nc.psum_tensor("psB", [D, T], F32))

    POOL = {"ld": [f"ld{i}" for i in range(6)],
            "g": [f"g{i}" for i in range(3)],
            "st": ["st0", "st1"],
            "mm": ["mm"], "dv": ["dv"], "ac": ["ac"], "cc": ["cc"]}
    ALLSEM = [n for v in POOL.values() for n in v]
    SEM = {n: SB(nc.semaphore("sem_" + n)) for n in ALLSEM}
    C = {n: 0 for n in ALLSEM}
    RR = {"ld": 0, "g": 0, "st": 0}

    def dma(qeng, out, in_, q):
        name = POOL[q][RR[q] % len(POOL[q])]
        RR[q] += 1
        qeng.wait_ge(SEM[name], C[name])
        r = qeng.dma_start(out=out, in_=in_)
        C[name] += 16
        r.then_inc(SEM[name], 16)
        return r

    def inc(instr, s, n=1):
        if s in ("dv", "ac"):
            return instr
        assert s in ("mm", "cc"), s
        C[s] += n
        instr.then_inc(SEM[s], n)
        return instr

    marks = {}

    def mark(name):
        marks[name] = dict(C)

    def wof(eng, name, *qs):
        for q in qs:
            for s in POOL[q]:
                eng.wait_ge(SEM[s], marks[name][s])

    def qgate():
        snap = dict(C)

        def _(sync):
            for sname in ("mm", "dv", "ac"):
                sync.wait_ge(SEM[sname], snap[sname])
            for sname in POOL["g"]:
                sync.wait_ge(SEM[sname], snap[sname])
        on_sync(_)

    xT_t = xT.rearrange("(i p) t -> i p t", p=D)
    xpa_t = xparkA.rearrange("(i p) t -> i p t", p=D)
    xpb_t = xparkB.rearrange("(i p) t -> i p t", p=D)
    yp_t = ypark.rearrange("(i p) t -> i p t", p=D)
    yT_t = yT.rearrange("(i p) t -> i p t", p=D)
    wg_sh_t = wg_sh.rearrange("(j i p) f -> j i p f", i=ET, p=D)
    wu_sh_t = wu_sh.rearrange("(j i p) f -> j i p f", i=ET, p=D)
    wd_sh_t = wd_sh.rearrange("(j p) t -> j p t", p=D)
    wg_all_t = wg_all.rearrange("(j i p) f -> j i p f", i=ET, p=D)
    wu_all_t = wu_all.rearrange("(j i p) f -> j i p f", i=ET, p=D)
    wd_all_t = wd_all.rearrange("(j p) t -> j p t", p=D)
    kin_t = kag_in.rearrange("(k p) t -> k p t", p=D)
    kout_t = kag_out.rearrange("(r k p) t -> r k p t", k=KV, p=D)

    blk = es.enter_context(nc.Block())

    class ESync:
        OPS = {"tensor_scalar", "tensor_tensor", "scalar_tensor_tensor", "tensor_copy",
               "reciprocal", "tensor_reduce", "activation", "copy", "sign", "sqrt",
               "square", "mul", "add", "tensor_add", "tensor_mul", "tensor_sub",
               "select", "tensor_tensor_scan"}

        def __init__(self, eng, key):
            self._e = eng
            self._k = key

        def __getattr__(self, n):
            a = getattr(self._e, n)
            if n in ESync.OPS:
                def f(*args, **kw):
                    self._e.wait_ge(SEM[self._k], C[self._k])
                    r = a(*args, **kw)
                    C[self._k] += 1
                    r.then_inc(SEM[self._k], 1)
                    return r
                return f
            return a

    def on_v(f):
        blk.vector(lambda v: f(ESync(v, "dv")))

    def on_s(f):
        blk.scalar(lambda sc: f(ESync(sc, "ac")))

    on_sync, on_g, on_t = blk.sync, blk.gpsimd, blk.tensor

    def p0():
        p0s = ExitStack()
        wsc_s = p0s.enter_context(nc.sbuf_tensor("wsc_s", [D, 3 * D], F32))
        def _(sync):
            for dst, src in ((kc_s, kc), (nqc_s, nqw_c), (nrc_s, nrw_c),
                             (gq_s, gqv), (gk_s, gkv), (pm_s, pmask),
                             (onfr_s, onesf_r), (onfc_s, onesf_c),
                             (onbc_s, onesb_c)):
                dma(sync, dst[:], src, "ld")
            dma(sync, beta_s[0:1, 0:1], beta_in, "ld")
            dma(sync, wsc_s[:, 0:D], wqa, "ld")
            dma(sync, wsc_s[:, D:2 * D], wka, "ld")
            dma(sync, wsc_s[:, 2 * D:3 * D], wva, "ld")
        on_sync(_)
        mark("p0_ld")

        def _(s):
            wof(s, "p0_ld", "ld")
            s.copy(out=wqa_s[:], in_=wsc_s[:, 0:D])
            s.copy(out=wka_s[:], in_=wsc_s[:, D:2 * D])
            s.copy(out=wva_s[:], in_=wsc_s[:, 2 * D:3 * D])
            inc(s.activation(out=beta_s[0:1, 1:2], in_=beta_s[0:1, 0:1], func=AF.Exp, scale=-1.0), "ac")
        on_s(_)
        mark("p0_exp")

        def _(v):
            wof(v, "p0_ld", "ld")
            v.tensor_scalar(out=gqsc_s[:], in0=gq_s[:], scalar1=ISD, scalar2=None, op0=OP.mult)
            wof(v, "p0_exp", "ac")
            v.tensor_scalar(out=beta_s[0:1, 2:3], in0=beta_s[0:1, 1:2], scalar1=1.0, scalar2=None, op0=OP.add)
            inc(v.reciprocal(out=beta_s[0:1, 3:4], in_=beta_s[0:1, 2:3]), "dv")
        on_v(_)
        mark("p0_sigb")

        def _(t):
            wof(t, "p0_sigb", "dv")
            inc(t.matmul(psA[0:D, 0:1], onfr_s[0:1, :], beta_s[0:1, 3:4], start=True, stop=True), "mm")
        on_t(_)
        mark("p0_bcb")

        def _(s):
            wof(s, "p0_bcb", "mm")
            inc(s.copy(out=sigb_s[:], in_=psA[0:D, 0:1]), "ac")
        on_s(_)
        mark("p0_done")
        p0s.close()

    def p1():
        qgate()
        p1s = ExitStack()
        P = p1s.enter_context
        cod_b = [P(nc.sbuf_tensor(f"cod{b}", [D, T], U8)) for b in range(2)]
        scl_b = [P(nc.sbuf_tensor(f"sclb{b}", [D, T], F32)) for b in range(2)]
        dq_t = P(nc.sbuf_tensor("dq_t", [D, T], U8))
        dq_sgn = P(nc.sbuf_tensor("dq_sgn", [D, T], U8))
        dq_sg = P(nc.sbuf_tensor("dq_sg", [D, T], F32))
        dq_r1 = P(nc.sbuf_tensor("dq_r1", [D, T], F32))
        dq_r2 = P(nc.sbuf_tensor("dq_r2", [D, T], F32))
        dq_m = P(nc.sbuf_tensor("dq_m", [D, T], F32))
        dq_w = [P(nc.sbuf_tensor(f"dq_w{b}", [D, T], BF16)) for b in range(2)]

        parts = [("g", gcodes, gscl), ("u", ucodes, uscl), ("d", dcodes, dscl)]
        for ip in range(24):
            part, codes_d, scl_d = parts[ip // 8]
            ci = ip % 8
            b = ip % 2

            def _(sync, ip=ip, b=b, codes_d=codes_d, scl_d=scl_d, ci=ci):
                if ip >= 2:
                    sync.wait_ge(SEM["dv"], marks[f"decw_{ip-2}_1"]["dv"])
                dma(sync, cod_b[b][:], codes_d[128 * ci:128 * ci + 128, :], "ld")
                dma(sync, scl_b[b][:], scl_d[128 * ci:128 * ci + 128, :], "ld")
            on_sync(_)
            mark(f"decld_{ip}")

            for half in range(2):
                def _(v, ip=ip, b=b, half=half):
                    if half == 0:
                        wof(v, f"decld_{ip}", "ld")
                        if ip >= 1:
                            v.wait_ge(SEM["ac"], marks[f"decact_{ip-1}_1"]["ac"])
                    else:
                        v.wait_ge(SEM["ac"], marks[f"decact_{ip}_0"]["ac"])
                    if half == 0:
                        v.tensor_scalar(out=dq_t[:], in0=cod_b[b][:], scalar1=7, scalar2=None, op0=OP.bitwise_and)
                        v.tensor_scalar(out=dq_sgn[:], in0=cod_b[b][:], scalar1=8, scalar2=None, op0=OP.bitwise_and)
                    else:
                        v.tensor_scalar(out=dq_t[:], in0=cod_b[b][:], scalar1=4, scalar2=7,
                                        op0=OP.logical_shift_right, op1=OP.bitwise_and)
                        v.tensor_scalar(out=dq_sgn[:], in0=cod_b[b][:], scalar1=4, scalar2=8,
                                        op0=OP.logical_shift_right, op1=OP.bitwise_and)
                    inc(v.tensor_scalar(out=dq_sg[:], in0=dq_sgn[:], scalar1=-0.25, scalar2=1.0,
                                        op0=OP.mult, op1=OP.add), "dv")
                on_v(_)
                mark(f"dect_{ip}_{half}")

                def _(s, ip=ip, half=half):
                    wof(s, f"dect_{ip}_{half}", "dv")
                    s.activation(out=dq_r1[:], in_=dq_t[:], func=AF.Relu, scale=0.5, bias=kc_s[:, 0:1])
                    inc(s.activation(out=dq_r2[:], in_=dq_t[:], func=AF.Relu, bias=kc_s[:, 1:2]), "ac")
                on_s(_)
                mark(f"decact_{ip}_{half}")

                def _(v, ip=ip, b=b, half=half):
                    wof(v, f"decact_{ip}_{half}", "ac")
                    v.scalar_tensor_tensor(out=dq_m[:], in0=dq_t[:], scalar=0.5, in1=dq_r1[:],
                                           op0=OP.mult, op1=OP.add)
                    v.tensor_tensor(out=dq_m[:], in0=dq_m[:], in1=dq_r2[:], op=OP.add)
                    v.tensor_tensor(out=dq_m[:], in0=dq_m[:], in1=dq_sg[:], op=OP.mult)
                    if ip >= 1:
                        [v.wait_ge(SEM[_s], marks[f"decst_{ip-1}"][_s]) for _s in POOL["g"]]
                    inc(v.tensor_tensor(out=dq_w[half][:], in0=dq_m[:], in1=scl_b[b][:], op=OP.mult), "dv")
                on_v(_)
                mark(f"decw_{ip}_{half}")

            def _(g, ip=ip, part=part, ci=ci):
                wof(g, f"decw_{ip}_1", "dv")
                if part in ("g", "u"):
                    dst = wg_sh_t if part == "g" else wu_sh_t
                    for half in range(2):
                        d_ap = dst[:, 2 * ci + half, :, :].rearrange("j p f -> p j f")
                        s_ap = dq_w[half].ap().rearrange("p (j f) -> p j f", f=D)
                        dma(g, d_ap, s_ap, "g")
                else:
                    for half in range(2):
                        dma(g, wd_sh_t[2 * ci + half], dq_w[half][:], "g")
            on_g(_)
            mark(f"decst_{ip}")
        mark("dec_all")

        def _(g):
            wof(g, "dec_all", "g")
            g.wait_ge(SEM["cc"], C["cc"])
            inc(g.collective_compute("AllGather", OP.bypass, replica_groups=[list(range(NCORE))],
                                     ins=[wg_sh.opt()], outs=[wg_all.opt()]), "cc")
            mark("wag_g")
            g.wait_ge(SEM["cc"], C["cc"])
            inc(g.collective_compute("AllGather", OP.bypass, replica_groups=[list(range(NCORE))],
                                     ins=[wu_sh.opt()], outs=[wu_all.opt()]), "cc")
            mark("wag_u")
            g.wait_ge(SEM["cc"], C["cc"])
            inc(g.collective_compute("AllGather", OP.bypass, replica_groups=[list(range(NCORE))],
                                     ins=[wd_sh.opt()], outs=[wd_all.opt()]), "cc")
            mark("wag_d")
        on_g(_)
        p1s.close()

    def stream_stats(name, src_fn, sb1, sb2, sq, add_fn=None, park_fn=None, xb=None,
                     xbmark=None):
        for i in range(ET):
            buf = (sb1, sb2)[i % 2] if src_fn else None
            src = src_fn(i) if src_fn else None
            dep = None
            if src is not None:
                def _(sync, i=i, buf=buf, src=src):
                    w = 1 if add_fn is not None else 2
                    if i >= w:
                        sync.wait_ge(SEM["mm"], marks[f"{name}_mm_{i-w}"]["mm"])
                        if park_fn is not None:
                            [sync.wait_ge(SEM[_s], marks[f"{name}_pk_{i-w}"][_s]) for _s in POOL["g"]]
                    dma(sync, buf[:], src, "ld")
                    if add_fn is not None:
                        dma(sync, sq[:], add_fn(i), "ld")
                on_sync(_)
                mark(f"{name}_ld_{i}")
                dep = (f"{name}_ld_{i}", "ld")

                if add_fn is not None:
                    def _(v, i=i, buf=buf):
                        wof(v, f"{name}_ld_{i}", "ld")
                        if i >= 1:
                            v.wait_ge(SEM["ac"], marks[f"{name}_sq_{i-1}"]["ac"])
                        inc(v.tensor_tensor(out=buf[:], in0=buf[:], in1=sq[:], op=OP.add), "dv")
                    on_v(_)
                    mark(f"{name}_add_{i}")
                    dep = (f"{name}_add_{i}", "dv")

                if park_fn is not None:
                    def _(g, i=i, buf=buf, dep=dep):
                        wof(g, dep[0], dep[1])
                        dma(g, park_fn(i), buf[:], "g")
                    on_g(_)
                    mark(f"{name}_pk_{i}")

            def _(s, i=i, buf=buf, src=src, dep=dep):
                if dep is not None:
                    wof(s, dep[0], dep[1])
                elif i == 0 and xbmark is not None:
                    wof(s, xbmark[0], xbmark[1])
                if i >= 1:
                    s.wait_ge(SEM["mm"], marks[f"{name}_mm_{i-1}"]["mm"])
                a = buf[:] if src is not None else xb[:, T * i:T * i + T]
                inc(s.activation(out=sq[:], in_=a, func=AF.Square), "ac")
            on_s(_)
            mark(f"{name}_sq_{i}")

            def _(t, i=i, buf=buf, src=src, dep=dep):
                if dep is not None:
                    wof(t, dep[0], dep[1])
                wof(t, f"{name}_sq_{i}", "ac")
                last = None
                for c in range(TC):
                    sl = slice(CH * c, CH * c + CH)
                    a = buf[:, sl] if src is not None else xb[:, T * i + CH * c:T * i + CH * c + CH]
                    last = t.matmul(psA[0:1, sl], onfc_s[:], a, start=(i == 0), stop=(i == ET - 1))
                for c in range(TC):
                    sl = slice(CH * c, CH * c + CH)
                    last = t.matmul(psB[0:1, sl], onfc_s[:], sq[:, sl], start=(i == 0), stop=(i == ET - 1))
                inc(last, "mm")
            on_t(_)
            mark(f"{name}_mm_{i}")
        mark(f"{name}_stats")

    def ln_rows(name, kind="ln"):
        def _(v):
            wof(v, f"{name}_stats", "mm")
            v.tensor_scalar(out=R2[0:1, 0:T], in0=psA[0:1, 0:T], scalar1=1.0 / E, scalar2=None, op0=OP.mult)
            v.tensor_scalar(out=R1[0:1, 0:T], in0=psB[0:1, 0:T], scalar1=1.0 / E, scalar2=None, op0=OP.mult)
            v.tensor_tensor(out=R3[0:1, 0:T], in0=R2[0:1, 0:T], in1=R2[0:1, 0:T], op=OP.mult)
            inc(v.tensor_tensor(out=R3[0:1, 0:T], in0=R1[0:1, 0:T], in1=R3[0:1, 0:T], op=OP.subtract), "dv")
        on_v(_)
        mark(f"{name}_var")

        def _(sc):
            wof(sc, f"{name}_var", "dv")
            sc.activation(out=R1[0:1, 0:T], in_=R3[0:1, 0:T], func=AF.Ln, bias=kc_s[0:1, 3:4])
            inc(sc.activation(out=R1[0:1, 0:T], in_=R1[0:1, 0:T], func=AF.Exp, scale=-0.5), "ac")
        on_s(_)
        mark(f"{name}_rstd")

        if kind == "lnq":
            def _(v):
                wof(v, f"{name}_rstd", "ac")
                v.tensor_scalar(out=R4[0:1, 0:T], in0=R3[0:1, 0:T], scalar1=1e-5, scalar2=None, op0=OP.add)
                v.reciprocal(out=R4[0:1, 0:T], in_=R4[0:1, 0:T])
                inc(v.tensor_tensor(out=R4[0:1, 0:T], in0=R3[0:1, 0:T], in1=R4[0:1, 0:T], op=OP.mult), "dv")
            on_v(_)
            mark(f"{name}_msql")

            def _(sc):
                wof(sc, f"{name}_msql", "dv")
                sc.activation(out=R4[0:1, 0:T], in_=R4[0:1, 0:T], func=AF.Ln, bias=kc_s[0:1, 2:3])
                inc(sc.activation(out=R4[0:1, 0:T], in_=R4[0:1, 0:T], func=AF.Exp, scale=-0.5), "ac")
            on_s(_)
            mark(f"{name}_rsq")

            def _(v):
                wof(v, f"{name}_rsq", "ac")
                v.tensor_tensor(out=R3[0:1, 0:T], in0=R1[0:1, 0:T], in1=R4[0:1, 0:T], op=OP.mult)
                inc(v.tensor_tensor(out=R4[0:1, 0:T], in0=R2[0:1, 0:T], in1=R3[0:1, 0:T], op=OP.mult), "dv")
            on_v(_)
        else:
            def _(v):
                wof(v, f"{name}_rstd", "ac")
                inc(v.tensor_tensor(out=R2[0:1, 0:T], in0=R2[0:1, 0:T], in1=R1[0:1, 0:T], op=OP.mult), "dv")
            on_v(_)
        mark(f"{name}_rows")

    def emit_lnx(name, rowsmark, src_fn, dst, sb1, sb2, sq, wcol=None, arow=None, brow=None):
        arow = arow if arow is not None else R1
        brow = brow if brow is not None else R2
        for i in range(ET):
            buf = (sb1, sb2)[i % 2]

            def _(sync, i=i, buf=buf):
                if i >= 2:
                    sync.wait_ge(SEM["dv"], marks[f"{name}_o_{i-2}"]["dv"])
                    sync.wait_ge(SEM["ac"], marks[f"{name}_o_{i-2}"]["ac"])
                dma(sync, buf[:], src_fn(i), "ld")
            on_sync(_)
            mark(f"{name}_ld_{i}")

            def _(t, i=i):
                wof(t, rowsmark, "dv")
                wof(t, f"{name}_ld_{i}", "ld")
                if i >= 1:
                    t.wait_ge(SEM["dv"], marks[f"{name}_o_{i-1}"]["dv"])
                last = None
                for c in range(TC):
                    sl = slice(CH * c, CH * c + CH)
                    t.matmul(psA[:, sl], onfr_s[0:1, 0:D], arow[0:1, sl], start=True, stop=True)
                    last = t.matmul(psB[:, sl], onfr_s[0:1, 0:D], brow[0:1, sl], start=True, stop=True)
                inc(last, "mm")
            on_t(_)
            mark(f"{name}_bc_{i}")

            def _(v, i=i, buf=buf):
                wof(v, f"{name}_bc_{i}", "mm")
                for c in range(TC):
                    sl = slice(CH * c, CH * c + CH)
                    v.tensor_tensor(out=sq[:, sl], in0=buf[:, sl], in1=psA[:, sl], op=OP.mult)
                last = None
                for c in range(TC):
                    sl = slice(CH * c, CH * c + CH)
                    if wcol is None:
                        last = v.tensor_tensor(out=dst[:, T * i + CH * c:T * i + CH * c + CH],
                                               in0=sq[:, sl], in1=psB[:, sl], op=OP.subtract)
                    else:
                        last = v.tensor_tensor(out=buf[:, sl], in0=sq[:, sl], in1=psB[:, sl], op=OP.subtract)
                inc(last, "dv")
            on_v(_)
            mark(f"{name}_vo_{i}")

            if wcol is not None:
                def _(sc, i=i, buf=buf):
                    wof(sc, f"{name}_vo_{i}", "dv")
                    inc(sc.activation(out=dst[:, T * i:T * i + T], in_=buf[:], func=AF.Copy,
                                      scale=wcol[:, i:i + 1]), "ac")
                on_s(_)
                mark(f"{name}_so_{i}")

            def noop():
                pass
            marks[f"{name}_o_{i}"] = dict(marks[f"{name}_so_{i}"] if wcol is not None else marks[f"{name}_vo_{i}"])
        mark(f"{name}_done")

    def p2():
        qgate()
        p2s = ExitStack()
        P = p2s.enter_context
        NQ = P(nc.sbuf_tensor("NQ", [D, ET * T], BF16))
        AFl = P(nc.sbuf_tensor("AFl", [D, ET * T], BF16))

        pre = ExitStack()
        sb1 = pre.enter_context(nc.sbuf_tensor("p2s1", [D, T], F32))
        sb2 = pre.enter_context(nc.sbuf_tensor("p2s2", [D, T], F32))
        sq = pre.enter_context(nc.sbuf_tensor("p2sq", [D, T], F32))
        stream_stats("axs", lambda i: xT_t[i], sb1, sb2, sq)
        ln_rows("axs", kind="lnq")
        qgate()
        emit_lnx("anq", "axs_rows", lambda i: xT_t[i], NQ, sb1, sb2, sq, wcol=nqc_s, arow=R3, brow=R4)
        if STAGE == 0:
            for i in range(ET):
                def _(sc, i=i):
                    wof(sc, "anq_done", "dv" if False else "ac")
                    if i >= 1:
                        [sc.wait_ge(SEM[_s], marks[f"st0_{i-1}"][_s]) for _s in POOL["g"]]
                    inc(sc.copy(out=sb1[:], in_=NQ[:, T * i:T * i + T]), "ac")
                on_s(_)
                mark(f"st0c_{i}")

                def _(g, i=i):
                    wof(g, f"st0c_{i}", "ac")
                    dma(g, xpa_t[i], sb1[:], "g")
                on_g(_)
                mark(f"st0_{i}")
            pre.close()
            p2s.close()
            return
        pre.close()

        qgate()
        hp_s = ExitStack()
        P2 = hp_s.enter_context
        wst = P2(nc.sbuf_tensor("awst", [D, ET * D // 2], F32))
        wih = [P2(nc.sbuf_tensor(f"awih{b}", [D, ET * D], BF16)) for b in range(2)]
        pats = P2(nc.sbuf_tensor("apats", [D, MA], F32))
        patb = [P2(nc.sbuf_tensor(f"apatb{b}", [D, MA], BF16)) for b in range(2)]
        projb = P2(nc.sbuf_tensor("aprojb", [D, T], BF16))
        qb = [P2(nc.sbuf_tensor(f"aqb{w}", [D, T], BF16)) for w in range(2)]
        kb = [P2(nc.sbuf_tensor(f"akb{w}", [D, MA], BF16)) for w in range(2)]
        vb = [P2(nc.sbuf_tensor(f"avb{w}", [D, MA], BF16)) for w in range(2)]
        pbuf = [P2(nc.sbuf_tensor(f"apb{w}", [D, CH], BF16)) for w in range(2)]
        rb = P2(nc.sbuf_tensor("arb", [D, CH], F32))
        svc = P2(nc.sbuf_tensor("asvc", [D, 2], F32))
        af1 = P2(nc.sbuf_tensor("aaf1", [D, CH], F32))

        for hp in range(H // 2):
            for hw in range(2):
                h = 2 * hp + hw

                for wh in range(2):
                    def _(sync, h=h, wh=wh):
                        if h + wh >= 1:
                            ph, pw = (h, 0) if wh == 1 else (h - 1, 1)
                            sync.wait_ge(SEM["ac"], marks[f"a_wcvt_{ph}_{pw}"]["ac"])
                        dma(sync, wst.ap().rearrange("p (i f) -> p i f", f=D),
                            winT[E // 2 * wh:E // 2 * (wh + 1), D * h:D * h + D].rearrange("(i p) f -> p i f", p=D), "ld")
                        if wh == 0:
                            dma(sync, pats[:], patT[D * h:D * h + D, :], "ld")
                    on_sync(_)
                    mark(f"a_wldh_{h}_{wh}")

                    def _(s, h=h, hw=hw, wh=wh):
                        wof(s, f"a_wldh_{h}_{wh}", "ld")
                        if h >= 2 and wh == 0:
                            s.wait_ge(SEM["mm"], marks[f"a_kv_{h-2}"]["mm"])
                        s.copy(out=wih[hw][:, ET * D // 2 * wh:ET * D // 2 * (wh + 1)], in_=wst[:])
                        if wh == 0:
                            inc(s.copy(out=patb[hw][:], in_=pats[:]), "ac")
                        else:
                            inc(s.copy(out=patb[hw][0:1, 0:1], in_=pats[0:1, 0:1]), "ac")
                    on_s(_)
                    mark(f"a_wcvt_{h}_{wh}")
                mark(f"a_wcvt_{h}")

                def _(t, h=h, hw=hw, hp=hp):
                    wof(t, f"a_wcvt_{h}_1", "ac")
                    wof(t, "anq_done", "dv")
                    if h >= 1:
                        t.wait_ge(SEM["ac"], marks[f"a_kv_ev_{h-1}"]["ac"])
                    if hp >= 1 and hw == 0:
                        t.wait_ge(SEM["dv"], marks[f"aat_{hp-1}_{TC-1}_fb"]["dv"])
                    last = None
                    for i in range(ET):
                        for c in range(TC):
                            sl = slice(CH * c, CH * c + CH)
                            last = t.matmul(psA[:, sl], wih[hw][:, D * i:D * i + D],
                                            NQ[:, T * i + CH * c:T * i + CH * c + CH],
                                            start=(i == 0), stop=(i == ET - 1))
                    inc(last, "mm")
                on_t(_)
                mark(f"a_proj_{h}")

                def _(s, h=h):
                    wof(s, f"a_proj_{h}", "mm")
                    if h >= 1:
                        s.wait_ge(SEM["mm"], marks[f"a_q_{h-1}"]["mm"])
                    inc(s.copy(out=projb[:], in_=psA[:, 0:T]), "ac")
                on_s(_)
                mark(f"a_projev_{h}")

                def _(t, h=h):
                    wof(t, f"a_projev_{h}", "ac")
                    if h >= 1:
                        t.wait_ge(SEM["ac"], marks[f"a_svc_{h-1}"]["ac"])
                    last = None
                    for c in range(TC):
                        sl = slice(CH * c, CH * c + CH)
                        last = t.matmul(psB[:, sl], wqa_s[:], projb[:, sl], start=True, stop=True)
                    inc(last, "mm")
                on_t(_)
                mark(f"a_q_{h}")

                def _(s, h=h, hw=hw):
                    wof(s, f"a_q_{h}", "mm")
                    inc(s.activation(out=qb[hw][:], in_=psB[:, 0:T], func=AF.Copy, scale=sigb_s[:]), "ac")
                on_s(_)
                mark(f"a_qev_{h}")

                def _(t, h=h, hw=hw):
                    wof(t, f"a_qev_{h}", "ac")
                    t.matmul(psA[:, 0:CH], wka_s[:], patb[hw][:, 0:CH], start=True, stop=True)
                    t.matmul(psA[:, CH:MA], wka_s[:], patb[hw][:, CH:MA], start=True, stop=True)
                    last = None
                    for mt in range(8):
                        last = t.matmul(psA[:, MA + D * mt:MA + D * mt + D],
                                        patb[hw][:, D * mt:D * mt + D], wva_s[:], start=True, stop=True)
                    inc(last, "mm")
                on_t(_)
                mark(f"a_kv_{h}")

                def _(s, h=h, hw=hw):
                    wof(s, f"a_kv_{h}", "mm")
                    s.copy(out=kb[hw][:], in_=psA[:, 0:MA])
                    inc(s.copy(out=vb[hw][:], in_=psA[:, MA:2 * MA]), "ac")
                on_s(_)
                mark(f"a_kv_ev_{h}")

                def _(t, h=h, hw=hw):
                    wof(t, f"a_kv_ev_{h}", "ac")
                    last = None
                    for par in range(2):
                        for mt in range(8):
                            last = t.matmul(psB[64 * hw:64 * hw + 64, 1024 + par:1024 + par + 1],
                                            vb[hw][:, D * mt + 64 * par:D * mt + 64 * par + 64],
                                            onbc_s[:], start=(mt == 0), stop=(mt == 7),
                                            tile_position=(0, 64 * hw))
                    inc(last, "mm")
                on_t(_)
                mark(f"a_sv_{h}")

                def _(s, h=h, hw=hw):
                    wof(s, f"a_sv_{h}", "mm")
                    inc(s.copy(out=svc[64 * hw:64 * hw + 64, 0:2], in_=psB[64 * hw:64 * hw + 64, 1024:1026]), "ac")
                on_s(_)
                mark(f"a_svc_{h}")

            for tc in range(TC):
                tsl = slice(CH * tc, CH * tc + CH)
                tag = f"aat_{hp}_{tc}"
                steps = [(hw, mt) for hw in range(2) for mt in range(8)]
                for si, (hw, mt) in enumerate(steps):
                    pp = si % 2

                    def _(t, si=si, hw=hw, mt=mt, pp=pp, tag=tag, tsl=tsl, hp=hp, tc=tc):
                        if si == 0:
                            wof(t, f"a_kv_ev_{2*hp+1}", "ac")
                            if tc > 0:
                                wof(t, f"aat_{hp}_{tc-1}_fb", "dv")
                        if si >= 2:
                            t.wait_ge(SEM["ac"], marks[f"{tag}_ex_{si-2}"]["ac"])
                        inc(t.matmul(psA[:, CH * pp:CH * pp + CH], kb[hw][:, D * mt:D * mt + D],
                                     qb[hw][:, tsl], start=True, stop=True), "mm")
                    on_t(_)
                    mark(f"{tag}_sc_{si}")

                    def _(s, si=si, pp=pp, tag=tag):
                        wof(s, f"{tag}_sc_{si}", "mm")
                        if si >= 2:
                            s.wait_ge(SEM["dv"], marks[f"{tag}_pd_{si-2}"]["dv"])
                        inc(s.activation(out=psB[:, 1024 + CH * pp:1024 + CH * pp + CH],
                                         in_=psA[:, CH * pp:CH * pp + CH], func=AF.Exp), "ac")
                    on_s(_)
                    mark(f"{tag}_ex_{si}")

                    def _(v, si=si, pp=pp, tag=tag):
                        wof(v, f"{tag}_ex_{si}", "ac")
                        if si >= 2:
                            v.wait_ge(SEM["mm"], marks[f"{tag}_da_{si-2}"]["mm"])
                        inc(v.tensor_scalar(out=pbuf[pp][:], in0=psB[:, 1024 + CH * pp:1024 + CH * pp + CH],
                                            scalar1=-1.0, scalar2=None, op0=OP.add), "dv")
                    on_v(_)
                    mark(f"{tag}_pd_{si}")

                    def _(t, si=si, hw=hw, mt=mt, pp=pp, tag=tag):
                        wof(t, f"{tag}_pd_{si}", "dv")
                        r = 32 * hw
                        t.matmul(psA[r:r + 1, MA:MA + CH], onbc_s[:], pbuf[pp][:],
                                 start=(mt == 0), stop=(mt == 7))
                        t.matmul(psA[64 * hw:64 * hw + 64, 1536:2048],
                                 vb[hw][:, D * mt:D * mt + 64], pbuf[pp][:],
                                 start=(mt == 0), stop=(mt == 7), tile_position=(0, 64 * hw))
                        inc(t.matmul(psB[64 * hw:64 * hw + 64, 0:CH],
                                     vb[hw][:, D * mt + 64:D * mt + 128], pbuf[pp][:],
                                     start=(mt == 0), stop=(mt == 7), tile_position=(0, 64 * hw)), "mm")
                    on_t(_)
                    mark(f"{tag}_da_{si}")

                def _(v, tag=tag):
                    wof(v, f"{tag}_da_15", "mm")
                    v.tensor_scalar(out=R4[0:1, 0:CH], in0=psA[0:1, MA:MA + CH], scalar1=float(MA), scalar2=None, op0=OP.add)
                    v.tensor_scalar(out=R4[0:1, CH:2 * CH], in0=psA[32:33, MA:MA + CH], scalar1=float(MA), scalar2=None, op0=OP.add)
                    v.reciprocal(out=R4[0:1, 0:CH], in_=R4[0:1, 0:CH])
                    inc(v.reciprocal(out=R4[0:1, CH:2 * CH], in_=R4[0:1, CH:2 * CH]), "dv")
                on_v(_)
                mark(f"{tag}_rd")

                def _(t, tag=tag):
                    wof(t, f"{tag}_rd", "dv")
                    t.matmul(psB[0:64, CH:2 * CH], onfr_s[0:1, 0:64], R4[0:1, 0:CH],
                             start=True, stop=True)
                    inc(t.matmul(psB[64:128, CH:2 * CH], onfr_s[0:1, 0:64], R4[0:1, CH:2 * CH],
                                 start=True, stop=True, tile_position=(0, 64)), "mm")
                on_t(_)
                mark(f"{tag}_rbc")

                def _(s, tag=tag):
                    wof(s, f"{tag}_rbc", "mm")
                    inc(s.copy(out=rb[:], in_=psB[:, CH:2 * CH]), "ac")
                on_s(_)
                mark(f"{tag}_rbs")

                def _(v, tag=tag, hp=hp, tc=tc):
                    wof(v, f"{tag}_rbs", "ac")
                    v.tensor_scalar(out=af1[:], in0=psA[:, 1536:2048], scalar1=svc[:, 0:1], scalar2=None, op0=OP.add)
                    v.tensor_tensor(out=AFl[:, T * (2 * hp) + CH * tc:T * (2 * hp) + CH * tc + CH],
                                    in0=af1[:], in1=rb[:], op=OP.mult)
                    v.tensor_scalar(out=af1[:], in0=psB[:, 0:CH], scalar1=svc[:, 1:2], scalar2=None, op0=OP.add)
                    inc(v.tensor_tensor(out=AFl[:, T * (2 * hp + 1) + CH * tc:T * (2 * hp + 1) + CH * tc + CH],
                                        in0=af1[:], in1=rb[:], op=OP.mult), "dv")
                on_v(_)
                mark(f"{tag}_fb")
        mark("a_attn_done")
        hp_s.close()

        if STAGE == 15:
            qgate()
            co15 = ExitStack()
            c15 = co15.enter_context(nc.sbuf_tensor("c15", [D, T], F32))
            for i in range(ET):
                def _(sc, i=i):
                    if i == 0:
                        wof(sc, "a_attn_done", "dv")
                    if i >= 1:
                        [sc.wait_ge(SEM[_s], marks[f"st15_{i-1}"][_s]) for _s in POOL["g"]]
                    inc(sc.copy(out=c15[:], in_=AFl[:, T * i:T * i + T]), "ac")
                on_s(_)
                mark(f"st15c_{i}")

                def _(g, i=i):
                    wof(g, f"st15c_{i}", "ac")
                    dma(g, xpa_t[i], c15[:], "g")
                on_g(_)
                mark(f"st15_{i}")
            co15.close()
            p2s.close()
            return

        qgate()
        co = ExitStack()
        cb1 = co.enter_context(nc.sbuf_tensor("acb1", [D, T], F32))
        cb2 = co.enter_context(nc.sbuf_tensor("acb2", [D, T], F32))
        cw1 = co.enter_context(nc.sbuf_tensor("acw1", [D, T], F32))
        csq = co.enter_context(nc.sbuf_tensor("acsq", [D, T], F32))

        for i in range(ET):
            def _(s, i=i):
                if i == 0:
                    wof(s, "a_attn_done", "dv")
                if i >= 1:
                    s.wait_ge(SEM["mm"], marks[f"afs_mm_{i-1}"]["mm"])
                inc(s.activation(out=csq[:], in_=AFl[:, T * i:T * i + T], func=AF.Square), "ac")
            on_s(_)
            mark(f"afs_sq_{i}")

            def _(t, i=i):
                wof(t, f"afs_sq_{i}", "ac")
                if i == 0:
                    wof(t, "a_attn_done", "dv")
                last = None
                for c in range(TC):
                    sl = slice(CH * c, CH * c + CH)
                    last = t.matmul(psA[0:1, sl], onfc_s[:], csq[:, sl], start=(i == 0), stop=(i == ET - 1))
                inc(last, "mm")
            on_t(_)
            mark(f"afs_mm_{i}")
        mark("afs_stats")

        def _(s):
            wof(s, "afs_stats", "mm")
            s.activation(out=R3[0:1, 0:T], in_=psA[0:1, 0:T], func=AF.Ln, scale=1.0 / E, bias=kc_s[0:1, 2:3])
            inc(s.activation(out=R3[0:1, 0:T], in_=R3[0:1, 0:T], func=AF.Exp, scale=-0.5), "ac")
        on_s(_)
        mark("afs_rr")

        def _(v):
            wof(v, "afs_rr", "ac")
            v.tensor_scalar(out=R4[0:1, 0:T], in0=R1[0:1, 0:T], scalar1=1.0, scalar2=None, op0=OP.add)
            inc(v.tensor_tensor(out=R2[0:1, 0:T], in0=R2[0:1, 0:T], in1=R1[0:1, 0:T], op=OP.mult), "dv")
        on_v(_)
        mark("acmb_rows")

        for i in range(ET):
            buf = (cb1, cb2)[i % 2]

            def _(sync, i=i, buf=buf):
                if i >= 2:
                    [sync.wait_ge(SEM[_s], marks[f"acmb_pk_{i-2}"][_s]) for _s in POOL["g"]]
                dma(sync, buf[:], xT_t[i], "ld")
            on_sync(_)
            mark(f"acmb_ld_{i}")

            def _(t, i=i):
                wof(t, "acmb_rows", "dv")
                wof(t, f"acmb_ld_{i}", "ld")
                if i >= 1:
                    t.wait_ge(SEM["dv"], marks[f"acmb_o_{i-1}"]["dv"])
                    t.wait_ge(SEM["ac"], marks[f"acmb_w_{i-1}"]["ac"])
                last = None
                for c in range(TC):
                    sl = slice(CH * c, CH * c + CH)
                    t.matmul(psA[:, sl], onfr_s[0:1, 0:D], R3[0:1, sl], start=True, stop=True)
                    last = t.matmul(psB[:, sl], onfr_s[0:1, 0:D], R4[0:1, sl], start=True, stop=True)
                inc(last, "mm")
            on_t(_)
            mark(f"acmb_bc1_{i}")

            def _(v, i=i, buf=buf):
                wof(v, f"acmb_bc1_{i}", "mm")
                for c in range(TC):
                    sl = slice(CH * c, CH * c + CH)
                    v.tensor_tensor(out=csq[:, sl], in0=AFl[:, T * i + CH * c:T * i + CH * c + CH],
                                    in1=psA[:, sl], op=OP.mult)
                last = None
                for c in range(TC):
                    sl = slice(CH * c, CH * c + CH)
                    last = v.tensor_tensor(out=buf[:, sl], in0=buf[:, sl], in1=psB[:, sl], op=OP.mult)
                inc(last, "dv")
            on_v(_)
            mark(f"acmb_s1_{i}")

            def _(sc, i=i):
                wof(sc, f"acmb_s1_{i}", "dv")
                inc(sc.activation(out=cw1[:], in_=csq[:], func=AF.Copy, scale=nrc_s[:, i:i + 1]), "ac")
            on_s(_)
            mark(f"acmb_w_{i}")

            def _(t, i=i):
                wof(t, f"acmb_s1_{i}", "dv")
                last = None
                for c in range(TC):
                    sl = slice(CH * c, CH * c + CH)
                    last = t.matmul(psA[:, sl], onfr_s[0:1, 0:D], R2[0:1, sl], start=True, stop=True)
                inc(last, "mm")
            on_t(_)
            mark(f"acmb_bc2_{i}")

            def _(v, i=i, buf=buf):
                wof(v, f"acmb_bc2_{i}", "mm")
                wof(v, f"acmb_w_{i}", "ac")
                for c in range(TC):
                    sl = slice(CH * c, CH * c + CH)
                    v.tensor_tensor(out=buf[:, sl], in0=buf[:, sl], in1=psA[:, sl], op=OP.subtract)
                last = None
                for c in range(TC):
                    sl = slice(CH * c, CH * c + CH)
                    last = v.tensor_tensor(out=buf[:, sl], in0=buf[:, sl], in1=cw1[:, sl], op=OP.add)
                inc(last, "dv")
            on_v(_)
            mark(f"acmb_o_{i}")

            def _(g, i=i, buf=buf):
                wof(g, f"acmb_o_{i}", "dv")
                dma(g, xpa_t[i], buf[:], "g")
            on_g(_)
            mark(f"acmb_pk_{i}")
        mark("p2_done")
        co.close()
        p2s.close()

    def ffn(call):
        f = f"f{call}"

        qgate()
        lnb = ExitStack()
        LNX = lnb.enter_context(nc.sbuf_tensor(f"{f}lnx", [D, ET * T], BF16))
        pre = ExitStack()
        sb1 = pre.enter_context(nc.sbuf_tensor(f"{f}s1", [D, T], F32))
        sb2 = pre.enter_context(nc.sbuf_tensor(f"{f}s2", [D, T], F32))
        sq = pre.enter_context(nc.sbuf_tensor(f"{f}sq", [D, T], F32))
        if call == 1:
            stream_stats(f + "s", lambda i: xpa_t[i], sb1, sb2, sq)
        else:
            stream_stats(f + "s", lambda i: xpb_t[i], sb1, sb2, sq,
                         add_fn=lambda i: yp_t[i], park_fn=lambda i: xpa_t[i])
        ln_rows(f + "s", kind="ln")

        qgate()
        emit_lnx(f + "l", f + "s_rows", lambda i: xpa_t[i], LNX, sb1, sb2, sq)
        pre.close()

        qgate()
        pa = ExitStack()
        PA = pa.enter_context
        wgt = [PA(nc.sbuf_tensor(f"{f}wg{b}", [D, ET * D], BF16)) for b in range(2)]
        wut = [PA(nc.sbuf_tensor(f"{f}wu{b}", [D, ET * D], BF16)) for b in range(2)]
        g1 = PA(nc.sbuf_tensor(f"{f}g1", [D, MA], F32))
        g2 = PA(nc.sbuf_tensor(f"{f}g2", [D, MA], F32))
        ib = [PA(nc.sbuf_tensor(f"{f}ib{b}", [D, MA], BF16)) for b in range(2)]

        for j in range(NT):
            bj = j % 2

            def _(sync, j=j, bj=bj):
                if j == 0:
                    wof(sync, "wag_u", "cc")
                if j >= 2:
                    sync.wait_ge(SEM["mm"], marks[f"{f}a_mm_{j-2}_1"]["mm"])
                dma(sync, wgt[bj].ap().rearrange("p (i f) -> p i f", f=D),
                    wg_all_t[j].rearrange("i p f -> p i f"), "ld")
                dma(sync, wut[bj].ap().rearrange("p (i f) -> p i f", f=D),
                    wu_all_t[j].rearrange("i p f -> p i f"), "ld")
            on_sync(_)
            mark(f"{f}a_wld_{j}")

            for th in range(2):
                def _(t, j=j, bj=bj, th=th):
                    wof(t, f"{f}a_wld_{j}", "ld")
                    if j == 0 and th == 0:
                        wof(t, f + "l_done", "dv")
                    if j >= 1:
                        t.wait_ge(SEM["dv"], marks[f"{f}a_int_{j-1}_{th}"]["dv"])
                    last = None
                    for i in range(ET):
                        for c2 in range(2):
                            gsl = slice(MA * th + CH * c2, MA * th + CH * c2 + CH)
                            rhs_sl = slice(T * i + MA * th + CH * c2, T * i + MA * th + CH * c2 + CH)
                            t.matmul(psA[:, gsl], wgt[bj][:, D * i:D * i + D], LNX[:, rhs_sl],
                                     start=(i == 0), stop=(i == ET - 1))
                            last = t.matmul(psB[:, gsl], wut[bj][:, D * i:D * i + D], LNX[:, rhs_sl],
                                            start=(i == 0), stop=(i == ET - 1))
                    inc(last, "mm")
                on_t(_)
                mark(f"{f}a_mm_{j}_{th}")

                def _(s, j=j, th=th):
                    wof(s, f"{f}a_mm_{j}_{th}", "mm")
                    if j + th >= 1:
                        pj, pt = (j, 0) if th == 1 else (j - 1, 1)
                        s.wait_ge(SEM["dv"], marks[f"{f}a_int_{pj}_{pt}"]["dv"])
                    inc(s.activation(out=g1[:], in_=psA[:, MA * th:MA * th + MA], func=AF.Square), "ac")
                on_s(_)
                mark(f"{f}a_sq_{j}_{th}")

                def _(v, j=j, th=th):
                    wof(v, f"{f}a_sq_{j}_{th}", "ac")
                    v.tensor_scalar(out=g2[:], in0=g1[:], scalar1=C2P, scalar2=C1P, op0=OP.mult, op1=OP.add)
                    inc(v.tensor_tensor(out=g1[:], in0=psA[:, MA * th:MA * th + MA], in1=g2[:], op=OP.mult), "dv")
                on_v(_)
                mark(f"{f}a_gel1_{j}_{th}")

                def _(s, j=j, th=th):
                    wof(s, f"{f}a_gel1_{j}_{th}", "dv")
                    inc(s.activation(out=g2[:], in_=g1[:], func=AF.Sigmoid), "ac")
                on_s(_)
                mark(f"{f}a_sig_{j}_{th}")

                def _(v, j=j, th=th):
                    wof(v, f"{f}a_sig_{j}_{th}", "ac")
                    v.tensor_tensor(out=g1[:], in0=psA[:, MA * th:MA * th + MA], in1=g2[:], op=OP.mult)
                    if j >= 1:
                        [v.wait_ge(SEM[_s], marks[f"{f}a_ipk_{j-1}_{th}"][_s]) for _s in POOL["g"]]
                    inc(v.tensor_tensor(out=ib[th][:], in0=g1[:], in1=psB[:, MA * th:MA * th + MA], op=OP.mult), "dv")
                on_v(_)
                mark(f"{f}a_int_{j}_{th}")

                def _(g, j=j, th=th):
                    wof(g, f"{f}a_int_{j}_{th}", "dv")
                    dma(g, interp[j, :, MA * th:MA * th + MA], ib[th][:], "g")
                on_g(_)
                mark(f"{f}a_ipk_{j}_{th}")
        mark(f + "a_done")
        pa.close()
        lnb.close()

        qgate()
        pbx = ExitStack()
        XBl = pbx.enter_context(nc.sbuf_tensor(f"{f}XB", [D, ET * T], F32))
        pbi = ExitStack()
        IB = pbi.enter_context(nc.sbuf_tensor(f"{f}IB", [D, 8 * T], BF16))
        wdt = [pbi.enter_context(nc.sbuf_tensor(f"{f}wd{b}", [D, 8 * D], BF16)) for b in range(2)]

        for i in range(ET):
            def _(sync, i=i):
                dma(sync, XBl[:, T * i:T * i + T], xpa_t[i], "ld")
            on_sync(_)
            mark(f"{f}b_xr_{i}")

        for g_ in range(16):
            def _(sync, g_=g_):
                if g_ == 0:
                    wof(sync, "wag_d", "cc")
                    wof(sync, f + "a_done", "g")
                if g_ >= 1:
                    sync.wait_ge(SEM["mm"], marks[f"{f}b_mm_{g_-1}_{ET-1}"]["mm"])
                src = interp[8 * g_:8 * g_ + 8, :, :].rearrange("j p t -> p j t")
                dma(sync, IB.ap().rearrange("p (j t) -> p j t", t=T), src, "ld")
            on_sync(_)
            mark(f"{f}b_ib_{g_}")

            for i in range(ET):
                bi = i % 2

                def _(sync, g_=g_, i=i, bi=bi):
                    if g_ > 0 or i >= 2:
                        pg, pi = (g_, i - 2) if i >= 2 else (g_ - 1, ET - 2 + i)
                        sync.wait_ge(SEM["mm"], marks[f"{f}b_mm_{pg}_{pi}"]["mm"])
                    ap = wd_all_t[8 * g_:8 * g_ + 8, :, D * i:D * i + D].rearrange("jj p f -> p jj f")
                    dma(sync, wdt[bi].ap().rearrange("p (jj f) -> p jj f", f=D), ap, "ld")
                on_sync(_)
                mark(f"{f}b_wd_{g_}_{i}")

                def _(t, g_=g_, i=i, bi=bi):
                    wof(t, f"{f}b_wd_{g_}_{i}", "ld")
                    wof(t, f"{f}b_ib_{g_}", "ld")
                    ps = (psA, psB)[i % 2]
                    if g_ > 0 or i >= 2:
                        pg, pi = (g_, i - 2) if i >= 2 else (g_ - 1, ET - 2 + i)
                        t.wait_ge(SEM["dv"], marks[f"{f}b_add_{pg}_{pi}"]["dv"])
                    last = None
                    for jj in range(8):
                        for c in range(TC):
                            sl = slice(CH * c, CH * c + CH)
                            last = t.matmul(ps[:, sl], wdt[bi][:, D * jj:D * jj + D],
                                            IB[:, T * jj + CH * c:T * jj + CH * c + CH],
                                            start=(jj == 0), stop=(jj == 7))
                    inc(last, "mm")
                on_t(_)
                mark(f"{f}b_mm_{g_}_{i}")

                def _(v, g_=g_, i=i):
                    wof(v, f"{f}b_mm_{g_}_{i}", "mm")
                    if g_ == 0:
                        wof(v, f"{f}b_xr_{i}", "ld")
                    ps = (psA, psB)[i % 2]
                    inc(v.tensor_tensor(out=XBl[:, T * i:T * i + T], in0=XBl[:, T * i:T * i + T],
                                        in1=ps[:, 0:T], op=OP.add), "dv")
                on_v(_)
                mark(f"{f}b_add_{g_}_{i}")
        mark(f + "b_done")
        pbi.close()
        return XBl, pbx

    def p4(XBl, pbx):
        pre = ExitStack()
        sq0 = pre.enter_context(nc.sbuf_tensor("b4sq", [D, T], F32))
        for i in range(ET):
            def _(g, i=i):
                if i == 0:
                    wof(g, "f1b_done", "dv")
                dma(g, xpb_t[i], XBl[:, T * i:T * i + T], "g")
            on_g(_)
            mark(f"b4pk_{i}")
        mark("b4pk_done")
        stream_stats("b4s", None, None, None, sq0, xb=XBl, xbmark=("f1b_done", "dv"))
        ln_rows("b4s", kind="ln")
        pre.close()

        def _(v):
            wof(v, "b4pk_done", "g")
            inc(v.tensor_scalar(out=R4[0:1, 0:4], in0=R4[0:1, 0:4], scalar1=1.0, scalar2=None, op0=OP.mult), "dv")
        on_v(_)
        mark("b4_xbfree")
        pbx.close()

        p4s = ExitStack()
        P4 = p4s.enter_context
        LNX = P4(nc.sbuf_tensor("blnx", [D, ET * T], BF16))
        pp4 = ExitStack()
        sb1 = pp4.enter_context(nc.sbuf_tensor("b4s1", [D, T], F32))
        sb2 = pp4.enter_context(nc.sbuf_tensor("b4s2", [D, T], F32))
        sqq = pp4.enter_context(nc.sbuf_tensor("b4sq2", [D, T], F32))

        def lnx_src(i):
            return xpb_t[i]

        def _(sync):
            wof(sync, "b4_xbfree", "dv")
        on_sync(_)
        qgate()
        emit_lnx("b4l", "b4s_rows", lnx_src, LNX, sb1, sb2, sqq)

        qgate()
        kvs = ExitStack()
        KX = kvs.enter_context
        wkb_b = KX(nc.sbuf_tensor("bwkb", [D, ET * KV * D], BF16))
        kvw = KX(nc.sbuf_tensor("bkvw", [D, T], BF16))

        for wq4 in range(4):
            def _(sync, wq4=wq4):
                if wq4 >= 1:
                    sync.wait_ge(SEM["ac"], marks[f"bkv_wcvt_{wq4-1}"]["ac"])
                dma(sync, sb1.ap().rearrange("p (i f) -> p i f", f=KV * D),
                    wkb[E // 4 * wq4:E // 4 * (wq4 + 1), :].rearrange("(i p) f -> p i f", p=D), "ld")
            on_sync(_)
            mark(f"bkv_wld_{wq4}")

            def _(s, wq4=wq4):
                wof(s, f"bkv_wld_{wq4}", "ld")
                if wq4 == 0:
                    wof(s, "b4l_done", "dv")
                inc(s.copy(out=wkb_b[:, 2048 * wq4:2048 * (wq4 + 1)], in_=sb1[:]), "ac")
            on_s(_)
            mark(f"bkv_wcvt_{wq4}")
        mark("bkv_wcvt")

        for kvh in range(KV):
            def _(t, kvh=kvh):
                wof(t, "bkv_wcvt", "ac")
                if kvh >= 1:
                    t.wait_ge(SEM["ac"], marks[f"bkv_km_{kvh-1}"]["ac"])
                last = None
                for i in range(ET):
                    for c in range(TC):
                        sl = slice(CH * c, CH * c + CH)
                        last = t.matmul(psA[:, sl], wkb_b[:, KV * D * i + D * kvh:KV * D * i + D * kvh + D],
                                        LNX[:, T * i + CH * c:T * i + CH * c + CH],
                                        start=(i == 0), stop=(i == ET - 1))
                inc(last, "mm")
            on_t(_)
            mark(f"bkv_kmm_{kvh}")

            def _(s, kvh=kvh):
                wof(s, f"bkv_kmm_{kvh}", "mm")
                inc(s.activation(out=sqq[:], in_=psA[:, 0:T], func=AF.Square), "ac")
            on_s(_)
            mark(f"bkv_ksq_{kvh}")

            def _(t, kvh=kvh):
                wof(t, f"bkv_ksq_{kvh}", "ac")
                last = None
                for c in range(TC):
                    sl = slice(CH * c, CH * c + CH)
                    last = t.matmul(psB[0:1, sl], onfc_s[:], sqq[:, sl], start=True, stop=True)
                inc(last, "mm")
            on_t(_)
            mark(f"bkv_ksmm_{kvh}")

            def _(s, kvh=kvh):
                wof(s, f"bkv_ksmm_{kvh}", "mm")
                s.activation(out=R1[0:1, 0:T], in_=psB[0:1, 0:T], func=AF.Ln, scale=1.0 / D, bias=kc_s[0:1, 2:3])
                inc(s.activation(out=R1[0:1, 0:T], in_=R1[0:1, 0:T], func=AF.Exp, scale=-0.5), "ac")
            on_s(_)
            mark(f"bkv_krs_{kvh}")

            for c in range(TC):
                def _(t, kvh=kvh, c=c):
                    wof(t, f"bkv_krs_{kvh}", "ac")
                    if c >= 1:
                        t.wait_ge(SEM["dv"], marks[f"bkv_kn_{kvh}_{c-1}"]["dv"])
                    sl = slice(CH * c, CH * c + CH)
                    inc(t.matmul(psB[:, 512:1024], onfr_s[0:1, 0:D], R1[0:1, sl], start=True, stop=True), "mm")
                on_t(_)
                mark(f"bkv_kbc_{kvh}_{c}")

                def _(s, kvh=kvh, c=c):
                    wof(s, f"bkv_kbc_{kvh}_{c}", "mm")
                    inc(s.copy(out=sqq[:, 0:CH], in_=psB[:, 512:1024]), "ac")
                on_s(_)
                mark(f"bkv_kbs_{kvh}_{c}")

                def _(v, kvh=kvh, c=c):
                    wof(v, f"bkv_kbs_{kvh}_{c}", "ac")
                    if c == 0 and kvh >= 1:
                        wof(v, f"bkv_kpk_{kvh-1}", "g")
                    sl = slice(CH * c, CH * c + CH)
                    v.tensor_tensor(out=sqq[:, CH:2 * CH], in0=psA[:, sl], in1=sqq[:, 0:CH], op=OP.mult)
                    inc(v.tensor_scalar(out=kvw[:, sl], in0=sqq[:, CH:2 * CH], scalar1=gk_s[:], scalar2=None,
                                        op0=OP.mult), "dv")
                on_v(_)
                mark(f"bkv_kn_{kvh}_{c}")

            def _(g, kvh=kvh):
                wof(g, f"bkv_kn_{kvh}_{TC-1}", "dv")
                dma(g, kin_t[kvh], kvw[:], "g")
            on_g(_)
            mark(f"bkv_kpk_{kvh}")

            def _(s, kvh=kvh):
                wof(s, f"bkv_kn_{kvh}_{TC-1}", "dv")
                inc(s.activation(out=sqq[:], in_=kvw[:], func=AF.Square), "ac")
            on_s(_)
            mark(f"bkv_kev_{kvh}")

            def _(t, kvh=kvh):
                wof(t, f"bkv_kev_{kvh}", "ac")
                last = None
                for c in range(TC):
                    sl = slice(CH * c, CH * c + CH)
                    last = t.matmul(psB[32:33, sl], onfc_s[:], sqq[:, sl], start=True, stop=True)
                inc(last, "mm")
            on_t(_)
            mark(f"bkv_kn2_{kvh}")

            def _(v, kvh=kvh):
                wof(v, f"bkv_kn2_{kvh}", "mm")
                inc(v.tensor_reduce(out=R3[0:1, 0:1], in_=psB[32:33, 0:T], axis=mybir.AxisListType.X,
                                    op=OP.max), "dv")
            on_v(_)
            mark(f"bkv_kmx_{kvh}")

            def _(s, kvh=kvh):
                wof(s, f"bkv_kmx_{kvh}", "dv")
                s.activation(out=R3[0:1, 1:2], in_=R3[0:1, 0:1], func=AF.Ln, bias=kc_s[0:1, 4:5])
                inc(s.activation(out=km_s[0:1, kvh:kvh + 1], in_=R3[0:1, 1:2], func=AF.Exp, scale=0.5), "ac")
            on_s(_)
            mark(f"bkv_km_{kvh}")
        kvs.close()

        qgate()
        vvs = ExitStack()
        wvb_b = vvs.enter_context(nc.sbuf_tensor("bwvb", [D, ET * KV * D], BF16))
        vst = vvs.enter_context(nc.sbuf_tensor("bvst", [D, T], BF16))

        for wq4 in range(4):
            def _(sync, wq4=wq4):
                if wq4 >= 1:
                    sync.wait_ge(SEM["ac"], marks[f"bvv_wcvt_{wq4-1}"]["ac"])
                dma(sync, sb2.ap().rearrange("p (i f) -> p i f", f=KV * D),
                    wvb[E // 4 * wq4:E // 4 * (wq4 + 1), :].rearrange("(i p) f -> p i f", p=D), "ld")
            on_sync(_)
            mark(f"bvv_wld_{wq4}")

            def _(s, wq4=wq4):
                wof(s, f"bvv_wld_{wq4}", "ld")
                if wq4 == 0:
                    wof(s, f"bkv_km_{KV-1}", "ac")
                inc(s.copy(out=wvb_b[:, 2048 * wq4:2048 * (wq4 + 1)], in_=sb2[:]), "ac")
            on_s(_)
            mark(f"bvv_wcvt_{wq4}")
        mark("bvv_wcvt")

        for mt in range(ET):
            def _(t, mt=mt):
                wof(t, "bvv_wcvt", "ac")
                if mt >= 2:
                    t.wait_ge(SEM["ac"], marks[f"bvv_ev_{mt-2}"]["ac"])
                ps = (psA, psB)[mt % 2]
                last = None
                for i in range(ET):
                    last = t.matmul(ps[:, 0:KV * D], LNX[:, T * i + D * mt:T * i + D * mt + D],
                                    wvb_b[:, KV * D * i:KV * D * i + KV * D],
                                    start=(i == 0), stop=(i == ET - 1))
                inc(last, "mm")
            on_t(_)
            mark(f"bvv_mm_{mt}")

            def _(s, mt=mt):
                wof(s, f"bvv_mm_{mt}", "mm")
                if mt % 4 == 0 and mt >= 4:
                    [s.wait_ge(SEM[_s], marks[f"bvv_pk_{mt//4-1}"][_s]) for _s in POOL["g"]]
                ps = (psA, psB)[mt % 2]
                inc(s.copy(out=vst[:, CH * (mt % 4):CH * (mt % 4) + CH], in_=ps[:, 0:KV * D]), "ac")
            on_s(_)
            mark(f"bvv_ev_{mt}")

            if mt % 4 == 3:
                def _(g, mt=mt):
                    wof(g, f"bvv_ev_{mt}", "ac")
                    dst = vag_in[D * (mt - 3):D * (mt + 1), :].rearrange("(m p) f -> p m f", p=D)
                    dma(g, dst, vst.ap().rearrange("p (m f) -> p m f", f=KV * D), "g")
                on_g(_)
                mark(f"bvv_pk_{mt//4}")
        mark("bkv_done")
        vvs.close()
        pp4.close()

        def _(g):
            wof(g, "bkv_done", "g")
            g.wait_ge(SEM["cc"], C["cc"])
            inc(g.collective_compute("AllGather", OP.bypass,
                                     replica_groups=[[0, 1], [2, 3], [4, 5], [6, 7]],
                                     ins=[kag_in.opt()], outs=[kag_out.opt()]), "cc")
            g.wait_ge(SEM["cc"], C["cc"])
            inc(g.collective_compute("AllGather", OP.bypass,
                                     replica_groups=[[0, 1], [2, 3], [4, 5], [6, 7]],
                                     ins=[vag_in.opt()], outs=[vag_out.opt()]), "cc")
        on_g(_)
        mark("kvag")

        qgate()
        at = ExitStack()
        AT = at.enter_context
        mtri_s = AT(nc.sbuf_tensor("mtri_s", [D, 4 * CH], F32))
        AO = AT(nc.sbuf_tensor("bAO", [D, H * HT], BF16))
        wqs = AT(nc.sbuf_tensor("bwqs", [D, ET * D], F32))
        wqh = [AT(nc.sbuf_tensor(f"bwqh{b}", [D, ET * D], BF16)) for b in range(2)]
        qbb = AT(nc.sbuf_tensor("bqb", [D, HT], BF16))
        klo = AT(nc.sbuf_tensor("bklo", [D, T], BF16))
        kpr = AT(nc.sbuf_tensor("bkpr", [D, T], BF16))
        vlo = AT(nc.sbuf_tensor("bvlo", [D, ET * D], BF16))
        vpr = AT(nc.sbuf_tensor("bvpr", [D, ET * D], BF16))
        clo = AT(nc.sbuf_tensor("bclo", [D, CH], F32))
        cpr = AT(nc.sbuf_tensor("bcpr", [D, CH], F32))
        sub = [AT(nc.sbuf_tensor(f"bsub{b}", [D, CH], F32)) for b in range(2)]
        pbu = [AT(nc.sbuf_tensor(f"bpb{b}", [D, CH], BF16)) for b in range(2)]
        rbb = AT(nc.sbuf_tensor("brb", [D, CH], F32))
        sqh = AT(nc.sbuf_tensor("bsqh", [D, HT], F32))
        yts = [AT(nc.sbuf_tensor(f"byts{b}", [D, HT], F32)) for b in range(2)]

        def _(sync):
            dma(sync, mtri_s[:], mtri, "ld")
        on_sync(_)
        mark("bat_mtld")

        def _(v):
            wof(v, "bat_mtld", "ld")
            inc(v.tensor_scalar(out=R4[0:1, 4:8], in0=R4[0:1, 4:8], scalar1=1.0, scalar2=None, op0=OP.mult), "dv")
        on_v(_)
        mark("bat_mtok")

        for th in range(2):
            hsl = slice(HT * th, HT * th + HT)
            for h in range(H):
                kvh = h // 4
                bh = h % 2
                ht = f"b{th}_{h}"
                if h % 4 == 0:
                    def _(sync, kvh=kvh, th=th, h=h):
                        wof(sync, "kvag", "cc")
                        if th > 0 or h > 0:
                            pth, ph = (th, h - 1) if h > 0 else (0, H - 1)
                            sync.wait_ge(SEM["mm"], marks[f"b{pth}_{ph}_alast"]["mm"])
                        dma(sync, klo[:], kin_t[kvh], "ld")
                        dma(sync, kpr[:], kout_t[0, kvh], "ld")
                        dma(sync, vlo.ap().rearrange("p (m f) -> p m f", f=D),
                            vag_in[:, D * kvh:D * kvh + D].rearrange("(m p) f -> p m f", p=D), "ld")
                        dma(sync, vpr.ap().rearrange("p (m f) -> p m f", f=D),
                            vag_out[0:T, D * kvh:D * kvh + D].rearrange("(m p) f -> p m f", p=D), "ld")
                    on_sync(_)
                    mark(f"b{th}_kvld_{kvh}")

                def _(sync, h=h, ht=ht, th=th):
                    if h >= 1:
                        sync.wait_ge(SEM["ac"], marks[f"b{th}_{h-1}_wcv"]["ac"])
                    elif th > 0:
                        sync.wait_ge(SEM["ac"], marks[f"bop{th-1}_{ET-1}_wcv"]["ac"])
                    dma(sync, wqs.ap().rearrange("p (i f) -> p i f", f=D),
                        wqb[:, D * h:D * h + D].rearrange("(i p) f -> p i f", p=D), "ld")
                on_sync(_)
                mark(f"{ht}_wld")

                def _(s, ht=ht, bh=bh, th=th, h=h):
                    wof(s, f"{ht}_wld", "ld")
                    if h >= 2:
                        s.wait_ge(SEM["mm"], marks[f"b{th}_{h-2}_qmm"]["mm"])
                    elif th > 0:
                        s.wait_ge(SEM["mm"], marks[f"bop{th-1}_{ET-2+h}_mm"]["mm"])
                    inc(s.copy(out=wqh[bh][:], in_=wqs[:]), "ac")
                on_s(_)
                mark(f"{ht}_wcv")

                def _(t, ht=ht, bh=bh, th=th, h=h, hsl=hsl):
                    wof(t, f"{ht}_wcv", "ac")
                    if th == 0 and h == 0:
                        wof(t, "bkv_done", "g")
                    else:
                        pth, ph = (th, h - 1) if h >= 1 else (0, H - 1)
                        t.wait_ge(SEM["dv"], marks[f"b{pth}_{ph}_qn_1"]["dv"])
                    last = None
                    for i in range(ET):
                        for c2 in range(2):
                            qsl = slice(CH * c2, CH * c2 + CH)
                            last = t.matmul(psB[:, qsl], wqh[bh][:, D * i:D * i + D],
                                            LNX[:, T * i + HT * th + CH * c2:T * i + HT * th + CH * c2 + CH],
                                            start=(i == 0), stop=(i == ET - 1))
                    inc(last, "mm")
                on_t(_)
                mark(f"{ht}_qmm")

                def _(s, ht=ht):
                    wof(s, f"{ht}_qmm", "mm")
                    inc(s.activation(out=sqh[:], in_=psB[:, 0:HT], func=AF.Square), "ac")
                on_s(_)
                mark(f"{ht}_qsq")

                def _(t, ht=ht, th=th, h=h):
                    wof(t, f"{ht}_qsq", "ac")
                    if th > 0 or h > 0:
                        pth, ph = (th, h - 1) if h >= 1 else (0, H - 1)
                        t.wait_ge(SEM["dv"], marks[f"b{pth}_{ph}_cr"]["dv"])
                    last = None
                    for c2 in range(2):
                        qsl = slice(CH * c2, CH * c2 + CH)
                        last = t.matmul(psA[0:1, qsl], onfc_s[:], sqh[:, qsl], start=True, stop=True)
                    inc(last, "mm")
                on_t(_)
                mark(f"{ht}_qs2")

                def _(s, ht=ht):
                    wof(s, f"{ht}_qs2", "mm")
                    s.activation(out=R1[0:1, 0:HT], in_=psA[0:1, 0:HT], func=AF.Ln, scale=1.0 / D, bias=kc_s[0:1, 2:3])
                    inc(s.activation(out=R1[0:1, 0:HT], in_=R1[0:1, 0:HT], func=AF.Exp, scale=-0.5), "ac")
                on_s(_)
                mark(f"{ht}_qrs")

                for c2 in range(2):
                    def _(t, ht=ht, c2=c2):
                        wof(t, f"{ht}_qrs", "ac")
                        if c2 >= 1:
                            t.wait_ge(SEM["dv"], marks[f"{ht}_qn_{c2-1}"]["dv"])
                        qsl = slice(CH * c2, CH * c2 + CH)
                        inc(t.matmul(psA[:, 512:1024], onfr_s[0:1, 0:D], R1[0:1, qsl], start=True, stop=True), "mm")
                    on_t(_)
                    mark(f"{ht}_qbc_{c2}")

                    def _(s, ht=ht, c2=c2):
                        wof(s, f"{ht}_qbc_{c2}", "mm")
                        inc(s.copy(out=sqh[:, 0:CH], in_=psA[:, 512:1024]), "ac")
                    on_s(_)
                    mark(f"{ht}_qbs_{c2}")

                    def _(v, ht=ht, c2=c2):
                        wof(v, f"{ht}_qbs_{c2}", "ac")
                        qsl = slice(CH * c2, CH * c2 + CH)
                        v.tensor_tensor(out=sqh[:, CH:2 * CH] if False else sqh[:, CH * c2:CH * c2 + CH],
                                        in0=psB[:, qsl], in1=sqh[:, 0:CH], op=OP.mult) if False else None
                        v.tensor_tensor(out=rbb[:], in0=psB[:, qsl], in1=sqh[:, 0:CH], op=OP.mult)
                        inc(v.tensor_scalar(out=qbb[:, qsl], in0=rbb[:], scalar1=gqsc_s[:], scalar2=None,
                                            op0=OP.mult), "dv")
                    on_v(_)
                    mark(f"{ht}_qn_{c2}")

                def _(s, ht=ht):
                    wof(s, f"{ht}_qn_1", "dv")
                    inc(s.activation(out=sqh[:], in_=qbb[:], func=AF.Square), "ac")
                on_s(_)
                mark(f"{ht}_cq1")

                def _(t, ht=ht):
                    wof(t, f"{ht}_cq1", "ac")
                    last = None
                    for c2 in range(2):
                        qsl = slice(CH * c2, CH * c2 + CH)
                        last = t.matmul(psA[32:33, qsl], onfc_s[:], sqh[:, qsl], start=True, stop=True)
                    inc(last, "mm")
                on_t(_)
                mark(f"{ht}_cq2")

                def _(s, ht=ht):
                    wof(s, f"{ht}_cq2", "mm")
                    s.activation(out=R3[0:1, 0:HT], in_=psA[32:33, 0:HT], func=AF.Ln, bias=kc_s[0:1, 4:5])
                    inc(s.activation(out=R3[0:1, 0:HT], in_=R3[0:1, 0:HT], func=AF.Exp, scale=0.5), "ac")
                on_s(_)
                mark(f"{ht}_cq3")

                def _(v, ht=ht, kvh=kvh):
                    wof(v, f"{ht}_cq3", "ac")
                    v.tensor_scalar(out=R3[0:1, 0:HT], in0=R3[0:1, 0:HT], scalar1=km_s[0:1, kvh:kvh + 1],
                                    scalar2=None, op0=OP.mult)
                    inc(v.tensor_scalar(out=R3[0:1, HT:2 * HT], in0=R3[0:1, 0:HT], scalar1=pm_s[0:1, 0:1],
                                        scalar2=None, op0=OP.add), "dv")
                on_v(_)
                mark(f"{ht}_cr")

                for tcl in range(2):
                    tc = 2 * th + tcl
                    tsl = slice(CH * tcl, CH * tcl + CH)
                    tag = f"{ht}_{tcl}"

                    def _(t, ht=ht, tag=tag, tcl=tcl):
                        if tcl == 0:
                            wof(t, f"{ht}_cr", "dv")
                        else:
                            wof(t, f"{ht}_0_aoev", "dv")
                        t.matmul(psA[:, 1024:1536], onfr_s[0:1, 0:D], R3[0:1, CH * tcl:CH * tcl + CH],
                                 start=True, stop=True)
                        inc(t.matmul(psA[:, 1536:2048], onfr_s[0:1, 0:D], R3[0:1, HT + CH * tcl:HT + CH * tcl + CH],
                                     start=True, stop=True), "mm")
                    on_t(_)
                    mark(f"{tag}_cbc")

                    def _(s, tag=tag):
                        wof(s, f"{tag}_cbc", "mm")
                        s.copy(out=clo[:], in_=psA[:, 1024:1536])
                        inc(s.copy(out=cpr[:], in_=psA[:, 1536:2048]), "ac")
                    on_s(_)
                    mark(f"{tag}_cbs")

                    msteps = [("p", mt) for mt in range(ET)] + [("l", mt) for mt in range(4 * tc + 4)]
                    ns = len(msteps)
                    for si, (src, mt) in enumerate(msteps):
                        pp = si % 2

                        def _(t, si=si, src=src, mt=mt, pp=pp, tag=tag, tsl=tsl):
                            if si == 0:
                                wof(t, f"{tag}_cbs", "ac")
                            if si >= 2:
                                t.wait_ge(SEM["dv"], marks[f"{tag}_sb_{si-2}"]["dv"])
                            kbuf = kpr if src == "p" else klo
                            inc(t.matmul(psA[:, CH * pp:CH * pp + CH], kbuf[:, D * mt:D * mt + D],
                                         qbb[:, tsl], start=True, stop=True), "mm")
                        on_t(_)
                        mark(f"{tag}_sc_{si}")

                        def _(v, si=si, src=src, mt=mt, pp=pp, tag=tag, tc=tc):
                            wof(v, f"{tag}_sc_{si}", "mm")
                            if si >= 2:
                                v.wait_ge(SEM["ac"], marks[f"{tag}_ex_{si-2}"]["ac"])
                            diag = (src == "l" and mt >= 4 * tc)
                            ce = cpr if src == "p" else clo
                            v.tensor_tensor(out=sub[pp][:], in0=psA[:, CH * pp:CH * pp + CH],
                                            in1=ce[:], op=OP.subtract)
                            if diag:
                                r = mt - 4 * tc
                                v.tensor_tensor(out=sub[pp][:], in0=sub[pp][:],
                                                in1=mtri_s[:, CH * r:CH * r + CH], op=OP.add)
                            inc(v.tensor_scalar(out=sub[pp][:, 0:4], in0=sub[pp][:, 0:4], scalar1=1.0,
                                                scalar2=None, op0=OP.mult), "dv")
                        on_v(_)
                        mark(f"{tag}_sb_{si}")

                        def _(s, si=si, pp=pp, tag=tag):
                            wof(s, f"{tag}_sb_{si}", "dv")
                            if si >= 2:
                                s.wait_ge(SEM["mm"], marks[f"{tag}_da_{si-2}"]["mm"])
                            inc(s.activation(out=pbu[pp][:], in_=sub[pp][:], func=AF.Exp), "ac")
                        on_s(_)
                        mark(f"{tag}_ex_{si}")

                        def _(t, si=si, src=src, mt=mt, pp=pp, tag=tag, ns=ns):
                            wof(t, f"{tag}_ex_{si}", "ac")
                            vbuf = vpr if src == "p" else vlo
                            t.matmul(psB[0:1, 1024:1536], onbc_s[:], pbu[pp][:],
                                     start=(si == 0), stop=(si == ns - 1))
                            inc(t.matmul(psB[:, 1536:2048], vbuf[:, D * mt:D * mt + D], pbu[pp][:],
                                         start=(si == 0), stop=(si == ns - 1)), "mm")
                        on_t(_)
                        mark(f"{tag}_da_{si}")

                    def _(v, tag=tag, ns=ns):
                        wof(v, f"{tag}_da_{ns-1}", "mm")
                        v.tensor_scalar(out=R4[0:1, 0:CH], in0=psB[0:1, 1024:1536], scalar1=1.0,
                                        scalar2=None, op0=OP.mult)
                        inc(v.reciprocal(out=R4[0:1, 0:CH], in_=R4[0:1, 0:CH]), "dv")
                    on_v(_)
                    mark(f"{tag}_rd")

                    def _(t, tag=tag):
                        wof(t, f"{tag}_rd", "dv")
                        inc(t.matmul(psA[:, 1024:1536], onfr_s[0:1, 0:D], R4[0:1, 0:CH], start=True, stop=True), "mm")
                    on_t(_)
                    mark(f"{tag}_rbc")

                    def _(s, tag=tag):
                        wof(s, f"{tag}_rbc", "mm")
                        inc(s.copy(out=rbb[:], in_=psA[:, 1024:1536]), "ac")
                    on_s(_)
                    mark(f"{tag}_rbs")

                    def _(v, tag=tag, h=h, tcl=tcl, th=th):
                        wof(v, f"{tag}_rbs", "ac")
                        if th > 0 and h == 0 and tcl == 0:
                            v.wait_ge(SEM["mm"], marks[f"bop{th-1}_{ET-1}_mm"]["mm"])
                        inc(v.tensor_tensor(out=AO[:, HT * h + CH * tcl:HT * h + CH * tcl + CH],
                                            in0=psB[:, 1536:2048], in1=rbb[:], op=OP.mult), "dv")
                    on_v(_)
                    mark(f"{tag}_aoev")
                mark(f"b{th}_{h}_alast")
            mark(f"bat{th}_done")

            for i in range(ET):
                bi = i % 2
                ot = f"bop{th}_{i}"

                def _(sync, i=i, ot=ot, th=th):
                    if i >= 1:
                        sync.wait_ge(SEM["ac"], marks[f"bop{th}_{i-1}_wcv"]["ac"])
                    else:
                        sync.wait_ge(SEM["ac"], marks[f"b{th}_{H-1}_wcv"]["ac"])
                    dma(sync, wqs.ap().rearrange("p (i f) -> p i f", f=D),
                        wob[:, D * i:D * i + D].rearrange("(k p) f -> p k f", p=D), "ld")
                on_sync(_)
                mark(f"{ot}_wld")

                def _(s, ot=ot, bi=bi, th=th, i=i):
                    wof(s, f"{ot}_wld", "ld")
                    if th > 0 or i >= 2:
                        pth, pi = (th, i - 2) if i >= 2 else (0, ET - 2 + i)
                        s.wait_ge(SEM["mm"], marks[f"bop{pth}_{pi}_mm"]["mm"])
                    inc(s.copy(out=wqh[bi][:], in_=wqs[:]), "ac")
                on_s(_)
                mark(f"{ot}_wcv")

                def _(t, ot=ot, bi=bi, th=th, i=i):
                    wof(t, f"{ot}_wcv", "ac")
                    if i == 0:
                        wof(t, f"bat{th}_done", "dv")
                    ps = (psA, psB)[i % 2]
                    if th > 0 or i >= 2:
                        pth, pi = (th, i - 2) if i >= 2 else (0, ET - 2 + i)
                        t.wait_ge(SEM["ac"], marks[f"bop{pth}_{pi}_ev"]["ac"])
                    last = None
                    for k in range(ET):
                        for c2 in range(2):
                            qsl = slice(CH * c2, CH * c2 + CH)
                            last = t.matmul(ps[:, qsl], wqh[bi][:, D * k:D * k + D],
                                            AO[:, HT * k + CH * c2:HT * k + CH * c2 + CH],
                                            start=(k == 0), stop=(k == ET - 1))
                    inc(last, "mm")
                on_t(_)
                mark(f"{ot}_mm")

                def _(s, ot=ot, bi=bi, th=th, i=i):
                    wof(s, f"{ot}_mm", "mm")
                    if th > 0 or i >= 2:
                        pth, pi = (th, i - 2) if i >= 2 else (0, ET - 2 + i)
                        [s.wait_ge(SEM[_s], marks[f"bop{pth}_{pi}_pk"][_s]) for _s in POOL["g"]]
                    ps = (psA, psB)[i % 2]
                    inc(s.copy(out=yts[bi][:], in_=ps[:, 0:HT]), "ac")
                on_s(_)
                mark(f"{ot}_ev")

                def _(g, ot=ot, i=i, th=th, bi=bi):
                    wof(g, f"{ot}_ev", "ac")
                    dma(g, yp_t[i][:, HT * th:HT * th + HT], yts[bi][:], "g")
                on_g(_)
                mark(f"{ot}_pk")
        mark("p4_done")
        at.close()
        p4s.close()

    def p6(xsrc_sbuf=None, dram_src=None):
        if xsrc_sbuf is not None:
            def _(v):
                inc(v.tensor_scalar(out=R4[0:1, 0:4], in0=R4[0:1, 0:4], scalar1=1.0,
                                    scalar2=None, op0=OP.mult), "dv")
            on_v(_)
            mark("p6_flush")

            def _(sync):
                wof(sync, "p6_flush", "dv")
                for i in range(ET):
                    dma(sync, yT_t[i], xsrc_sbuf[:, T * i:T * i + T], "st")
                for _s in POOL["st"]:
                    sync.wait_ge(SEM[_s], C[_s])
            on_sync(_)
        else:
            def _(sync):
                for q in ("ld", "g", "mm", "dv", "ac"):
                    for _s in POOL[q]:
                        sync.wait_ge(SEM[_s], C[_s])
                dma(sync, yT, dram_src, "st")
                for _s in POOL["st"]:
                    sync.wait_ge(SEM[_s], C[_s])
            on_sync(_)

    p0()
    if STAGE >= 2 and STAGE != 15:
        p1()
    p2()
    if STAGE in (0, 15):
        p6(dram_src=xparkA)
    elif STAGE == 1:
        p6(dram_src=xparkA)
    elif STAGE == 2:
        XBl, pbx = ffn(1)
        p6(xsrc_sbuf=XBl)
    else:
        XBl, pbx = ffn(1)
        p4(XBl, pbx)
        if STAGE == 3:
            qgate()
            st3 = ExitStack()
            s1 = st3.enter_context(nc.sbuf_tensor("st3a", [D, T], F32))
            s2 = st3.enter_context(nc.sbuf_tensor("st3b", [D, T], F32))
            s3 = st3.enter_context(nc.sbuf_tensor("st3c", [D, T], F32))
            stream_stats("x3s", lambda i: xpb_t[i], s1, s2, s3,
                         add_fn=lambda i: yp_t[i], park_fn=lambda i: xpa_t[i])
            st3.close()

            def _(sync):
                wof(sync, "x3s_stats", "mm")
                for _s in POOL["g"]:
                    sync.wait_ge(SEM[_s], C[_s])
                dma(sync, yT, xparkA, "st")
                for _s in POOL["st"]:
                    sync.wait_ge(SEM[_s], C[_s])
            on_sync(_)
        else:
            XB2, pbx2 = ffn(2)
            p6(xsrc_sbuf=XB2)

    return nc


_PE = perm256(E)
_PN = perm256(NI)


def _stage_inputs(inputs, core):
    ii = inputs
    p, hf = core // 2, core % 2
    tok = slice(T * hf, T * hf + T)
    f32 = np.float32

    x = np.asarray(ii["L0_attn_a_query_input"], f32)
    d = {}
    d["xT"] = np.ascontiguousarray(x[p, tok, :].T[_PE, :])
    d["winT"] = np.ascontiguousarray(np.asarray(ii["L0_attn_a_W_in"], f32).T[_PE, :])
    d["wqa"] = np.ascontiguousarray(np.asarray(ii["L0_attn_a_W_q"], f32).T)
    d["wka"] = np.ascontiguousarray(np.asarray(ii["L0_attn_a_W_k"], f32).T)
    dperm = np.concatenate([np.arange(0, D, 2), np.arange(1, D, 2)])
    d["wva"] = np.ascontiguousarray(np.asarray(ii["L0_attn_a_W_v"], f32).T[:, dperm])
    d["patT"] = np.ascontiguousarray(
        np.asarray(ii["L0_attn_a_stored_patterns"], f32).transpose(0, 2, 1).reshape(H * D, MA))
    d["nqw_c"] = np.ascontiguousarray(np.asarray(ii["L0_attn_a_norm_query_w"], f32)[_PE].reshape(ET, D).T)
    d["nrw_c"] = np.ascontiguousarray(np.asarray(ii["L0_attn_a_norm_retrieved_w"], f32)[_PE].reshape(ET, D).T)
    d["wqb"] = np.ascontiguousarray(np.asarray(ii["L0_attn_b_Wq"], f32)[_PE, :])
    d["wkb"] = np.ascontiguousarray(np.asarray(ii["L0_attn_b_Wk"], f32)[_PE, :])
    d["wvb"] = np.ascontiguousarray(np.asarray(ii["L0_attn_b_Wv"], f32)[_PE, :])
    d["wob"] = np.ascontiguousarray(np.asarray(ii["L0_attn_b_Wo"], f32)[:, _PE])
    d["gqv"] = np.ascontiguousarray(np.asarray(ii["L0_attn_b_gq"], f32)[:, None])
    d["gkv"] = np.ascontiguousarray(np.asarray(ii["L0_attn_b_gk"], f32)[:, None])

    def codes_u8(a):
        a = np.ascontiguousarray(np.asarray(a, np.int32))
        return a.view(np.uint8).reshape(a.shape[0], a.shape[1], 4)[:, :, 0]

    nsl = slice(T * core, T * core + T)
    gT = codes_u8(ii["L0_ffn_gate_W_packed"]).T[:, _PN]
    uT = codes_u8(ii["L0_ffn_up_W_packed"]).T[:, _PN]
    dT = codes_u8(ii["L0_ffn_down_W_packed"]).T[:, _PE]
    d["gcodes"] = np.ascontiguousarray(gT[:, nsl])
    d["ucodes"] = np.ascontiguousarray(uT[:, nsl])
    d["dcodes"] = np.ascontiguousarray(dT[1024 * core:1024 * core + 1024, :])

    def scl_rep(s, permc, rowsl, colsl):
        sT = np.asarray(s, f32).T[:, permc]
        return np.ascontiguousarray(np.repeat(sT, 16, axis=0)[rowsl, colsl])

    d["gscl"] = scl_rep(ii["L0_ffn_gate_scales"], _PN, slice(None), nsl)
    d["uscl"] = scl_rep(ii["L0_ffn_up_scales"], _PN, slice(None), nsl)
    d["dscl"] = scl_rep(ii["L0_ffn_down_scales"], _PE, slice(1024 * core, 1024 * core + 1024), slice(None))

    kcv = np.zeros((D, 8), f32)
    kcv[:, 0] = -2.0
    kcv[:, 1] = -6.0
    kcv[:, 2] = 1e-6
    kcv[:, 3] = 1e-5
    kcv[:, 4] = 1e-20
    d["kc"] = kcv
    d["beta_in"] = np.asarray(ii["L0_attn_a_beta"], f32).reshape(1, 1)
    d["pmask"] = np.full((1, 1), 1e30 if hf == 0 else 0.0, f32)
    mt = np.zeros((D, 4 * CH), f32)
    mrow = np.arange(D)[:, None]
    scol = np.arange(CH)[None, :]
    for r in range(4):
        mt[:, CH * r:CH * r + CH] = np.where(mrow <= scol - D * r, 0.0, -1e30)
    d["mtri"] = mt
    d["onesf_r"] = np.ones((D, D), f32)
    d["onesf_c"] = np.ones((D, 1), f32)
    d["onesb_c"] = np.ones((D, 1), ml_dtypes.bfloat16)
    return d


def kernel(**inputs):
    nc = build_program()
    in_maps = [_stage_inputs(inputs, c) for c in range(NCORE)]
    res = run_bass_kernel_spmd(nc, in_maps, core_ids=list(range(NCORE)))
    out = np.empty((4, 4096, E), np.float32)
    inv = np.empty(E, np.int64)
    inv[_PE] = np.arange(E)
    for c in range(NCORE):
        shard = np.asarray(res.results[c]["yT"], np.float32)
        p, hf = c // 2, c % 2
        out[p, T * hf:T * hf + T, :] = shard[inv, :].T
    return out


# version 02
# speedup vs baseline: 1.2542x; 1.2542x over previous
import numpy as np
import ml_dtypes
import concourse.bass as bass
import concourse.mybir as mybir
from concourse.bass_utils import run_bass_kernel_spmd
from contextlib import ExitStack

F32 = mybir.dt.float32
BF16 = mybir.dt.bfloat16
U8 = mybir.dt.uint8
AF = mybir.ActivationFunctionType
OP = mybir.AluOpType

E = 2048
T = 2048
HT = 1024
ET = 16
TC = 4
CH = 512
H = 16
D = 128
MA = 1024
KV = 4
NI = 16384
NT = 128
NCORE = 8
ISD = float(D) ** -0.5
C1P = 2.0 * 0.7978845608028654
C2P = 2.0 * 0.7978845608028654 * 0.044715

STAGE = 4


def perm256(n):
    return np.arange(n).reshape(-1, 128, 2).transpose(0, 2, 1).reshape(-1)


def build_program():
    nc = bass.Bass("TRN2", target_bir_lowering=False)

    def din(name, shape, dt=F32):
        return nc.dram_tensor(name, shape, dt, kind="ExternalInput").ap()

    xT = din("xT", [E, T])
    winT = din("winT", [E, E])
    wqa = din("wqa", [D, D])
    wka = din("wka", [D, D])
    wva = din("wva", [D, D])
    patT = din("patT", [H * D, MA])
    nqw_c = din("nqw_c", [D, ET])
    nrw_c = din("nrw_c", [D, ET])
    wqb = din("wqb", [E, E])
    wkb = din("wkb", [E, KV * D])
    wvb = din("wvb", [E, KV * D])
    wob = din("wob", [E, E])
    gqv = din("gqv", [D, 1])
    gkv = din("gkv", [D, 1])
    gcodes = din("gcodes", [NI // 16, T], U8)
    ucodes = din("ucodes", [NI // 16, T], U8)
    dcodes = din("dcodes", [NI // 16, T], U8)
    gscl = din("gscl", [NI // 16, T])
    uscl = din("uscl", [NI // 16, T])
    dscl = din("dscl", [NI // 16, T])
    kc = din("kc", [D, 8])
    beta_in = din("beta_in", [1, 1])
    pmask = din("pmask", [1, 1])
    mtri = din("mtri", [D, 4 * CH])
    onesf_r = din("onesf_r", [D, D])
    onesf_c = din("onesf_c", [D, 1])
    onesb_c = din("onesb_c", [D, 1], BF16)

    yT = nc.dram_tensor("yT", [E, T], F32, kind="ExternalOutput").ap()

    def dint(name, shape, dt):
        return nc.dram_tensor(name, shape, dt, kind="Internal").ap()

    xparkA = dint("xparkA", [E, T], F32)
    xparkB = dint("xparkB", [E, T], F32)
    ypark = dint("ypark", [E, T], F32)
    interp = dint("interp", [NT, D, T], BF16)
    wg_sh = dint("wg_sh", [16 * ET * D, D], BF16)
    wu_sh = dint("wu_sh", [16 * ET * D, D], BF16)
    wd_sh = dint("wd_sh", [16 * D, T], BF16)
    wg_all = dint("wg_all", [NT * ET * D, D], BF16)
    wu_all = dint("wu_all", [NT * ET * D, D], BF16)
    wd_all = dint("wd_all", [NT * D, T], BF16)
    kag_in = dint("kag_in", [KV * D, T], BF16)
    kag_out = dint("kag_out", [2 * KV * D, T], BF16)
    vag_in = dint("vag_in", [T, KV * D], BF16)
    vag_out = dint("vag_out", [2 * T, KV * D], BF16)

    es = ExitStack()
    SB = es.enter_context

    R1 = SB(nc.sbuf_tensor("R1", [D, T], F32))
    R2 = SB(nc.sbuf_tensor("R2", [D, T], F32))
    R3 = SB(nc.sbuf_tensor("R3", [D, T], F32))
    R4 = SB(nc.sbuf_tensor("R4", [D, T], F32))
    kc_s = SB(nc.sbuf_tensor("kc_s", [D, 8], F32))
    nqc_s = SB(nc.sbuf_tensor("nqc_s", [D, ET], F32))
    nrc_s = SB(nc.sbuf_tensor("nrc_s", [D, ET], F32))
    gq_s = SB(nc.sbuf_tensor("gq_s", [D, 1], F32))
    gqsc_s = SB(nc.sbuf_tensor("gqsc_s", [D, 1], F32))
    gk_s = SB(nc.sbuf_tensor("gk_s", [D, 1], F32))
    beta_s = SB(nc.sbuf_tensor("beta_s", [1, 4], F32))
    pm_s = SB(nc.sbuf_tensor("pm_s", [1, 1], F32))
    km_s = SB(nc.sbuf_tensor("km_s", [1, 8], F32))
    onfr_s = SB(nc.sbuf_tensor("onfr_s", [D, D], F32))
    onfc_s = SB(nc.sbuf_tensor("onfc_s", [D, 1], F32))
    onbc_s = SB(nc.sbuf_tensor("onbc_s", [D, 1], BF16))
    sigb_s = SB(nc.sbuf_tensor("sigb_s", [D, 1], F32))
    wqa_s = SB(nc.sbuf_tensor("wqa_s", [D, D], BF16))
    wka_s = SB(nc.sbuf_tensor("wka_s", [D, D], BF16))
    wva_s = SB(nc.sbuf_tensor("wva_s", [D, D], BF16))

    psA = SB(nc.psum_tensor("psA", [D, T], F32))
    psB = SB(nc.psum_tensor("psB", [D, T], F32))

    POOL = {"ld": [f"ld{i}" for i in range(6)],
            "g": [f"g{i}" for i in range(3)],
            "st": ["st0", "st1"],
            "mm": ["mm"], "dv": ["dv"], "ac": ["ac"], "cc": ["cc"]}
    ALLSEM = [n for v in POOL.values() for n in v]
    SEM = {n: SB(nc.semaphore("sem_" + n)) for n in ALLSEM}
    C = {n: 0 for n in ALLSEM}
    RR = {"ld": 0, "g": 0, "st": 0}

    def dma(qeng, out, in_, q):
        name = POOL[q][RR[q] % len(POOL[q])]
        RR[q] += 1
        qeng.wait_ge(SEM[name], C[name])
        r = qeng.dma_start(out=out, in_=in_)
        C[name] += 16
        r.then_inc(SEM[name], 16)
        return r

    def inc(instr, s, n=1):
        if s in ("dv", "ac"):
            return instr
        assert s in ("mm", "cc"), s
        C[s] += n
        instr.then_inc(SEM[s], n)
        return instr

    marks = {}

    def mark(name):
        marks[name] = dict(C)

    def wof(eng, name, *qs):
        for q in qs:
            for s in POOL[q]:
                eng.wait_ge(SEM[s], marks[name][s])

    def qgate():
        snap = dict(C)

        def _(sync):
            for sname in ("mm", "dv", "ac"):
                sync.wait_ge(SEM[sname], snap[sname])
            for sname in POOL["g"]:
                sync.wait_ge(SEM[sname], snap[sname])
        on_sync(_)

    xT_t = xT.rearrange("(i p) t -> i p t", p=D)
    xpa_t = xparkA.rearrange("(i p) t -> i p t", p=D)
    xpb_t = xparkB.rearrange("(i p) t -> i p t", p=D)
    yp_t = ypark.rearrange("(i p) t -> i p t", p=D)
    yT_t = yT.rearrange("(i p) t -> i p t", p=D)
    wg_sh_t = wg_sh.rearrange("(j i p) f -> j i p f", i=ET, p=D)
    wu_sh_t = wu_sh.rearrange("(j i p) f -> j i p f", i=ET, p=D)
    wd_sh_t = wd_sh.rearrange("(j p) t -> j p t", p=D)
    wg_all_t = wg_all.rearrange("(j i p) f -> j i p f", i=ET, p=D)
    wu_all_t = wu_all.rearrange("(j i p) f -> j i p f", i=ET, p=D)
    wd_all_t = wd_all.rearrange("(j p) t -> j p t", p=D)
    kin_t = kag_in.rearrange("(k p) t -> k p t", p=D)
    kout_t = kag_out.rearrange("(r k p) t -> r k p t", k=KV, p=D)

    blk = es.enter_context(nc.Block())

    class ESync:
        OPS = {"tensor_scalar", "tensor_tensor", "scalar_tensor_tensor", "tensor_copy",
               "reciprocal", "tensor_reduce", "activation", "copy", "sign", "sqrt",
               "square", "mul", "add", "tensor_add", "tensor_mul", "tensor_sub",
               "select", "tensor_tensor_scan"}

        def __init__(self, eng, key):
            self._e = eng
            self._k = key

        def __getattr__(self, n):
            a = getattr(self._e, n)
            if n in ESync.OPS:
                def f(*args, **kw):
                    self._e.wait_ge(SEM[self._k], C[self._k])
                    r = a(*args, **kw)
                    C[self._k] += 1
                    r.then_inc(SEM[self._k], 1)
                    return r
                return f
            return a

    def on_v(f):
        blk.vector(lambda v: f(ESync(v, "dv")))

    def on_s(f):
        blk.scalar(lambda sc: f(ESync(sc, "ac")))

    on_sync, on_g, on_t = blk.sync, blk.gpsimd, blk.tensor

    def p0():
        p0s = ExitStack()
        wsc_s = p0s.enter_context(nc.sbuf_tensor("wsc_s", [D, 3 * D], F32))
        def _(sync):
            for dst, src in ((kc_s, kc), (nqc_s, nqw_c), (nrc_s, nrw_c),
                             (gq_s, gqv), (gk_s, gkv), (pm_s, pmask),
                             (onfr_s, onesf_r), (onfc_s, onesf_c),
                             (onbc_s, onesb_c)):
                dma(sync, dst[:], src, "ld")
            dma(sync, beta_s[0:1, 0:1], beta_in, "ld")
            dma(sync, wsc_s[:, 0:D], wqa, "ld")
            dma(sync, wsc_s[:, D:2 * D], wka, "ld")
            dma(sync, wsc_s[:, 2 * D:3 * D], wva, "ld")
        on_sync(_)
        mark("p0_ld")

        def _(s):
            wof(s, "p0_ld", "ld")
            s.copy(out=wqa_s[:], in_=wsc_s[:, 0:D])
            s.copy(out=wka_s[:], in_=wsc_s[:, D:2 * D])
            s.copy(out=wva_s[:], in_=wsc_s[:, 2 * D:3 * D])
            inc(s.activation(out=beta_s[0:1, 1:2], in_=beta_s[0:1, 0:1], func=AF.Exp, scale=-1.0), "ac")
        on_s(_)
        mark("p0_exp")

        def _(v):
            wof(v, "p0_ld", "ld")
            v.tensor_scalar(out=gqsc_s[:], in0=gq_s[:], scalar1=ISD, scalar2=None, op0=OP.mult)
            wof(v, "p0_exp", "ac")
            v.tensor_scalar(out=beta_s[0:1, 2:3], in0=beta_s[0:1, 1:2], scalar1=1.0, scalar2=None, op0=OP.add)
            inc(v.reciprocal(out=beta_s[0:1, 3:4], in_=beta_s[0:1, 2:3]), "dv")
        on_v(_)
        mark("p0_sigb")

        def _(t):
            wof(t, "p0_sigb", "dv")
            inc(t.matmul(psA[0:D, 0:1], onfr_s[0:1, :], beta_s[0:1, 3:4], start=True, stop=True), "mm")
        on_t(_)
        mark("p0_bcb")

        def _(s):
            wof(s, "p0_bcb", "mm")
            inc(s.copy(out=sigb_s[:], in_=psA[0:D, 0:1]), "ac")
        on_s(_)
        mark("p0_done")
        p0s.close()

    def p1():
        qgate()
        p1s = ExitStack()
        P = p1s.enter_context
        cod_b = [P(nc.sbuf_tensor(f"cod{b}", [D, T], U8)) for b in range(2)]
        scl_b = [P(nc.sbuf_tensor(f"sclb{b}", [D, T], F32)) for b in range(2)]
        dq_t = P(nc.sbuf_tensor("dq_t", [D, T], U8))
        dq_sgn = P(nc.sbuf_tensor("dq_sgn", [D, T], U8))
        dq_sg = P(nc.sbuf_tensor("dq_sg", [D, T], F32))
        dq_r1 = P(nc.sbuf_tensor("dq_r1", [D, T], F32))
        dq_r2 = P(nc.sbuf_tensor("dq_r2", [D, T], F32))
        dq_m = P(nc.sbuf_tensor("dq_m", [D, T], F32))
        dq_w = [P(nc.sbuf_tensor(f"dq_w{b}", [D, T], BF16)) for b in range(2)]

        parts = [("g", gcodes, gscl), ("u", ucodes, uscl), ("d", dcodes, dscl)]
        for ip in range(24):
            part, codes_d, scl_d = parts[ip // 8]
            ci = ip % 8
            b = ip % 2

            def _(sync, ip=ip, b=b, codes_d=codes_d, scl_d=scl_d, ci=ci):
                if ip >= 2:
                    sync.wait_ge(SEM["dv"], marks[f"decw_{ip-2}_1"]["dv"])
                dma(sync, cod_b[b][:], codes_d[128 * ci:128 * ci + 128, :], "ld")
                dma(sync, scl_b[b][:], scl_d[128 * ci:128 * ci + 128, :], "ld")
            on_sync(_)
            mark(f"decld_{ip}")

            for half in range(2):
                def _(v, ip=ip, b=b, half=half):
                    if half == 0:
                        wof(v, f"decld_{ip}", "ld")
                        if ip >= 1:
                            v.wait_ge(SEM["ac"], marks[f"decact_{ip-1}_1"]["ac"])
                    else:
                        v.wait_ge(SEM["ac"], marks[f"decact_{ip}_0"]["ac"])
                    if half == 0:
                        v.tensor_scalar(out=dq_t[:], in0=cod_b[b][:], scalar1=7, scalar2=None, op0=OP.bitwise_and)
                        v.tensor_scalar(out=dq_sgn[:], in0=cod_b[b][:], scalar1=8, scalar2=None, op0=OP.bitwise_and)
                    else:
                        v.tensor_scalar(out=dq_t[:], in0=cod_b[b][:], scalar1=4, scalar2=7,
                                        op0=OP.logical_shift_right, op1=OP.bitwise_and)
                        v.tensor_scalar(out=dq_sgn[:], in0=cod_b[b][:], scalar1=4, scalar2=8,
                                        op0=OP.logical_shift_right, op1=OP.bitwise_and)
                    inc(v.tensor_scalar(out=dq_sg[:], in0=dq_sgn[:], scalar1=-0.25, scalar2=1.0,
                                        op0=OP.mult, op1=OP.add), "dv")
                on_v(_)
                mark(f"dect_{ip}_{half}")

                def _(s, ip=ip, half=half):
                    wof(s, f"dect_{ip}_{half}", "dv")
                    s.activation(out=dq_r1[:], in_=dq_t[:], func=AF.Relu, scale=0.5, bias=kc_s[:, 0:1])
                    inc(s.activation(out=dq_r2[:], in_=dq_t[:], func=AF.Relu, bias=kc_s[:, 1:2]), "ac")
                on_s(_)
                mark(f"decact_{ip}_{half}")

                def _(v, ip=ip, b=b, half=half):
                    wof(v, f"decact_{ip}_{half}", "ac")
                    v.scalar_tensor_tensor(out=dq_m[:], in0=dq_t[:], scalar=0.5, in1=dq_r1[:],
                                           op0=OP.mult, op1=OP.add)
                    v.tensor_tensor(out=dq_m[:], in0=dq_m[:], in1=dq_r2[:], op=OP.add)
                    v.tensor_tensor(out=dq_m[:], in0=dq_m[:], in1=dq_sg[:], op=OP.mult)
                    if ip >= 1:
                        [v.wait_ge(SEM[_s], marks[f"decst_{ip-1}"][_s]) for _s in POOL["g"]]
                    inc(v.tensor_tensor(out=dq_w[half][:], in0=dq_m[:], in1=scl_b[b][:], op=OP.mult), "dv")
                on_v(_)
                mark(f"decw_{ip}_{half}")

            def _(g, ip=ip, part=part, ci=ci):
                wof(g, f"decw_{ip}_1", "dv")
                if part in ("g", "u"):
                    dst = wg_sh_t if part == "g" else wu_sh_t
                    for half in range(2):
                        d_ap = dst[:, 2 * ci + half, :, :].rearrange("j p f -> p j f")
                        s_ap = dq_w[half].ap().rearrange("p (j f) -> p j f", f=D)
                        dma(g, d_ap, s_ap, "g")
                else:
                    for half in range(2):
                        dma(g, wd_sh_t[2 * ci + half], dq_w[half][:], "g")
            on_g(_)
            mark(f"decst_{ip}")
        mark("dec_all")

        def _(g):
            wof(g, "dec_all", "g")
            g.wait_ge(SEM["cc"], C["cc"])
            inc(g.collective_compute("AllGather", OP.bypass, replica_groups=[list(range(NCORE))],
                                     ins=[wg_sh.opt()], outs=[wg_all.opt()]), "cc")
            mark("wag_g")
            g.wait_ge(SEM["cc"], C["cc"])
            inc(g.collective_compute("AllGather", OP.bypass, replica_groups=[list(range(NCORE))],
                                     ins=[wu_sh.opt()], outs=[wu_all.opt()]), "cc")
            mark("wag_u")
            g.wait_ge(SEM["cc"], C["cc"])
            inc(g.collective_compute("AllGather", OP.bypass, replica_groups=[list(range(NCORE))],
                                     ins=[wd_sh.opt()], outs=[wd_all.opt()]), "cc")
            mark("wag_d")
        on_g(_)
        p1s.close()

    def stream_stats(name, src_fn, sb1, sb2, sq, add_fn=None, park_fn=None, xb=None,
                     xbmark=None):
        for i in range(ET):
            buf = (sb1, sb2)[i % 2] if src_fn else None
            src = src_fn(i) if src_fn else None
            dep = None
            if src is not None:
                def _(sync, i=i, buf=buf, src=src):
                    w = 1 if add_fn is not None else 2
                    if i >= w:
                        sync.wait_ge(SEM["mm"], marks[f"{name}_mm_{i-w}"]["mm"])
                        if park_fn is not None:
                            [sync.wait_ge(SEM[_s], marks[f"{name}_pk_{i-w}"][_s]) for _s in POOL["g"]]
                    dma(sync, buf[:], src, "ld")
                    if add_fn is not None:
                        dma(sync, sq[:], add_fn(i), "ld")
                on_sync(_)
                mark(f"{name}_ld_{i}")
                dep = (f"{name}_ld_{i}", "ld")

                if add_fn is not None:
                    def _(v, i=i, buf=buf):
                        wof(v, f"{name}_ld_{i}", "ld")
                        if i >= 1:
                            v.wait_ge(SEM["ac"], marks[f"{name}_sq_{i-1}"]["ac"])
                        inc(v.tensor_tensor(out=buf[:], in0=buf[:], in1=sq[:], op=OP.add), "dv")
                    on_v(_)
                    mark(f"{name}_add_{i}")
                    dep = (f"{name}_add_{i}", "dv")

                if park_fn is not None:
                    def _(g, i=i, buf=buf, dep=dep):
                        wof(g, dep[0], dep[1])
                        dma(g, park_fn(i), buf[:], "g")
                    on_g(_)
                    mark(f"{name}_pk_{i}")

            def _(s, i=i, buf=buf, src=src, dep=dep):
                if dep is not None:
                    wof(s, dep[0], dep[1])
                elif i == 0 and xbmark is not None:
                    wof(s, xbmark[0], xbmark[1])
                if i >= 1:
                    s.wait_ge(SEM["mm"], marks[f"{name}_mm_{i-1}"]["mm"])
                a = buf[:] if src is not None else xb[:, T * i:T * i + T]
                inc(s.activation(out=sq[:], in_=a, func=AF.Square), "ac")
            on_s(_)
            mark(f"{name}_sq_{i}")

            def _(t, i=i, buf=buf, src=src, dep=dep):
                if dep is not None:
                    wof(t, dep[0], dep[1])
                wof(t, f"{name}_sq_{i}", "ac")
                last = None
                for c in range(TC):
                    sl = slice(CH * c, CH * c + CH)
                    a = buf[:, sl] if src is not None else xb[:, T * i + CH * c:T * i + CH * c + CH]
                    last = t.matmul(psA[0:1, sl], onfc_s[:], a, start=(i == 0), stop=(i == ET - 1))
                for c in range(TC):
                    sl = slice(CH * c, CH * c + CH)
                    last = t.matmul(psB[0:1, sl], onfc_s[:], sq[:, sl], start=(i == 0), stop=(i == ET - 1))
                inc(last, "mm")
            on_t(_)
            mark(f"{name}_mm_{i}")
        mark(f"{name}_stats")

    def ln_rows(name, kind="ln"):
        def _(v):
            wof(v, f"{name}_stats", "mm")
            v.tensor_scalar(out=R2[0:1, 0:T], in0=psA[0:1, 0:T], scalar1=1.0 / E, scalar2=None, op0=OP.mult)
            v.tensor_scalar(out=R1[0:1, 0:T], in0=psB[0:1, 0:T], scalar1=1.0 / E, scalar2=None, op0=OP.mult)
            v.tensor_tensor(out=R3[0:1, 0:T], in0=R2[0:1, 0:T], in1=R2[0:1, 0:T], op=OP.mult)
            inc(v.tensor_tensor(out=R3[0:1, 0:T], in0=R1[0:1, 0:T], in1=R3[0:1, 0:T], op=OP.subtract), "dv")
        on_v(_)
        mark(f"{name}_var")

        def _(sc):
            wof(sc, f"{name}_var", "dv")
            sc.activation(out=R1[0:1, 0:T], in_=R3[0:1, 0:T], func=AF.Ln, bias=kc_s[0:1, 3:4])
            inc(sc.activation(out=R1[0:1, 0:T], in_=R1[0:1, 0:T], func=AF.Exp, scale=-0.5), "ac")
        on_s(_)
        mark(f"{name}_rstd")

        if kind == "lnq":
            def _(v):
                wof(v, f"{name}_rstd", "ac")
                v.tensor_scalar(out=R4[0:1, 0:T], in0=R3[0:1, 0:T], scalar1=1e-5, scalar2=None, op0=OP.add)
                v.reciprocal(out=R4[0:1, 0:T], in_=R4[0:1, 0:T])
                inc(v.tensor_tensor(out=R4[0:1, 0:T], in0=R3[0:1, 0:T], in1=R4[0:1, 0:T], op=OP.mult), "dv")
            on_v(_)
            mark(f"{name}_msql")

            def _(sc):
                wof(sc, f"{name}_msql", "dv")
                sc.activation(out=R4[0:1, 0:T], in_=R4[0:1, 0:T], func=AF.Ln, bias=kc_s[0:1, 2:3])
                inc(sc.activation(out=R4[0:1, 0:T], in_=R4[0:1, 0:T], func=AF.Exp, scale=-0.5), "ac")
            on_s(_)
            mark(f"{name}_rsq")

            def _(v):
                wof(v, f"{name}_rsq", "ac")
                v.tensor_tensor(out=R3[0:1, 0:T], in0=R1[0:1, 0:T], in1=R4[0:1, 0:T], op=OP.mult)
                inc(v.tensor_tensor(out=R4[0:1, 0:T], in0=R2[0:1, 0:T], in1=R3[0:1, 0:T], op=OP.mult), "dv")
            on_v(_)
        else:
            def _(v):
                wof(v, f"{name}_rstd", "ac")
                inc(v.tensor_tensor(out=R2[0:1, 0:T], in0=R2[0:1, 0:T], in1=R1[0:1, 0:T], op=OP.mult), "dv")
            on_v(_)
        mark(f"{name}_rows")

    def emit_lnx(name, rowsmark, src_fn, dst, sb1, sb2, sq, wcol=None, arow=None, brow=None):
        arow = arow if arow is not None else R1
        brow = brow if brow is not None else R2
        for i in range(ET):
            buf = (sb1, sb2)[i % 2]

            def _(sync, i=i, buf=buf):
                if i >= 2:
                    sync.wait_ge(SEM["dv"], marks[f"{name}_o_{i-2}"]["dv"])
                    sync.wait_ge(SEM["ac"], marks[f"{name}_o_{i-2}"]["ac"])
                dma(sync, buf[:], src_fn(i), "ld")
            on_sync(_)
            mark(f"{name}_ld_{i}")

            def _(t, i=i):
                wof(t, rowsmark, "dv")
                wof(t, f"{name}_ld_{i}", "ld")
                if i >= 1:
                    t.wait_ge(SEM["dv"], marks[f"{name}_o_{i-1}"]["dv"])
                last = None
                for c in range(TC):
                    sl = slice(CH * c, CH * c + CH)
                    t.matmul(psA[:, sl], onfr_s[0:1, 0:D], arow[0:1, sl], start=True, stop=True)
                    last = t.matmul(psB[:, sl], onfr_s[0:1, 0:D], brow[0:1, sl], start=True, stop=True)
                inc(last, "mm")
            on_t(_)
            mark(f"{name}_bc_{i}")

            def _(v, i=i, buf=buf):
                wof(v, f"{name}_bc_{i}", "mm")
                for c in range(TC):
                    sl = slice(CH * c, CH * c + CH)
                    v.tensor_tensor(out=sq[:, sl], in0=buf[:, sl], in1=psA[:, sl], op=OP.mult)
                last = None
                for c in range(TC):
                    sl = slice(CH * c, CH * c + CH)
                    if wcol is None:
                        last = v.tensor_tensor(out=dst[:, T * i + CH * c:T * i + CH * c + CH],
                                               in0=sq[:, sl], in1=psB[:, sl], op=OP.subtract)
                    else:
                        last = v.tensor_tensor(out=buf[:, sl], in0=sq[:, sl], in1=psB[:, sl], op=OP.subtract)
                inc(last, "dv")
            on_v(_)
            mark(f"{name}_vo_{i}")

            if wcol is not None:
                def _(sc, i=i, buf=buf):
                    wof(sc, f"{name}_vo_{i}", "dv")
                    inc(sc.activation(out=dst[:, T * i:T * i + T], in_=buf[:], func=AF.Copy,
                                      scale=wcol[:, i:i + 1]), "ac")
                on_s(_)
                mark(f"{name}_so_{i}")

            def noop():
                pass
            marks[f"{name}_o_{i}"] = dict(marks[f"{name}_so_{i}"] if wcol is not None else marks[f"{name}_vo_{i}"])
        mark(f"{name}_done")

    def p2():
        qgate()
        p2s = ExitStack()
        P = p2s.enter_context
        NQ = P(nc.sbuf_tensor("NQ", [D, ET * T], BF16))
        AFl = P(nc.sbuf_tensor("AFl", [D, ET * T], BF16))

        pre = ExitStack()
        sb1 = pre.enter_context(nc.sbuf_tensor("p2s1", [D, T], F32))
        sb2 = pre.enter_context(nc.sbuf_tensor("p2s2", [D, T], F32))
        sq = pre.enter_context(nc.sbuf_tensor("p2sq", [D, T], F32))
        stream_stats("axs", lambda i: xT_t[i], sb1, sb2, sq)
        ln_rows("axs", kind="lnq")
        qgate()
        emit_lnx("anq", "axs_rows", lambda i: xT_t[i], NQ, sb1, sb2, sq, wcol=nqc_s, arow=R3, brow=R4)
        if STAGE == 0:
            for i in range(ET):
                def _(sc, i=i):
                    wof(sc, "anq_done", "dv" if False else "ac")
                    if i >= 1:
                        [sc.wait_ge(SEM[_s], marks[f"st0_{i-1}"][_s]) for _s in POOL["g"]]
                    inc(sc.copy(out=sb1[:], in_=NQ[:, T * i:T * i + T]), "ac")
                on_s(_)
                mark(f"st0c_{i}")

                def _(g, i=i):
                    wof(g, f"st0c_{i}", "ac")
                    dma(g, xpa_t[i], sb1[:], "g")
                on_g(_)
                mark(f"st0_{i}")
            pre.close()
            p2s.close()
            return
        pre.close()

        qgate()
        hp_s = ExitStack()
        P2 = hp_s.enter_context
        wst = P2(nc.sbuf_tensor("awst", [D, ET * D // 2], F32))
        wih = [P2(nc.sbuf_tensor(f"awih{b}", [D, ET * D], BF16)) for b in range(2)]
        pats = P2(nc.sbuf_tensor("apats", [D, MA // 2], F32))
        patb = [P2(nc.sbuf_tensor(f"apatb{b}", [D, MA], BF16)) for b in range(2)]
        projb = P2(nc.sbuf_tensor("aprojb", [D, T], BF16))
        qb = [P2(nc.sbuf_tensor(f"aqb{w}", [D, T], BF16)) for w in range(2)]
        kb = [P2(nc.sbuf_tensor(f"akb{w}", [D, MA], BF16)) for w in range(2)]
        vb = [P2(nc.sbuf_tensor(f"avb{w}", [D, MA], BF16)) for w in range(2)]
        pbuf = [P2(nc.sbuf_tensor(f"apb{w}", [D, CH], BF16)) for w in range(4)]
        rb = P2(nc.sbuf_tensor("arb", [D, CH], F32))
        svc = P2(nc.sbuf_tensor("asvc", [D, 2], F32))
        af1 = P2(nc.sbuf_tensor("aaf1", [D, CH], F32))

        for hp in range(H // 2):
            for hw in range(2):
                h = 2 * hp + hw

                for wh in range(2):
                    def _(sync, h=h, wh=wh):
                        if h + wh >= 1:
                            ph, pw = (h, 0) if wh == 1 else (h - 1, 1)
                            sync.wait_ge(SEM["ac"], marks[f"a_wcvt_{ph}_{pw}"]["ac"])
                        dma(sync, wst.ap().rearrange("p (i f) -> p i f", f=D),
                            winT[E // 2 * wh:E // 2 * (wh + 1), D * h:D * h + D].rearrange("(i p) f -> p i f", p=D), "ld")
                        dma(sync, pats[:], patT[D * h:D * h + D, MA // 2 * wh:MA // 2 * (wh + 1)], "ld")
                    on_sync(_)
                    mark(f"a_wldh_{h}_{wh}")

                    def _(s, h=h, hw=hw, wh=wh):
                        wof(s, f"a_wldh_{h}_{wh}", "ld")
                        if h >= 2 and wh == 0:
                            s.wait_ge(SEM["mm"], marks[f"a_kv_{h-2}"]["mm"])
                        s.copy(out=wih[hw][:, ET * D // 2 * wh:ET * D // 2 * (wh + 1)], in_=wst[:])
                        inc(s.copy(out=patb[hw][:, MA // 2 * wh:MA // 2 * (wh + 1)], in_=pats[:]), "ac")
                    on_s(_)
                    mark(f"a_wcvt_{h}_{wh}")
                mark(f"a_wcvt_{h}")

                def _(t, h=h, hw=hw, hp=hp):
                    wof(t, f"a_wcvt_{h}_1", "ac")
                    wof(t, "anq_done", "dv")
                    if h >= 1:
                        t.wait_ge(SEM["ac"], marks[f"a_kv_ev_{h-1}"]["ac"])
                    if hp >= 1 and hw == 0:
                        t.wait_ge(SEM["dv"], marks[f"aat_{hp-1}_{TC-1}_fb"]["dv"])
                    last = None
                    for i in range(ET):
                        for c in range(TC):
                            sl = slice(CH * c, CH * c + CH)
                            last = t.matmul(psA[:, sl], wih[hw][:, D * i:D * i + D],
                                            NQ[:, T * i + CH * c:T * i + CH * c + CH],
                                            start=(i == 0), stop=(i == ET - 1))
                    inc(last, "mm")
                on_t(_)
                mark(f"a_proj_{h}")

                def _(s, h=h):
                    wof(s, f"a_proj_{h}", "mm")
                    if h >= 1:
                        s.wait_ge(SEM["mm"], marks[f"a_q_{h-1}"]["mm"])
                    inc(s.copy(out=projb[:], in_=psA[:, 0:T]), "ac")
                on_s(_)
                mark(f"a_projev_{h}")

                def _(t, h=h):
                    wof(t, f"a_projev_{h}", "ac")
                    if h >= 1:
                        t.wait_ge(SEM["ac"], marks[f"a_svc_{h-1}"]["ac"])
                    last = None
                    for c in range(TC):
                        sl = slice(CH * c, CH * c + CH)
                        last = t.matmul(psB[:, sl], wqa_s[:], projb[:, sl], start=True, stop=True)
                    inc(last, "mm")
                on_t(_)
                mark(f"a_q_{h}")

                def _(s, h=h, hw=hw):
                    wof(s, f"a_q_{h}", "mm")
                    inc(s.activation(out=qb[hw][:], in_=psB[:, 0:T], func=AF.Copy, scale=sigb_s[:]), "ac")
                on_s(_)
                mark(f"a_qev_{h}")

                def _(t, h=h, hw=hw):
                    wof(t, f"a_qev_{h}", "ac")
                    t.matmul(psA[:, 0:CH], wka_s[:], patb[hw][:, 0:CH], start=True, stop=True)
                    t.matmul(psA[:, CH:MA], wka_s[:], patb[hw][:, CH:MA], start=True, stop=True)
                    last = None
                    for mt in range(8):
                        last = t.matmul(psA[:, MA + D * mt:MA + D * mt + D],
                                        patb[hw][:, D * mt:D * mt + D], wva_s[:], start=True, stop=True)
                    inc(last, "mm")
                on_t(_)
                mark(f"a_kv_{h}")

                def _(s, h=h, hw=hw):
                    wof(s, f"a_kv_{h}", "mm")
                    s.copy(out=kb[hw][:], in_=psA[:, 0:MA])
                    inc(s.copy(out=vb[hw][:], in_=psA[:, MA:2 * MA]), "ac")
                on_s(_)
                mark(f"a_kv_ev_{h}")

                def _(t, h=h, hw=hw):
                    wof(t, f"a_kv_ev_{h}", "ac")
                    last = None
                    for par in range(2):
                        for mt in range(8):
                            last = t.matmul(psB[64 * hw:64 * hw + 64, 1024 + par:1024 + par + 1],
                                            vb[hw][:, D * mt + 64 * par:D * mt + 64 * par + 64],
                                            onbc_s[:], start=(mt == 0), stop=(mt == 7),
                                            tile_position=(0, 64 * hw))
                    inc(last, "mm")
                on_t(_)
                mark(f"a_sv_{h}")

                def _(s, h=h, hw=hw):
                    wof(s, f"a_sv_{h}", "mm")
                    inc(s.copy(out=svc[64 * hw:64 * hw + 64, 0:2], in_=psB[64 * hw:64 * hw + 64, 1024:1026]), "ac")
                on_s(_)
                mark(f"a_svc_{h}")

            for tc in range(TC):
                tsl = slice(CH * tc, CH * tc + CH)
                tag = f"aat_{hp}_{tc}"
                steps = [(hw, mt) for hw in range(2) for mt in range(8)]
                ns = len(steps)
                ADL = 3
                SDEP = 3
                EDEP = 2
                PDEP = 4
                mm0, dv0, ac0 = C["mm"], C["dv"], C["ac"]
                SSL = [(psA, 0), (psA, CH), (psB, CH)]

                def posc(si):
                    return mm0 + si + 1 + max(0, si - ADL)

                def posd(j):
                    return mm0 + min(j + ADL + 1, ns) + j + 1

                def _(s, tag=tag, ns=ns, mm0=mm0, dv0=dv0, ac0=ac0, posc=posc):
                    for si in range(ns):
                        pst, off = SSL[si % SDEP]
                        ep = si % EDEP
                        s.wait_ge(SEM["mm"], posc(si))
                        s.wait_ge(SEM["dv"], dv0 + si - EDEP + 1 if si >= EDEP else dv0)
                        s.activation(out=psB[:, 1024 + CH * ep:1024 + CH * ep + CH],
                                     in_=pst[:, off:off + CH],
                                     func=AF.Exp).then_inc(SEM["ac"], 1)
                blk.scalar(_)

                def _(v, tag=tag, ns=ns, mm0=mm0, ac0=ac0, posd=posd):
                    for si in range(ns):
                        ep = si % EDEP
                        v.wait_ge(SEM["ac"], ac0 + si + 1)
                        v.wait_ge(SEM["mm"], posd(si - PDEP) if si >= PDEP else mm0)
                        v.tensor_scalar(out=pbuf[si % PDEP][:],
                                        in0=psB[:, 1024 + CH * ep:1024 + CH * ep + CH],
                                        scalar1=-1.0, scalar2=None,
                                        op0=OP.add).then_inc(SEM["dv"], 1)
                blk.vector(_)

                def _(t, steps=steps, tag=tag, tsl=tsl, hp=hp, tc=tc, ns=ns,
                      dv0=dv0, ac0=ac0):
                    for si in range(ns + ADL):
                        if si < ns:
                            hw, mt = steps[si]
                            pst, off = SSL[si % SDEP]
                            if si == 0:
                                wof(t, f"a_kv_ev_{2*hp+1}", "ac")
                                wof(t, f"a_svc_{2*hp+1}", "ac")
                                if tc > 0:
                                    wof(t, f"aat_{hp}_{tc-1}_fb", "dv")
                            t.wait_ge(SEM["ac"], ac0 + si - SDEP + 1 if si >= SDEP else ac0)
                            t.matmul(pst[:, off:off + CH], kb[hw][:, D * mt:D * mt + D],
                                     qb[hw][:, tsl], start=True,
                                     stop=True).then_inc(SEM["mm"], 1)
                        if si >= ADL:
                            j = si - ADL
                            jhw, jmt = steps[j]
                            jp = j % PDEP
                            t.wait_ge(SEM["dv"], dv0 + j + 1)
                            r = 32 * jhw
                            t.matmul(psA[r:r + 1, MA:MA + CH], onbc_s[:], pbuf[jp][:],
                                     start=(jmt == 0), stop=(jmt == 7))
                            t.matmul(psA[64 * jhw:64 * jhw + 64, 1536:2048],
                                     vb[jhw][:, D * jmt:D * jmt + 64], pbuf[jp][:],
                                     start=(jmt == 0), stop=(jmt == 7), tile_position=(0, 64 * jhw))
                            t.matmul(psB[64 * jhw:64 * jhw + 64, 0:CH],
                                     vb[jhw][:, D * jmt + 64:D * jmt + 128], pbuf[jp][:],
                                     start=(jmt == 0), stop=(jmt == 7),
                                     tile_position=(0, 64 * jhw)).then_inc(SEM["mm"], 1)
                on_t(_)
                C["mm"] += 2 * ns
                C["dv"] += ns
                C["ac"] += ns
                mark(f"{tag}_da_{ns-1}")

                def _(v, tag=tag):
                    wof(v, f"{tag}_da_15", "mm")
                    v.tensor_scalar(out=R4[0:1, 0:CH], in0=psA[0:1, MA:MA + CH], scalar1=float(MA), scalar2=None, op0=OP.add)
                    v.tensor_scalar(out=R4[0:1, CH:2 * CH], in0=psA[32:33, MA:MA + CH], scalar1=float(MA), scalar2=None, op0=OP.add)
                    v.reciprocal(out=R4[0:1, 0:CH], in_=R4[0:1, 0:CH])
                    inc(v.reciprocal(out=R4[0:1, CH:2 * CH], in_=R4[0:1, CH:2 * CH]), "dv")
                on_v(_)
                mark(f"{tag}_rd")

                def _(t, tag=tag):
                    wof(t, f"{tag}_rd", "dv")
                    t.matmul(psB[0:64, CH:2 * CH], onfr_s[0:1, 0:64], R4[0:1, 0:CH],
                             start=True, stop=True)
                    inc(t.matmul(psB[64:128, CH:2 * CH], onfr_s[0:1, 0:64], R4[0:1, CH:2 * CH],
                                 start=True, stop=True, tile_position=(0, 64)), "mm")
                on_t(_)
                mark(f"{tag}_rbc")

                def _(s, tag=tag):
                    wof(s, f"{tag}_rbc", "mm")
                    inc(s.copy(out=rb[:], in_=psB[:, CH:2 * CH]), "ac")
                on_s(_)
                mark(f"{tag}_rbs")

                def _(v, tag=tag, hp=hp, tc=tc):
                    wof(v, f"{tag}_rbs", "ac")
                    v.tensor_scalar(out=af1[:], in0=psA[:, 1536:2048], scalar1=svc[:, 0:1], scalar2=None, op0=OP.add)
                    v.tensor_tensor(out=AFl[:, T * (2 * hp) + CH * tc:T * (2 * hp) + CH * tc + CH],
                                    in0=af1[:], in1=rb[:], op=OP.mult)
                    v.tensor_scalar(out=af1[:], in0=psB[:, 0:CH], scalar1=svc[:, 1:2], scalar2=None, op0=OP.add)
                    inc(v.tensor_tensor(out=AFl[:, T * (2 * hp + 1) + CH * tc:T * (2 * hp + 1) + CH * tc + CH],
                                        in0=af1[:], in1=rb[:], op=OP.mult), "dv")
                on_v(_)
                mark(f"{tag}_fb")
        mark("a_attn_done")
        hp_s.close()

        if STAGE == 15:
            qgate()
            co15 = ExitStack()
            c15 = co15.enter_context(nc.sbuf_tensor("c15", [D, T], F32))
            for i in range(ET):
                def _(sc, i=i):
                    if i == 0:
                        wof(sc, "a_attn_done", "dv")
                    if i >= 1:
                        [sc.wait_ge(SEM[_s], marks[f"st15_{i-1}"][_s]) for _s in POOL["g"]]
                    inc(sc.copy(out=c15[:], in_=AFl[:, T * i:T * i + T]), "ac")
                on_s(_)
                mark(f"st15c_{i}")

                def _(g, i=i):
                    wof(g, f"st15c_{i}", "ac")
                    dma(g, xpa_t[i], c15[:], "g")
                on_g(_)
                mark(f"st15_{i}")
            co15.close()
            p2s.close()
            return

        qgate()
        co = ExitStack()
        cb1 = co.enter_context(nc.sbuf_tensor("acb1", [D, T], F32))
        cb2 = co.enter_context(nc.sbuf_tensor("acb2", [D, T], F32))
        cw1 = co.enter_context(nc.sbuf_tensor("acw1", [D, T], F32))
        csq = co.enter_context(nc.sbuf_tensor("acsq", [D, T], F32))

        for i in range(ET):
            def _(s, i=i):
                if i == 0:
                    wof(s, "a_attn_done", "dv")
                if i >= 1:
                    s.wait_ge(SEM["mm"], marks[f"afs_mm_{i-1}"]["mm"])
                inc(s.activation(out=csq[:], in_=AFl[:, T * i:T * i + T], func=AF.Square), "ac")
            on_s(_)
            mark(f"afs_sq_{i}")

            def _(t, i=i):
                wof(t, f"afs_sq_{i}", "ac")
                if i == 0:
                    wof(t, "a_attn_done", "dv")
                last = None
                for c in range(TC):
                    sl = slice(CH * c, CH * c + CH)
                    last = t.matmul(psA[0:1, sl], onfc_s[:], csq[:, sl], start=(i == 0), stop=(i == ET - 1))
                inc(last, "mm")
            on_t(_)
            mark(f"afs_mm_{i}")
        mark("afs_stats")

        def _(s):
            wof(s, "afs_stats", "mm")
            s.activation(out=R3[0:1, 0:T], in_=psA[0:1, 0:T], func=AF.Ln, scale=1.0 / E, bias=kc_s[0:1, 2:3])
            inc(s.activation(out=R3[0:1, 0:T], in_=R3[0:1, 0:T], func=AF.Exp, scale=-0.5), "ac")
        on_s(_)
        mark("afs_rr")

        def _(v):
            wof(v, "afs_rr", "ac")
            v.tensor_scalar(out=R4[0:1, 0:T], in0=R1[0:1, 0:T], scalar1=1.0, scalar2=None, op0=OP.add)
            inc(v.tensor_tensor(out=R2[0:1, 0:T], in0=R2[0:1, 0:T], in1=R1[0:1, 0:T], op=OP.mult), "dv")
        on_v(_)
        mark("acmb_rows")

        for i in range(ET):
            buf = (cb1, cb2)[i % 2]

            def _(sync, i=i, buf=buf):
                if i >= 2:
                    [sync.wait_ge(SEM[_s], marks[f"acmb_pk_{i-2}"][_s]) for _s in POOL["g"]]
                dma(sync, buf[:], xT_t[i], "ld")
            on_sync(_)
            mark(f"acmb_ld_{i}")

            def _(t, i=i):
                wof(t, "acmb_rows", "dv")
                wof(t, f"acmb_ld_{i}", "ld")
                if i >= 1:
                    t.wait_ge(SEM["dv"], marks[f"acmb_o_{i-1}"]["dv"])
                    t.wait_ge(SEM["ac"], marks[f"acmb_w_{i-1}"]["ac"])
                last = None
                for c in range(TC):
                    sl = slice(CH * c, CH * c + CH)
                    t.matmul(psA[:, sl], onfr_s[0:1, 0:D], R3[0:1, sl], start=True, stop=True)
                    last = t.matmul(psB[:, sl], onfr_s[0:1, 0:D], R4[0:1, sl], start=True, stop=True)
                inc(last, "mm")
            on_t(_)
            mark(f"acmb_bc1_{i}")

            def _(v, i=i, buf=buf):
                wof(v, f"acmb_bc1_{i}", "mm")
                for c in range(TC):
                    sl = slice(CH * c, CH * c + CH)
                    v.tensor_tensor(out=csq[:, sl], in0=AFl[:, T * i + CH * c:T * i + CH * c + CH],
                                    in1=psA[:, sl], op=OP.mult)
                last = None
                for c in range(TC):
                    sl = slice(CH * c, CH * c + CH)
                    last = v.tensor_tensor(out=buf[:, sl], in0=buf[:, sl], in1=psB[:, sl], op=OP.mult)
                inc(last, "dv")
            on_v(_)
            mark(f"acmb_s1_{i}")

            def _(sc, i=i):
                wof(sc, f"acmb_s1_{i}", "dv")
                inc(sc.activation(out=cw1[:], in_=csq[:], func=AF.Copy, scale=nrc_s[:, i:i + 1]), "ac")
            on_s(_)
            mark(f"acmb_w_{i}")

            def _(t, i=i):
                wof(t, f"acmb_s1_{i}", "dv")
                last = None
                for c in range(TC):
                    sl = slice(CH * c, CH * c + CH)
                    last = t.matmul(psA[:, sl], onfr_s[0:1, 0:D], R2[0:1, sl], start=True, stop=True)
                inc(last, "mm")
            on_t(_)
            mark(f"acmb_bc2_{i}")

            def _(v, i=i, buf=buf):
                wof(v, f"acmb_bc2_{i}", "mm")
                wof(v, f"acmb_w_{i}", "ac")
                for c in range(TC):
                    sl = slice(CH * c, CH * c + CH)
                    v.tensor_tensor(out=buf[:, sl], in0=buf[:, sl], in1=psA[:, sl], op=OP.subtract)
                last = None
                for c in range(TC):
                    sl = slice(CH * c, CH * c + CH)
                    last = v.tensor_tensor(out=buf[:, sl], in0=buf[:, sl], in1=cw1[:, sl], op=OP.add)
                inc(last, "dv")
            on_v(_)
            mark(f"acmb_o_{i}")

            def _(g, i=i, buf=buf):
                wof(g, f"acmb_o_{i}", "dv")
                dma(g, xpa_t[i], buf[:], "g")
            on_g(_)
            mark(f"acmb_pk_{i}")
        mark("p2_done")
        co.close()
        p2s.close()

    def ffn(call):
        f = f"f{call}"

        qgate()
        lnb = ExitStack()
        LNX = lnb.enter_context(nc.sbuf_tensor(f"{f}lnx", [D, ET * T], BF16))
        pre = ExitStack()
        sb1 = pre.enter_context(nc.sbuf_tensor(f"{f}s1", [D, T], F32))
        sb2 = pre.enter_context(nc.sbuf_tensor(f"{f}s2", [D, T], F32))
        sq = pre.enter_context(nc.sbuf_tensor(f"{f}sq", [D, T], F32))
        if call == 1:
            stream_stats(f + "s", lambda i: xpa_t[i], sb1, sb2, sq)
        else:
            stream_stats(f + "s", lambda i: xpb_t[i], sb1, sb2, sq,
                         add_fn=lambda i: yp_t[i], park_fn=lambda i: xpa_t[i])
        ln_rows(f + "s", kind="ln")

        qgate()
        emit_lnx(f + "l", f + "s_rows", lambda i: xpa_t[i], LNX, sb1, sb2, sq)
        pre.close()

        qgate()
        pa = ExitStack()
        PA = pa.enter_context
        wgt = [PA(nc.sbuf_tensor(f"{f}wg{b}", [D, ET * D], BF16)) for b in range(2)]
        wut = [PA(nc.sbuf_tensor(f"{f}wu{b}", [D, ET * D], BF16)) for b in range(2)]
        g1 = PA(nc.sbuf_tensor(f"{f}g1", [D, MA], F32))
        g2 = PA(nc.sbuf_tensor(f"{f}g2", [D, MA], F32))
        ib = [PA(nc.sbuf_tensor(f"{f}ib{b}", [D, MA], BF16)) for b in range(2)]

        for j in range(NT):
            bj = j % 2

            def _(sync, j=j, bj=bj):
                if j == 0:
                    wof(sync, "wag_u", "cc")
                if j >= 2:
                    sync.wait_ge(SEM["mm"], marks[f"{f}a_mm_{j-2}_1"]["mm"])
                dma(sync, wgt[bj].ap().rearrange("p (i f) -> p i f", f=D),
                    wg_all_t[j].rearrange("i p f -> p i f"), "ld")
                dma(sync, wut[bj].ap().rearrange("p (i f) -> p i f", f=D),
                    wu_all_t[j].rearrange("i p f -> p i f"), "ld")
            on_sync(_)
            mark(f"{f}a_wld_{j}")

            for th in range(2):
                def _(t, j=j, bj=bj, th=th):
                    wof(t, f"{f}a_wld_{j}", "ld")
                    if j == 0 and th == 0:
                        wof(t, f + "l_done", "dv")
                    if j >= 1:
                        t.wait_ge(SEM["dv"], marks[f"{f}a_int_{j-1}_{th}"]["dv"])
                    last = None
                    for i in range(ET):
                        for c2 in range(2):
                            gsl = slice(MA * th + CH * c2, MA * th + CH * c2 + CH)
                            rhs_sl = slice(T * i + MA * th + CH * c2, T * i + MA * th + CH * c2 + CH)
                            t.matmul(psA[:, gsl], wgt[bj][:, D * i:D * i + D], LNX[:, rhs_sl],
                                     start=(i == 0), stop=(i == ET - 1))
                            last = t.matmul(psB[:, gsl], wut[bj][:, D * i:D * i + D], LNX[:, rhs_sl],
                                            start=(i == 0), stop=(i == ET - 1))
                    inc(last, "mm")
                on_t(_)
                mark(f"{f}a_mm_{j}_{th}")

                def _(s, j=j, th=th):
                    wof(s, f"{f}a_mm_{j}_{th}", "mm")
                    if j + th >= 1:
                        pj, pt = (j, 0) if th == 1 else (j - 1, 1)
                        s.wait_ge(SEM["dv"], marks[f"{f}a_int_{pj}_{pt}"]["dv"])
                    inc(s.activation(out=g1[:], in_=psA[:, MA * th:MA * th + MA], func=AF.Square), "ac")
                on_s(_)
                mark(f"{f}a_sq_{j}_{th}")

                def _(v, j=j, th=th):
                    wof(v, f"{f}a_sq_{j}_{th}", "ac")
                    v.tensor_scalar(out=g2[:], in0=g1[:], scalar1=C2P, scalar2=C1P, op0=OP.mult, op1=OP.add)
                    inc(v.tensor_tensor(out=g1[:], in0=psA[:, MA * th:MA * th + MA], in1=g2[:], op=OP.mult), "dv")
                on_v(_)
                mark(f"{f}a_gel1_{j}_{th}")

                def _(s, j=j, th=th):
                    wof(s, f"{f}a_gel1_{j}_{th}", "dv")
                    inc(s.activation(out=g2[:], in_=g1[:], func=AF.Sigmoid), "ac")
                on_s(_)
                mark(f"{f}a_sig_{j}_{th}")

                def _(v, j=j, th=th):
                    wof(v, f"{f}a_sig_{j}_{th}", "ac")
                    v.tensor_tensor(out=g1[:], in0=psA[:, MA * th:MA * th + MA], in1=g2[:], op=OP.mult)
                    if j >= 1:
                        [v.wait_ge(SEM[_s], marks[f"{f}a_ipk_{j-1}_{th}"][_s]) for _s in POOL["g"]]
                    inc(v.tensor_tensor(out=ib[th][:], in0=g1[:], in1=psB[:, MA * th:MA * th + MA], op=OP.mult), "dv")
                on_v(_)
                mark(f"{f}a_int_{j}_{th}")

                def _(g, j=j, th=th):
                    wof(g, f"{f}a_int_{j}_{th}", "dv")
                    dma(g, interp[j, :, MA * th:MA * th + MA], ib[th][:], "g")
                on_g(_)
                mark(f"{f}a_ipk_{j}_{th}")
        mark(f + "a_done")
        pa.close()
        lnb.close()

        qgate()
        pbx = ExitStack()
        XBl = pbx.enter_context(nc.sbuf_tensor(f"{f}XB", [D, ET * T], F32))
        pbi = ExitStack()
        IB = pbi.enter_context(nc.sbuf_tensor(f"{f}IB", [D, 8 * T], BF16))
        wdt = [pbi.enter_context(nc.sbuf_tensor(f"{f}wd{b}", [D, 8 * D], BF16)) for b in range(2)]

        for i in range(ET):
            def _(sync, i=i):
                dma(sync, XBl[:, T * i:T * i + T], xpa_t[i], "ld")
            on_sync(_)
            mark(f"{f}b_xr_{i}")

        for g_ in range(16):
            def _(sync, g_=g_):
                if g_ == 0:
                    wof(sync, "wag_d", "cc")
                    wof(sync, f + "a_done", "g")
                if g_ >= 1:
                    sync.wait_ge(SEM["mm"], marks[f"{f}b_mm_{g_-1}_{ET-1}"]["mm"])
                src = interp[8 * g_:8 * g_ + 8, :, :].rearrange("j p t -> p j t")
                dma(sync, IB.ap().rearrange("p (j t) -> p j t", t=T), src, "ld")
            on_sync(_)
            mark(f"{f}b_ib_{g_}")

            for i in range(ET):
                bi = i % 2

                def _(sync, g_=g_, i=i, bi=bi):
                    if g_ > 0 or i >= 2:
                        pg, pi = (g_, i - 2) if i >= 2 else (g_ - 1, ET - 2 + i)
                        sync.wait_ge(SEM["mm"], marks[f"{f}b_mm_{pg}_{pi}"]["mm"])
                    ap = wd_all_t[8 * g_:8 * g_ + 8, :, D * i:D * i + D].rearrange("jj p f -> p jj f")
                    dma(sync, wdt[bi].ap().rearrange("p (jj f) -> p jj f", f=D), ap, "ld")
                on_sync(_)
                mark(f"{f}b_wd_{g_}_{i}")

                def _(t, g_=g_, i=i, bi=bi):
                    wof(t, f"{f}b_wd_{g_}_{i}", "ld")
                    wof(t, f"{f}b_ib_{g_}", "ld")
                    ps = (psA, psB)[i % 2]
                    if g_ > 0 or i >= 2:
                        pg, pi = (g_, i - 2) if i >= 2 else (g_ - 1, ET - 2 + i)
                        t.wait_ge(SEM["dv"], marks[f"{f}b_add_{pg}_{pi}"]["dv"])
                    last = None
                    for jj in range(8):
                        for c in range(TC):
                            sl = slice(CH * c, CH * c + CH)
                            last = t.matmul(ps[:, sl], wdt[bi][:, D * jj:D * jj + D],
                                            IB[:, T * jj + CH * c:T * jj + CH * c + CH],
                                            start=(jj == 0), stop=(jj == 7))
                    inc(last, "mm")
                on_t(_)
                mark(f"{f}b_mm_{g_}_{i}")

                def _(v, g_=g_, i=i):
                    wof(v, f"{f}b_mm_{g_}_{i}", "mm")
                    if g_ == 0:
                        wof(v, f"{f}b_xr_{i}", "ld")
                    ps = (psA, psB)[i % 2]
                    inc(v.tensor_tensor(out=XBl[:, T * i:T * i + T], in0=XBl[:, T * i:T * i + T],
                                        in1=ps[:, 0:T], op=OP.add), "dv")
                on_v(_)
                mark(f"{f}b_add_{g_}_{i}")
        mark(f + "b_done")
        pbi.close()
        return XBl, pbx

    def p4(XBl, pbx):
        pre = ExitStack()
        sq0 = pre.enter_context(nc.sbuf_tensor("b4sq", [D, T], F32))
        for i in range(ET):
            def _(g, i=i):
                if i == 0:
                    wof(g, "f1b_done", "dv")
                dma(g, xpb_t[i], XBl[:, T * i:T * i + T], "g")
            on_g(_)
            mark(f"b4pk_{i}")
        mark("b4pk_done")
        stream_stats("b4s", None, None, None, sq0, xb=XBl, xbmark=("f1b_done", "dv"))
        ln_rows("b4s", kind="ln")
        pre.close()

        def _(v):
            wof(v, "b4pk_done", "g")
            inc(v.tensor_scalar(out=R4[0:1, 0:4], in0=R4[0:1, 0:4], scalar1=1.0, scalar2=None, op0=OP.mult), "dv")
        on_v(_)
        mark("b4_xbfree")
        pbx.close()

        p4s = ExitStack()
        P4 = p4s.enter_context
        LNX = P4(nc.sbuf_tensor("blnx", [D, ET * T], BF16))
        pp4 = ExitStack()
        sb1 = pp4.enter_context(nc.sbuf_tensor("b4s1", [D, T], F32))
        sb2 = pp4.enter_context(nc.sbuf_tensor("b4s2", [D, T], F32))
        sqq = pp4.enter_context(nc.sbuf_tensor("b4sq2", [D, T], F32))

        def lnx_src(i):
            return xpb_t[i]

        def _(sync):
            wof(sync, "b4_xbfree", "dv")
        on_sync(_)
        qgate()
        emit_lnx("b4l", "b4s_rows", lnx_src, LNX, sb1, sb2, sqq)

        qgate()
        kvs = ExitStack()
        KX = kvs.enter_context
        wkb_b = KX(nc.sbuf_tensor("bwkb", [D, ET * KV * D], BF16))
        kvw = KX(nc.sbuf_tensor("bkvw", [D, T], BF16))

        for wq4 in range(4):
            def _(sync, wq4=wq4):
                if wq4 >= 1:
                    sync.wait_ge(SEM["ac"], marks[f"bkv_wcvt_{wq4-1}"]["ac"])
                dma(sync, sb1.ap().rearrange("p (i f) -> p i f", f=KV * D),
                    wkb[E // 4 * wq4:E // 4 * (wq4 + 1), :].rearrange("(i p) f -> p i f", p=D), "ld")
            on_sync(_)
            mark(f"bkv_wld_{wq4}")

            def _(s, wq4=wq4):
                wof(s, f"bkv_wld_{wq4}", "ld")
                if wq4 == 0:
                    wof(s, "b4l_done", "dv")
                inc(s.copy(out=wkb_b[:, 2048 * wq4:2048 * (wq4 + 1)], in_=sb1[:]), "ac")
            on_s(_)
            mark(f"bkv_wcvt_{wq4}")
        mark("bkv_wcvt")

        for kvh in range(KV):
            def _(t, kvh=kvh):
                wof(t, "bkv_wcvt", "ac")
                if kvh >= 1:
                    t.wait_ge(SEM["ac"], marks[f"bkv_km_{kvh-1}"]["ac"])
                last = None
                for i in range(ET):
                    for c in range(TC):
                        sl = slice(CH * c, CH * c + CH)
                        last = t.matmul(psA[:, sl], wkb_b[:, KV * D * i + D * kvh:KV * D * i + D * kvh + D],
                                        LNX[:, T * i + CH * c:T * i + CH * c + CH],
                                        start=(i == 0), stop=(i == ET - 1))
                inc(last, "mm")
            on_t(_)
            mark(f"bkv_kmm_{kvh}")

            def _(s, kvh=kvh):
                wof(s, f"bkv_kmm_{kvh}", "mm")
                inc(s.activation(out=sqq[:], in_=psA[:, 0:T], func=AF.Square), "ac")
            on_s(_)
            mark(f"bkv_ksq_{kvh}")

            def _(t, kvh=kvh):
                wof(t, f"bkv_ksq_{kvh}", "ac")
                last = None
                for c in range(TC):
                    sl = slice(CH * c, CH * c + CH)
                    last = t.matmul(psB[0:1, sl], onfc_s[:], sqq[:, sl], start=True, stop=True)
                inc(last, "mm")
            on_t(_)
            mark(f"bkv_ksmm_{kvh}")

            def _(s, kvh=kvh):
                wof(s, f"bkv_ksmm_{kvh}", "mm")
                s.activation(out=R1[0:1, 0:T], in_=psB[0:1, 0:T], func=AF.Ln, scale=1.0 / D, bias=kc_s[0:1, 2:3])
                inc(s.activation(out=R1[0:1, 0:T], in_=R1[0:1, 0:T], func=AF.Exp, scale=-0.5), "ac")
            on_s(_)
            mark(f"bkv_krs_{kvh}")

            for c in range(TC):
                def _(t, kvh=kvh, c=c):
                    wof(t, f"bkv_krs_{kvh}", "ac")
                    if c >= 1:
                        t.wait_ge(SEM["dv"], marks[f"bkv_kn_{kvh}_{c-1}"]["dv"])
                    sl = slice(CH * c, CH * c + CH)
                    inc(t.matmul(psB[:, 512:1024], onfr_s[0:1, 0:D], R1[0:1, sl], start=True, stop=True), "mm")
                on_t(_)
                mark(f"bkv_kbc_{kvh}_{c}")

                def _(s, kvh=kvh, c=c):
                    wof(s, f"bkv_kbc_{kvh}_{c}", "mm")
                    inc(s.copy(out=sqq[:, 0:CH], in_=psB[:, 512:1024]), "ac")
                on_s(_)
                mark(f"bkv_kbs_{kvh}_{c}")

                def _(v, kvh=kvh, c=c):
                    wof(v, f"bkv_kbs_{kvh}_{c}", "ac")
                    if c == 0 and kvh >= 1:
                        wof(v, f"bkv_kpk_{kvh-1}", "g")
                    sl = slice(CH * c, CH * c + CH)
                    v.tensor_tensor(out=sqq[:, CH:2 * CH], in0=psA[:, sl], in1=sqq[:, 0:CH], op=OP.mult)
                    inc(v.tensor_scalar(out=kvw[:, sl], in0=sqq[:, CH:2 * CH], scalar1=gk_s[:], scalar2=None,
                                        op0=OP.mult), "dv")
                on_v(_)
                mark(f"bkv_kn_{kvh}_{c}")

            def _(g, kvh=kvh):
                wof(g, f"bkv_kn_{kvh}_{TC-1}", "dv")
                dma(g, kin_t[kvh], kvw[:], "g")
            on_g(_)
            mark(f"bkv_kpk_{kvh}")

            def _(s, kvh=kvh):
                wof(s, f"bkv_kn_{kvh}_{TC-1}", "dv")
                inc(s.activation(out=sqq[:], in_=kvw[:], func=AF.Square), "ac")
            on_s(_)
            mark(f"bkv_kev_{kvh}")

            def _(t, kvh=kvh):
                wof(t, f"bkv_kev_{kvh}", "ac")
                last = None
                for c in range(TC):
                    sl = slice(CH * c, CH * c + CH)
                    last = t.matmul(psB[32:33, sl], onfc_s[:], sqq[:, sl], start=True, stop=True)
                inc(last, "mm")
            on_t(_)
            mark(f"bkv_kn2_{kvh}")

            def _(v, kvh=kvh):
                wof(v, f"bkv_kn2_{kvh}", "mm")
                inc(v.tensor_reduce(out=R3[0:1, 0:1], in_=psB[32:33, 0:T], axis=mybir.AxisListType.X,
                                    op=OP.max), "dv")
            on_v(_)
            mark(f"bkv_kmx_{kvh}")

            def _(s, kvh=kvh):
                wof(s, f"bkv_kmx_{kvh}", "dv")
                s.activation(out=R3[0:1, 1:2], in_=R3[0:1, 0:1], func=AF.Ln, bias=kc_s[0:1, 4:5])
                inc(s.activation(out=km_s[0:1, kvh:kvh + 1], in_=R3[0:1, 1:2], func=AF.Exp, scale=0.5), "ac")
            on_s(_)
            mark(f"bkv_km_{kvh}")
        kvs.close()

        qgate()
        vvs = ExitStack()
        wvb_b = vvs.enter_context(nc.sbuf_tensor("bwvb", [D, ET * KV * D], BF16))
        vst = vvs.enter_context(nc.sbuf_tensor("bvst", [D, T], BF16))

        for wq4 in range(4):
            def _(sync, wq4=wq4):
                if wq4 >= 1:
                    sync.wait_ge(SEM["ac"], marks[f"bvv_wcvt_{wq4-1}"]["ac"])
                dma(sync, sb2.ap().rearrange("p (i f) -> p i f", f=KV * D),
                    wvb[E // 4 * wq4:E // 4 * (wq4 + 1), :].rearrange("(i p) f -> p i f", p=D), "ld")
            on_sync(_)
            mark(f"bvv_wld_{wq4}")

            def _(s, wq4=wq4):
                wof(s, f"bvv_wld_{wq4}", "ld")
                if wq4 == 0:
                    wof(s, f"bkv_km_{KV-1}", "ac")
                inc(s.copy(out=wvb_b[:, 2048 * wq4:2048 * (wq4 + 1)], in_=sb2[:]), "ac")
            on_s(_)
            mark(f"bvv_wcvt_{wq4}")
        mark("bvv_wcvt")

        for mt in range(ET):
            def _(t, mt=mt):
                wof(t, "bvv_wcvt", "ac")
                if mt >= 2:
                    t.wait_ge(SEM["ac"], marks[f"bvv_ev_{mt-2}"]["ac"])
                ps = (psA, psB)[mt % 2]
                last = None
                for i in range(ET):
                    last = t.matmul(ps[:, 0:KV * D], LNX[:, T * i + D * mt:T * i + D * mt + D],
                                    wvb_b[:, KV * D * i:KV * D * i + KV * D],
                                    start=(i == 0), stop=(i == ET - 1))
                inc(last, "mm")
            on_t(_)
            mark(f"bvv_mm_{mt}")

            def _(s, mt=mt):
                wof(s, f"bvv_mm_{mt}", "mm")
                if mt % 4 == 0 and mt >= 4:
                    [s.wait_ge(SEM[_s], marks[f"bvv_pk_{mt//4-1}"][_s]) for _s in POOL["g"]]
                ps = (psA, psB)[mt % 2]
                inc(s.copy(out=vst[:, CH * (mt % 4):CH * (mt % 4) + CH], in_=ps[:, 0:KV * D]), "ac")
            on_s(_)
            mark(f"bvv_ev_{mt}")

            if mt % 4 == 3:
                def _(g, mt=mt):
                    wof(g, f"bvv_ev_{mt}", "ac")
                    dst = vag_in[D * (mt - 3):D * (mt + 1), :].rearrange("(m p) f -> p m f", p=D)
                    dma(g, dst, vst.ap().rearrange("p (m f) -> p m f", f=KV * D), "g")
                on_g(_)
                mark(f"bvv_pk_{mt//4}")
        mark("bkv_done")
        vvs.close()
        pp4.close()

        def _(g):
            wof(g, "bkv_done", "g")
            g.wait_ge(SEM["cc"], C["cc"])
            inc(g.collective_compute("AllGather", OP.bypass,
                                     replica_groups=[[0, 1], [2, 3], [4, 5], [6, 7]],
                                     ins=[kag_in.opt()], outs=[kag_out.opt()]), "cc")
            g.wait_ge(SEM["cc"], C["cc"])
            inc(g.collective_compute("AllGather", OP.bypass,
                                     replica_groups=[[0, 1], [2, 3], [4, 5], [6, 7]],
                                     ins=[vag_in.opt()], outs=[vag_out.opt()]), "cc")
        on_g(_)
        mark("kvag")

        qgate()
        at = ExitStack()
        AT = at.enter_context
        mtri_s = AT(nc.sbuf_tensor("mtri_s", [D, 4 * CH], F32))
        AO = AT(nc.sbuf_tensor("bAO", [D, H * HT], BF16))
        wqs = AT(nc.sbuf_tensor("bwqs", [D, ET * D], F32))
        wqh = [AT(nc.sbuf_tensor(f"bwqh{b}", [D, ET * D], BF16)) for b in range(2)]
        qbb = AT(nc.sbuf_tensor("bqb", [D, HT], BF16))
        klo = AT(nc.sbuf_tensor("bklo", [D, T], BF16))
        kpr = AT(nc.sbuf_tensor("bkpr", [D, T], BF16))
        vlo = AT(nc.sbuf_tensor("bvlo", [D, ET * D], BF16))
        vpr = AT(nc.sbuf_tensor("bvpr", [D, ET * D], BF16))
        clo = AT(nc.sbuf_tensor("bclo", [D, CH], F32))
        cpr = AT(nc.sbuf_tensor("bcpr", [D, CH], F32))
        sub = [AT(nc.sbuf_tensor(f"bsub{b}", [D, CH], F32)) for b in range(4)]
        pbu = [AT(nc.sbuf_tensor(f"bpb{b}", [D, CH], BF16)) for b in range(4)]
        rbb = AT(nc.sbuf_tensor("brb", [D, CH], F32))
        sqh = AT(nc.sbuf_tensor("bsqh", [D, HT], F32))
        yts = [AT(nc.sbuf_tensor(f"byts{b}", [D, HT], F32)) for b in range(2)]

        def _(sync):
            dma(sync, mtri_s[:], mtri, "ld")
        on_sync(_)
        mark("bat_mtld")

        def _(v):
            wof(v, "bat_mtld", "ld")
            inc(v.tensor_scalar(out=R4[0:1, 4:8], in0=R4[0:1, 4:8], scalar1=1.0, scalar2=None, op0=OP.mult), "dv")
        on_v(_)
        mark("bat_mtok")

        for th in range(2):
            hsl = slice(HT * th, HT * th + HT)
            for h in range(H):
                kvh = h // 4
                bh = h % 2
                ht = f"b{th}_{h}"
                if h % 4 == 0:
                    def _(sync, kvh=kvh, th=th, h=h):
                        wof(sync, "kvag", "cc")
                        if th > 0 or h > 0:
                            pth, ph = (th, h - 1) if h > 0 else (0, H - 1)
                            sync.wait_ge(SEM["mm"], marks[f"b{pth}_{ph}_alast"]["mm"])
                        dma(sync, klo[:], kin_t[kvh], "ld")
                        dma(sync, kpr[:], kout_t[0, kvh], "ld")
                        dma(sync, vlo.ap().rearrange("p (m f) -> p m f", f=D),
                            vag_in[:, D * kvh:D * kvh + D].rearrange("(m p) f -> p m f", p=D), "ld")
                        dma(sync, vpr.ap().rearrange("p (m f) -> p m f", f=D),
                            vag_out[0:T, D * kvh:D * kvh + D].rearrange("(m p) f -> p m f", p=D), "ld")
                    on_sync(_)
                    mark(f"b{th}_kvld_{kvh}")

                def _(sync, h=h, ht=ht, th=th):
                    if h >= 1:
                        sync.wait_ge(SEM["ac"], marks[f"b{th}_{h-1}_wcv"]["ac"])
                    elif th > 0:
                        sync.wait_ge(SEM["ac"], marks[f"bop{th-1}_{ET-1}_wcv"]["ac"])
                    dma(sync, wqs.ap().rearrange("p (i f) -> p i f", f=D),
                        wqb[:, D * h:D * h + D].rearrange("(i p) f -> p i f", p=D), "ld")
                on_sync(_)
                mark(f"{ht}_wld")

                def _(s, ht=ht, bh=bh, th=th, h=h):
                    wof(s, f"{ht}_wld", "ld")
                    if h >= 2:
                        s.wait_ge(SEM["mm"], marks[f"b{th}_{h-2}_qmm"]["mm"])
                    elif th > 0:
                        s.wait_ge(SEM["mm"], marks[f"bop{th-1}_{ET-2+h}_mm"]["mm"])
                    inc(s.copy(out=wqh[bh][:], in_=wqs[:]), "ac")
                on_s(_)
                mark(f"{ht}_wcv")

                def _(t, ht=ht, bh=bh, th=th, h=h, hsl=hsl):
                    wof(t, f"{ht}_wcv", "ac")
                    if th == 0 and h == 0:
                        wof(t, "bkv_done", "g")
                    else:
                        pth, ph = (th, h - 1) if h >= 1 else (0, H - 1)
                        t.wait_ge(SEM["dv"], marks[f"b{pth}_{ph}_qn_1"]["dv"])
                    last = None
                    for i in range(ET):
                        for c2 in range(2):
                            qsl = slice(CH * c2, CH * c2 + CH)
                            last = t.matmul(psB[:, qsl], wqh[bh][:, D * i:D * i + D],
                                            LNX[:, T * i + HT * th + CH * c2:T * i + HT * th + CH * c2 + CH],
                                            start=(i == 0), stop=(i == ET - 1))
                    inc(last, "mm")
                on_t(_)
                mark(f"{ht}_qmm")

                def _(s, ht=ht):
                    wof(s, f"{ht}_qmm", "mm")
                    inc(s.activation(out=sqh[:], in_=psB[:, 0:HT], func=AF.Square), "ac")
                on_s(_)
                mark(f"{ht}_qsq")

                def _(t, ht=ht, th=th, h=h):
                    wof(t, f"{ht}_qsq", "ac")
                    if th > 0 or h > 0:
                        pth, ph = (th, h - 1) if h >= 1 else (0, H - 1)
                        t.wait_ge(SEM["dv"], marks[f"b{pth}_{ph}_cr"]["dv"])
                    last = None
                    for c2 in range(2):
                        qsl = slice(CH * c2, CH * c2 + CH)
                        last = t.matmul(psA[0:1, qsl], onfc_s[:], sqh[:, qsl], start=True, stop=True)
                    inc(last, "mm")
                on_t(_)
                mark(f"{ht}_qs2")

                def _(s, ht=ht):
                    wof(s, f"{ht}_qs2", "mm")
                    s.activation(out=R1[0:1, 0:HT], in_=psA[0:1, 0:HT], func=AF.Ln, scale=1.0 / D, bias=kc_s[0:1, 2:3])
                    inc(s.activation(out=R1[0:1, 0:HT], in_=R1[0:1, 0:HT], func=AF.Exp, scale=-0.5), "ac")
                on_s(_)
                mark(f"{ht}_qrs")

                for c2 in range(2):
                    def _(t, ht=ht, c2=c2):
                        wof(t, f"{ht}_qrs", "ac")
                        if c2 >= 1:
                            t.wait_ge(SEM["dv"], marks[f"{ht}_qn_{c2-1}"]["dv"])
                        qsl = slice(CH * c2, CH * c2 + CH)
                        inc(t.matmul(psA[:, 512:1024], onfr_s[0:1, 0:D], R1[0:1, qsl], start=True, stop=True), "mm")
                    on_t(_)
                    mark(f"{ht}_qbc_{c2}")

                    def _(s, ht=ht, c2=c2):
                        wof(s, f"{ht}_qbc_{c2}", "mm")
                        inc(s.copy(out=sqh[:, 0:CH], in_=psA[:, 512:1024]), "ac")
                    on_s(_)
                    mark(f"{ht}_qbs_{c2}")

                    def _(v, ht=ht, c2=c2):
                        wof(v, f"{ht}_qbs_{c2}", "ac")
                        qsl = slice(CH * c2, CH * c2 + CH)
                        v.tensor_tensor(out=sqh[:, CH:2 * CH] if False else sqh[:, CH * c2:CH * c2 + CH],
                                        in0=psB[:, qsl], in1=sqh[:, 0:CH], op=OP.mult) if False else None
                        v.tensor_tensor(out=rbb[:], in0=psB[:, qsl], in1=sqh[:, 0:CH], op=OP.mult)
                        inc(v.tensor_scalar(out=qbb[:, qsl], in0=rbb[:], scalar1=gqsc_s[:], scalar2=None,
                                            op0=OP.mult), "dv")
                    on_v(_)
                    mark(f"{ht}_qn_{c2}")

                def _(s, ht=ht):
                    wof(s, f"{ht}_qn_1", "dv")
                    inc(s.activation(out=sqh[:], in_=qbb[:], func=AF.Square), "ac")
                on_s(_)
                mark(f"{ht}_cq1")

                def _(t, ht=ht):
                    wof(t, f"{ht}_cq1", "ac")
                    last = None
                    for c2 in range(2):
                        qsl = slice(CH * c2, CH * c2 + CH)
                        last = t.matmul(psA[32:33, qsl], onfc_s[:], sqh[:, qsl], start=True, stop=True)
                    inc(last, "mm")
                on_t(_)
                mark(f"{ht}_cq2")

                def _(s, ht=ht):
                    wof(s, f"{ht}_cq2", "mm")
                    s.activation(out=R3[0:1, 0:HT], in_=psA[32:33, 0:HT], func=AF.Ln, bias=kc_s[0:1, 4:5])
                    inc(s.activation(out=R3[0:1, 0:HT], in_=R3[0:1, 0:HT], func=AF.Exp, scale=0.5), "ac")
                on_s(_)
                mark(f"{ht}_cq3")

                def _(v, ht=ht, kvh=kvh):
                    wof(v, f"{ht}_cq3", "ac")
                    v.tensor_scalar(out=R3[0:1, 0:HT], in0=R3[0:1, 0:HT], scalar1=km_s[0:1, kvh:kvh + 1],
                                    scalar2=None, op0=OP.mult)
                    inc(v.tensor_scalar(out=R3[0:1, HT:2 * HT], in0=R3[0:1, 0:HT], scalar1=pm_s[0:1, 0:1],
                                        scalar2=None, op0=OP.add), "dv")
                on_v(_)
                mark(f"{ht}_cr")

                for tcl in range(2):
                    tc = 2 * th + tcl
                    tsl = slice(CH * tcl, CH * tcl + CH)
                    tag = f"{ht}_{tcl}"

                    def _(t, ht=ht, tag=tag, tcl=tcl):
                        if tcl == 0:
                            wof(t, f"{ht}_cr", "dv")
                        else:
                            wof(t, f"{ht}_0_aoev", "dv")
                        t.matmul(psA[:, 1024:1536], onfr_s[0:1, 0:D], R3[0:1, CH * tcl:CH * tcl + CH],
                                 start=True, stop=True)
                        inc(t.matmul(psA[:, 1536:2048], onfr_s[0:1, 0:D], R3[0:1, HT + CH * tcl:HT + CH * tcl + CH],
                                     start=True, stop=True), "mm")
                    on_t(_)
                    mark(f"{tag}_cbc")

                    def _(s, tag=tag):
                        wof(s, f"{tag}_cbc", "mm")
                        s.copy(out=clo[:], in_=psA[:, 1024:1536])
                        inc(s.copy(out=cpr[:], in_=psA[:, 1536:2048]), "ac")
                    on_s(_)
                    mark(f"{tag}_cbs")

                    msteps = [("p", mt) for mt in range(ET)] + [("l", mt) for mt in range(4 * tc + 4)]
                    ns = len(msteps)
                    ADL = 3
                    ADEP = 4
                    mm0, dv0, ac0 = C["mm"], C["dv"], C["ac"]
                    diags = [(src == "l" and mt >= 4 * tc) for src, mt in msteps]
                    cum = []
                    tot = 0
                    for dgf in diags:
                        tot += 2 if dgf else 1
                        cum.append(tot)

                    def posc(si):
                        return mm0 + si + 1 + max(0, si - ADL)

                    def posd(j):
                        return mm0 + min(j + ADL + 1, ns) + j + 1

                    def _(s, ns=ns, mm0=mm0, dv0=dv0, cum=cum, posd=posd):
                        for si in range(ns):
                            pp = si % ADEP
                            s.wait_ge(SEM["dv"], dv0 + cum[si])
                            s.wait_ge(SEM["mm"], posd(si - ADEP) if si >= ADEP else mm0)
                            s.activation(out=pbu[pp][:], in_=sub[pp][:],
                                         func=AF.Exp).then_inc(SEM["ac"], 1)
                    blk.scalar(_)

                    def _(v, msteps=msteps, tag=tag, tc=tc, ns=ns, mm0=mm0, dv0=dv0,
                          ac0=ac0, cum=cum, diags=diags, posc=posc):
                        for si, (src, mt) in enumerate(msteps):
                            pp = si % ADEP
                            if si == 0:
                                wof(v, f"{tag}_cbs", "ac")
                            v.wait_ge(SEM["mm"], posc(si))
                            v.wait_ge(SEM["ac"], ac0 + si - ADEP + 1 if si >= ADEP else ac0)
                            ce = cpr if src == "p" else clo
                            i1 = v.tensor_tensor(out=sub[pp][:], in0=psA[:, CH * pp:CH * pp + CH],
                                                 in1=ce[:], op=OP.subtract)
                            i1.then_inc(SEM["dv"], 1)
                            if diags[si]:
                                r = mt - 4 * tc
                                v.wait_ge(SEM["dv"], dv0 + cum[si] - 1)
                                v.tensor_tensor(out=sub[pp][:], in0=sub[pp][:],
                                                in1=mtri_s[:, CH * r:CH * r + CH],
                                                op=OP.add).then_inc(SEM["dv"], 1)
                    blk.vector(_)

                    def _(t, msteps=msteps, tag=tag, tsl=tsl, ns=ns, dv0=dv0, ac0=ac0,
                          cum=cum):
                        for si in range(ns + ADL):
                            if si < ns:
                                src, mt = msteps[si]
                                pp = si % ADEP
                                if si == 0:
                                    wof(t, f"{tag}_cbs", "ac")
                                t.wait_ge(SEM["dv"], dv0 + cum[si - ADEP] if si >= ADEP else dv0)
                                kbuf = kpr if src == "p" else klo
                                t.matmul(psA[:, CH * pp:CH * pp + CH], kbuf[:, D * mt:D * mt + D],
                                         qbb[:, tsl], start=True, stop=True).then_inc(SEM["mm"], 1)
                            if si >= ADL:
                                j = si - ADL
                                jsrc, jmt = msteps[j]
                                jp = j % ADEP
                                t.wait_ge(SEM["ac"], ac0 + j + 1)
                                vbuf = vpr if jsrc == "p" else vlo
                                t.matmul(psB[0:1, 1024:1536], onbc_s[:], pbu[jp][:],
                                         start=(j == 0), stop=(j == ns - 1))
                                t.matmul(psB[:, 1536:2048], vbuf[:, D * jmt:D * jmt + D], pbu[jp][:],
                                         start=(j == 0), stop=(j == ns - 1)).then_inc(SEM["mm"], 1)
                    on_t(_)
                    C["mm"] += 2 * ns
                    C["dv"] += cum[-1]
                    C["ac"] += ns
                    mark(f"{tag}_da_{ns-1}")

                    def _(v, tag=tag, ns=ns):
                        wof(v, f"{tag}_da_{ns-1}", "mm")
                        v.tensor_scalar(out=R4[0:1, 0:CH], in0=psB[0:1, 1024:1536], scalar1=1.0,
                                        scalar2=None, op0=OP.mult)
                        inc(v.reciprocal(out=R4[0:1, 0:CH], in_=R4[0:1, 0:CH]), "dv")
                    on_v(_)
                    mark(f"{tag}_rd")

                    def _(t, tag=tag):
                        wof(t, f"{tag}_rd", "dv")
                        inc(t.matmul(psA[:, 1024:1536], onfr_s[0:1, 0:D], R4[0:1, 0:CH], start=True, stop=True), "mm")
                    on_t(_)
                    mark(f"{tag}_rbc")

                    def _(s, tag=tag):
                        wof(s, f"{tag}_rbc", "mm")
                        inc(s.copy(out=rbb[:], in_=psA[:, 1024:1536]), "ac")
                    on_s(_)
                    mark(f"{tag}_rbs")

                    def _(v, tag=tag, h=h, tcl=tcl, th=th):
                        wof(v, f"{tag}_rbs", "ac")
                        if th > 0 and h == 0 and tcl == 0:
                            v.wait_ge(SEM["mm"], marks[f"bop{th-1}_{ET-1}_mm"]["mm"])
                        inc(v.tensor_tensor(out=AO[:, HT * h + CH * tcl:HT * h + CH * tcl + CH],
                                            in0=psB[:, 1536:2048], in1=rbb[:], op=OP.mult), "dv")
                    on_v(_)
                    mark(f"{tag}_aoev")
                mark(f"b{th}_{h}_alast")
            mark(f"bat{th}_done")

            for i in range(ET):
                bi = i % 2
                ot = f"bop{th}_{i}"

                def _(sync, i=i, ot=ot, th=th):
                    if i >= 1:
                        sync.wait_ge(SEM["ac"], marks[f"bop{th}_{i-1}_wcv"]["ac"])
                    else:
                        sync.wait_ge(SEM["ac"], marks[f"b{th}_{H-1}_wcv"]["ac"])
                    dma(sync, wqs.ap().rearrange("p (i f) -> p i f", f=D),
                        wob[:, D * i:D * i + D].rearrange("(k p) f -> p k f", p=D), "ld")
                on_sync(_)
                mark(f"{ot}_wld")

                def _(s, ot=ot, bi=bi, th=th, i=i):
                    wof(s, f"{ot}_wld", "ld")
                    if th > 0 or i >= 2:
                        pth, pi = (th, i - 2) if i >= 2 else (0, ET - 2 + i)
                        s.wait_ge(SEM["mm"], marks[f"bop{pth}_{pi}_mm"]["mm"])
                    inc(s.copy(out=wqh[bi][:], in_=wqs[:]), "ac")
                on_s(_)
                mark(f"{ot}_wcv")

                def _(t, ot=ot, bi=bi, th=th, i=i):
                    wof(t, f"{ot}_wcv", "ac")
                    if i == 0:
                        wof(t, f"bat{th}_done", "dv")
                    ps = (psA, psB)[i % 2]
                    if th > 0 or i >= 2:
                        pth, pi = (th, i - 2) if i >= 2 else (0, ET - 2 + i)
                        t.wait_ge(SEM["ac"], marks[f"bop{pth}_{pi}_ev"]["ac"])
                    last = None
                    for k in range(ET):
                        for c2 in range(2):
                            qsl = slice(CH * c2, CH * c2 + CH)
                            last = t.matmul(ps[:, qsl], wqh[bi][:, D * k:D * k + D],
                                            AO[:, HT * k + CH * c2:HT * k + CH * c2 + CH],
                                            start=(k == 0), stop=(k == ET - 1))
                    inc(last, "mm")
                on_t(_)
                mark(f"{ot}_mm")

                def _(s, ot=ot, bi=bi, th=th, i=i):
                    wof(s, f"{ot}_mm", "mm")
                    if th > 0 or i >= 2:
                        pth, pi = (th, i - 2) if i >= 2 else (0, ET - 2 + i)
                        [s.wait_ge(SEM[_s], marks[f"bop{pth}_{pi}_pk"][_s]) for _s in POOL["g"]]
                    ps = (psA, psB)[i % 2]
                    inc(s.copy(out=yts[bi][:], in_=ps[:, 0:HT]), "ac")
                on_s(_)
                mark(f"{ot}_ev")

                def _(g, ot=ot, i=i, th=th, bi=bi):
                    wof(g, f"{ot}_ev", "ac")
                    dma(g, yp_t[i][:, HT * th:HT * th + HT], yts[bi][:], "g")
                on_g(_)
                mark(f"{ot}_pk")
        mark("p4_done")
        at.close()
        p4s.close()

    def p6(xsrc_sbuf=None, dram_src=None):
        if xsrc_sbuf is not None:
            def _(v):
                inc(v.tensor_scalar(out=R4[0:1, 0:4], in0=R4[0:1, 0:4], scalar1=1.0,
                                    scalar2=None, op0=OP.mult), "dv")
            on_v(_)
            mark("p6_flush")

            def _(sync):
                wof(sync, "p6_flush", "dv")
                for i in range(ET):
                    dma(sync, yT_t[i], xsrc_sbuf[:, T * i:T * i + T], "st")
                for _s in POOL["st"]:
                    sync.wait_ge(SEM[_s], C[_s])
            on_sync(_)
        else:
            def _(sync):
                for q in ("ld", "g", "mm", "dv", "ac"):
                    for _s in POOL[q]:
                        sync.wait_ge(SEM[_s], C[_s])
                dma(sync, yT, dram_src, "st")
                for _s in POOL["st"]:
                    sync.wait_ge(SEM[_s], C[_s])
            on_sync(_)

    p0()
    if STAGE >= 2 and STAGE != 15:
        p1()
    p2()
    if STAGE in (0, 15):
        p6(dram_src=xparkA)
    elif STAGE == 1:
        p6(dram_src=xparkA)
    elif STAGE == 2:
        XBl, pbx = ffn(1)
        p6(xsrc_sbuf=XBl)
    else:
        XBl, pbx = ffn(1)
        p4(XBl, pbx)
        if STAGE == 3:
            qgate()
            st3 = ExitStack()
            s1 = st3.enter_context(nc.sbuf_tensor("st3a", [D, T], F32))
            s2 = st3.enter_context(nc.sbuf_tensor("st3b", [D, T], F32))
            s3 = st3.enter_context(nc.sbuf_tensor("st3c", [D, T], F32))
            stream_stats("x3s", lambda i: xpb_t[i], s1, s2, s3,
                         add_fn=lambda i: yp_t[i], park_fn=lambda i: xpa_t[i])
            st3.close()

            def _(sync):
                wof(sync, "x3s_stats", "mm")
                for _s in POOL["g"]:
                    sync.wait_ge(SEM[_s], C[_s])
                dma(sync, yT, xparkA, "st")
                for _s in POOL["st"]:
                    sync.wait_ge(SEM[_s], C[_s])
            on_sync(_)
        else:
            XB2, pbx2 = ffn(2)
            p6(xsrc_sbuf=XB2)

    return nc


_PE = perm256(E)
_PN = perm256(NI)


def _stage_inputs(inputs, core):
    ii = inputs
    p, hf = core // 2, core % 2
    tok = slice(T * hf, T * hf + T)
    f32 = np.float32

    x = np.asarray(ii["L0_attn_a_query_input"], f32)
    d = {}
    d["xT"] = np.ascontiguousarray(x[p, tok, :].T[_PE, :])
    d["winT"] = np.ascontiguousarray(np.asarray(ii["L0_attn_a_W_in"], f32).T[_PE, :])
    d["wqa"] = np.ascontiguousarray(np.asarray(ii["L0_attn_a_W_q"], f32).T)
    d["wka"] = np.ascontiguousarray(np.asarray(ii["L0_attn_a_W_k"], f32).T)
    dperm = np.concatenate([np.arange(0, D, 2), np.arange(1, D, 2)])
    d["wva"] = np.ascontiguousarray(np.asarray(ii["L0_attn_a_W_v"], f32).T[:, dperm])
    d["patT"] = np.ascontiguousarray(
        np.asarray(ii["L0_attn_a_stored_patterns"], f32).transpose(0, 2, 1).reshape(H * D, MA))
    d["nqw_c"] = np.ascontiguousarray(np.asarray(ii["L0_attn_a_norm_query_w"], f32)[_PE].reshape(ET, D).T)
    d["nrw_c"] = np.ascontiguousarray(np.asarray(ii["L0_attn_a_norm_retrieved_w"], f32)[_PE].reshape(ET, D).T)
    d["wqb"] = np.ascontiguousarray(np.asarray(ii["L0_attn_b_Wq"], f32)[_PE, :])
    d["wkb"] = np.ascontiguousarray(np.asarray(ii["L0_attn_b_Wk"], f32)[_PE, :])
    d["wvb"] = np.ascontiguousarray(np.asarray(ii["L0_attn_b_Wv"], f32)[_PE, :])
    d["wob"] = np.ascontiguousarray(np.asarray(ii["L0_attn_b_Wo"], f32)[:, _PE])
    d["gqv"] = np.ascontiguousarray(np.asarray(ii["L0_attn_b_gq"], f32)[:, None])
    d["gkv"] = np.ascontiguousarray(np.asarray(ii["L0_attn_b_gk"], f32)[:, None])

    def codes_u8(a):
        a = np.ascontiguousarray(np.asarray(a, np.int32))
        return a.view(np.uint8).reshape(a.shape[0], a.shape[1], 4)[:, :, 0]

    nsl = slice(T * core, T * core + T)
    gT = codes_u8(ii["L0_ffn_gate_W_packed"]).T[:, _PN]
    uT = codes_u8(ii["L0_ffn_up_W_packed"]).T[:, _PN]
    dT = codes_u8(ii["L0_ffn_down_W_packed"]).T[:, _PE]
    d["gcodes"] = np.ascontiguousarray(gT[:, nsl])
    d["ucodes"] = np.ascontiguousarray(uT[:, nsl])
    d["dcodes"] = np.ascontiguousarray(dT[1024 * core:1024 * core + 1024, :])

    def scl_rep(s, permc, rowsl, colsl):
        sT = np.asarray(s, f32).T[:, permc]
        return np.ascontiguousarray(np.repeat(sT, 16, axis=0)[rowsl, colsl])

    d["gscl"] = scl_rep(ii["L0_ffn_gate_scales"], _PN, slice(None), nsl)
    d["uscl"] = scl_rep(ii["L0_ffn_up_scales"], _PN, slice(None), nsl)
    d["dscl"] = scl_rep(ii["L0_ffn_down_scales"], _PE, slice(1024 * core, 1024 * core + 1024), slice(None))

    kcv = np.zeros((D, 8), f32)
    kcv[:, 0] = -2.0
    kcv[:, 1] = -6.0
    kcv[:, 2] = 1e-6
    kcv[:, 3] = 1e-5
    kcv[:, 4] = 1e-20
    d["kc"] = kcv
    d["beta_in"] = np.asarray(ii["L0_attn_a_beta"], f32).reshape(1, 1)
    d["pmask"] = np.full((1, 1), 1e30 if hf == 0 else 0.0, f32)
    mt = np.zeros((D, 4 * CH), f32)
    mrow = np.arange(D)[:, None]
    scol = np.arange(CH)[None, :]
    for r in range(4):
        mt[:, CH * r:CH * r + CH] = np.where(mrow <= scol - D * r, 0.0, -1e30)
    d["mtri"] = mt
    d["onesf_r"] = np.ones((D, D), f32)
    d["onesf_c"] = np.ones((D, 1), f32)
    d["onesb_c"] = np.ones((D, 1), ml_dtypes.bfloat16)
    return d


def kernel(**inputs):
    nc = build_program()
    in_maps = [_stage_inputs(inputs, c) for c in range(NCORE)]
    res = run_bass_kernel_spmd(nc, in_maps, core_ids=list(range(NCORE)))
    out = np.empty((4, 4096, E), np.float32)
    inv = np.empty(E, np.int64)
    inv[_PE] = np.arange(E)
    for c in range(NCORE):
        shard = np.asarray(res.results[c]["yT"], np.float32)
        p, hf = c // 2, c % 2
        out[p, T * hf:T * hf + T, :] = shard[inv, :].T
    return out


# version 03 (final)
# speedup vs baseline: 1.3755x; 1.0967x over previous
import numpy as np
import ml_dtypes
import concourse.bass as bass
import concourse.mybir as mybir
from concourse.bass_utils import run_bass_kernel_spmd
from contextlib import ExitStack

F32 = mybir.dt.float32
BF16 = mybir.dt.bfloat16
U8 = mybir.dt.uint8
AF = mybir.ActivationFunctionType
OP = mybir.AluOpType

E = 2048
T = 2048
HT = 1024
ET = 16
TC = 4
CH = 512
H = 16
D = 128
MA = 1024
KV = 4
NI = 16384
NT = 128
NCORE = 8
ISD = float(D) ** -0.5
C1P = 2.0 * 0.7978845608028654
C2P = 2.0 * 0.7978845608028654 * 0.044715

STAGE = 4


def perm256(n):
    return np.arange(n).reshape(-1, 128, 2).transpose(0, 2, 1).reshape(-1)


def build_program():
    nc = bass.Bass("TRN2", target_bir_lowering=False)

    def din(name, shape, dt=F32):
        return nc.dram_tensor(name, shape, dt, kind="ExternalInput").ap()

    xT = din("xT", [E, T])
    winT = din("winT", [E, E])
    wqa = din("wqa", [D, D])
    wka = din("wka", [D, D])
    wva = din("wva", [D, D])
    patT = din("patT", [H * D, MA])
    nqw_c = din("nqw_c", [D, ET])
    nrw_c = din("nrw_c", [D, ET])
    wqb = din("wqb", [E, E])
    wkb = din("wkb", [E, KV * D])
    wvb = din("wvb", [E, KV * D])
    wob = din("wob", [E, E])
    gqv = din("gqv", [D, 1])
    gkv = din("gkv", [D, 1])
    gcodes = din("gcodes", [NI // 16, T], U8)
    ucodes = din("ucodes", [NI // 16, T], U8)
    dcodes = din("dcodes", [NI // 16, T], U8)
    gscl = din("gscl", [NI // 16, T])
    uscl = din("uscl", [NI // 16, T])
    dscl = din("dscl", [NI // 16, T])
    kc = din("kc", [D, 8])
    beta_in = din("beta_in", [1, 1])
    pmask = din("pmask", [1, 1])
    mtri = din("mtri", [D, 4 * CH])
    onesf_r = din("onesf_r", [D, D])
    onesf_c = din("onesf_c", [D, 1])
    onesb_c = din("onesb_c", [D, 1], BF16)

    yT = nc.dram_tensor("yT", [E, T], F32, kind="ExternalOutput").ap()

    def dint(name, shape, dt):
        return nc.dram_tensor(name, shape, dt, kind="Internal").ap()

    xparkA = dint("xparkA", [E, T], F32)
    xparkB = dint("xparkB", [E, T], F32)
    ypark = dint("ypark", [E, T], F32)
    interp = dint("interp", [NT, D, T], BF16)
    wg_sh = dint("wg_sh", [16 * ET * D, D], BF16)
    wu_sh = dint("wu_sh", [16 * ET * D, D], BF16)
    wd_sh = dint("wd_sh", [16 * D, T], BF16)
    wg_all = dint("wg_all", [NT * ET * D, D], BF16)
    wu_all = dint("wu_all", [NT * ET * D, D], BF16)
    wd_all = dint("wd_all", [NT * D, T], BF16)
    kag_in = dint("kag_in", [KV * D, T], BF16)
    kag_out = dint("kag_out", [2 * KV * D, T], BF16)
    vag_in = dint("vag_in", [T, KV * D], BF16)
    vag_out = dint("vag_out", [2 * T, KV * D], BF16)

    es = ExitStack()
    SB = es.enter_context

    R1 = SB(nc.sbuf_tensor("R1", [D, T], F32))
    R2 = SB(nc.sbuf_tensor("R2", [D, T], F32))
    R3 = SB(nc.sbuf_tensor("R3", [D, T], F32))
    R4 = SB(nc.sbuf_tensor("R4", [D, T], F32))
    kc_s = SB(nc.sbuf_tensor("kc_s", [D, 8], F32))
    nqc_s = SB(nc.sbuf_tensor("nqc_s", [D, ET], F32))
    nrc_s = SB(nc.sbuf_tensor("nrc_s", [D, ET], F32))
    gq_s = SB(nc.sbuf_tensor("gq_s", [D, 1], F32))
    gqsc_s = SB(nc.sbuf_tensor("gqsc_s", [D, 1], F32))
    gk_s = SB(nc.sbuf_tensor("gk_s", [D, 1], F32))
    beta_s = SB(nc.sbuf_tensor("beta_s", [1, 4], F32))
    pm_s = SB(nc.sbuf_tensor("pm_s", [1, 1], F32))
    km_s = SB(nc.sbuf_tensor("km_s", [1, 8], F32))
    onfr_s = SB(nc.sbuf_tensor("onfr_s", [D, D], F32))
    onfc_s = SB(nc.sbuf_tensor("onfc_s", [D, 1], F32))
    onbc_s = SB(nc.sbuf_tensor("onbc_s", [D, 1], BF16))
    sigb_s = SB(nc.sbuf_tensor("sigb_s", [D, 1], F32))
    wqa_s = SB(nc.sbuf_tensor("wqa_s", [D, D], BF16))
    wka_s = SB(nc.sbuf_tensor("wka_s", [D, D], BF16))
    wva_s = SB(nc.sbuf_tensor("wva_s", [D, D], BF16))

    psA = SB(nc.psum_tensor("psA", [D, T], F32))
    psB = SB(nc.psum_tensor("psB", [D, T], F32))

    POOL = {"ld": [f"ld{i}" for i in range(6)],
            "g": [f"g{i}" for i in range(3)],
            "st": ["st0", "st1"],
            "mm": ["mm"], "dv": ["dv"], "ac": ["ac"], "cc": ["cc"]}
    ALLSEM = [n for v in POOL.values() for n in v]
    SEM = {n: SB(nc.semaphore("sem_" + n)) for n in ALLSEM}
    C = {n: 0 for n in ALLSEM}
    RR = {"ld": 0, "g": 0, "st": 0}

    def dma(qeng, out, in_, q):
        name = POOL[q][RR[q] % len(POOL[q])]
        RR[q] += 1
        qeng.wait_ge(SEM[name], C[name])
        r = qeng.dma_start(out=out, in_=in_)
        C[name] += 16
        r.then_inc(SEM[name], 16)
        return r

    def inc(instr, s, n=1):
        if s in ("dv", "ac"):
            return instr
        assert s in ("mm", "cc"), s
        C[s] += n
        instr.then_inc(SEM[s], n)
        return instr

    marks = {}

    def mark(name):
        marks[name] = dict(C)

    def wof(eng, name, *qs):
        for q in qs:
            for s in POOL[q]:
                eng.wait_ge(SEM[s], marks[name][s])

    def qgate():
        snap = dict(C)

        def _(sync):
            for sname in ("mm", "dv", "ac"):
                sync.wait_ge(SEM[sname], snap[sname])
            for sname in POOL["g"]:
                sync.wait_ge(SEM[sname], snap[sname])
        on_sync(_)

    xT_t = xT.rearrange("(i p) t -> i p t", p=D)
    xpa_t = xparkA.rearrange("(i p) t -> i p t", p=D)
    xpb_t = xparkB.rearrange("(i p) t -> i p t", p=D)
    yp_t = ypark.rearrange("(i p) t -> i p t", p=D)
    yT_t = yT.rearrange("(i p) t -> i p t", p=D)
    wg_sh_t = wg_sh.rearrange("(j i p) f -> j i p f", i=ET, p=D)
    wu_sh_t = wu_sh.rearrange("(j i p) f -> j i p f", i=ET, p=D)
    wd_sh_t = wd_sh.rearrange("(j p) t -> j p t", p=D)
    wg_all_t = wg_all.rearrange("(j i p) f -> j i p f", i=ET, p=D)
    wu_all_t = wu_all.rearrange("(j i p) f -> j i p f", i=ET, p=D)
    wd_all_t = wd_all.rearrange("(j p) t -> j p t", p=D)
    kin_t = kag_in.rearrange("(k p) t -> k p t", p=D)
    kout_t = kag_out.rearrange("(r k p) t -> r k p t", k=KV, p=D)

    blk = es.enter_context(nc.Block())

    class ESync:
        OPS = {"tensor_scalar", "tensor_tensor", "scalar_tensor_tensor", "tensor_copy",
               "reciprocal", "tensor_reduce", "activation", "copy", "sign", "sqrt",
               "square", "mul", "add", "tensor_add", "tensor_mul", "tensor_sub",
               "select", "tensor_tensor_scan"}

        def __init__(self, eng, key):
            self._e = eng
            self._k = key

        def __getattr__(self, n):
            a = getattr(self._e, n)
            if n in ESync.OPS:
                def f(*args, **kw):
                    self._e.wait_ge(SEM[self._k], C[self._k])
                    r = a(*args, **kw)
                    C[self._k] += 1
                    r.then_inc(SEM[self._k], 1)
                    return r
                return f
            return a

    def on_v(f):
        blk.vector(lambda v: f(ESync(v, "dv")))

    def on_s(f):
        blk.scalar(lambda sc: f(ESync(sc, "ac")))

    on_sync, on_g, on_t = blk.sync, blk.gpsimd, blk.tensor

    def p0():
        p0s = ExitStack()
        wsc_s = p0s.enter_context(nc.sbuf_tensor("wsc_s", [D, 3 * D], F32))
        def _(sync):
            for dst, src in ((kc_s, kc), (nqc_s, nqw_c), (nrc_s, nrw_c),
                             (gq_s, gqv), (gk_s, gkv), (pm_s, pmask),
                             (onfr_s, onesf_r), (onfc_s, onesf_c),
                             (onbc_s, onesb_c)):
                dma(sync, dst[:], src, "ld")
            dma(sync, beta_s[0:1, 0:1], beta_in, "ld")
            dma(sync, wsc_s[:, 0:D], wqa, "ld")
            dma(sync, wsc_s[:, D:2 * D], wka, "ld")
            dma(sync, wsc_s[:, 2 * D:3 * D], wva, "ld")
        on_sync(_)
        mark("p0_ld")

        def _(s):
            wof(s, "p0_ld", "ld")
            s.copy(out=wqa_s[:], in_=wsc_s[:, 0:D])
            s.copy(out=wka_s[:], in_=wsc_s[:, D:2 * D])
            s.copy(out=wva_s[:], in_=wsc_s[:, 2 * D:3 * D])
            inc(s.activation(out=beta_s[0:1, 1:2], in_=beta_s[0:1, 0:1], func=AF.Exp, scale=-1.0), "ac")
        on_s(_)
        mark("p0_exp")

        def _(v):
            wof(v, "p0_ld", "ld")
            v.tensor_scalar(out=gqsc_s[:], in0=gq_s[:], scalar1=ISD, scalar2=None, op0=OP.mult)
            wof(v, "p0_exp", "ac")
            v.tensor_scalar(out=beta_s[0:1, 2:3], in0=beta_s[0:1, 1:2], scalar1=1.0, scalar2=None, op0=OP.add)
            inc(v.reciprocal(out=beta_s[0:1, 3:4], in_=beta_s[0:1, 2:3]), "dv")
        on_v(_)
        mark("p0_sigb")

        def _(t):
            wof(t, "p0_sigb", "dv")
            inc(t.matmul(psA[0:D, 0:1], onfr_s[0:1, :], beta_s[0:1, 3:4], start=True, stop=True), "mm")
        on_t(_)
        mark("p0_bcb")

        def _(s):
            wof(s, "p0_bcb", "mm")
            inc(s.copy(out=sigb_s[:], in_=psA[0:D, 0:1]), "ac")
        on_s(_)
        mark("p0_done")
        p0s.close()

    def p1():
        qgate()
        p1s = ExitStack()
        P = p1s.enter_context
        cod_b = [P(nc.sbuf_tensor(f"cod{b}", [D, T], U8)) for b in range(2)]
        scl_b = [P(nc.sbuf_tensor(f"sclb{b}", [D, T], F32)) for b in range(2)]
        dq_t = P(nc.sbuf_tensor("dq_t", [D, T], U8))
        dq_sgn = P(nc.sbuf_tensor("dq_sgn", [D, T], U8))
        dq_sg = P(nc.sbuf_tensor("dq_sg", [D, T], F32))
        dq_r1 = P(nc.sbuf_tensor("dq_r1", [D, T], F32))
        dq_r2 = P(nc.sbuf_tensor("dq_r2", [D, T], F32))
        dq_m = P(nc.sbuf_tensor("dq_m", [D, T], F32))
        dq_w = [P(nc.sbuf_tensor(f"dq_w{b}", [D, T], BF16)) for b in range(2)]

        parts = [("g", gcodes, gscl), ("u", ucodes, uscl), ("d", dcodes, dscl)]
        for ip in range(24):
            part, codes_d, scl_d = parts[ip // 8]
            ci = ip % 8
            b = ip % 2

            def _(sync, ip=ip, b=b, codes_d=codes_d, scl_d=scl_d, ci=ci):
                if ip >= 2:
                    sync.wait_ge(SEM["dv"], marks[f"decw_{ip-2}_1"]["dv"])
                dma(sync, cod_b[b][:], codes_d[128 * ci:128 * ci + 128, :], "ld")
                dma(sync, scl_b[b][:], scl_d[128 * ci:128 * ci + 128, :], "ld")
            on_sync(_)
            mark(f"decld_{ip}")

            for half in range(2):
                def _(v, ip=ip, b=b, half=half):
                    if half == 0:
                        wof(v, f"decld_{ip}", "ld")
                        if ip >= 1:
                            v.wait_ge(SEM["ac"], marks[f"decact_{ip-1}_1"]["ac"])
                    else:
                        v.wait_ge(SEM["ac"], marks[f"decact_{ip}_0"]["ac"])
                    if half == 0:
                        v.tensor_scalar(out=dq_t[:], in0=cod_b[b][:], scalar1=7, scalar2=None, op0=OP.bitwise_and)
                        v.tensor_scalar(out=dq_sgn[:], in0=cod_b[b][:], scalar1=8, scalar2=None, op0=OP.bitwise_and)
                    else:
                        v.tensor_scalar(out=dq_t[:], in0=cod_b[b][:], scalar1=4, scalar2=7,
                                        op0=OP.logical_shift_right, op1=OP.bitwise_and)
                        v.tensor_scalar(out=dq_sgn[:], in0=cod_b[b][:], scalar1=4, scalar2=8,
                                        op0=OP.logical_shift_right, op1=OP.bitwise_and)
                    inc(v.tensor_scalar(out=dq_sg[:], in0=dq_sgn[:], scalar1=-0.25, scalar2=1.0,
                                        op0=OP.mult, op1=OP.add), "dv")
                on_v(_)
                mark(f"dect_{ip}_{half}")

                def _(s, ip=ip, half=half):
                    wof(s, f"dect_{ip}_{half}", "dv")
                    s.activation(out=dq_r1[:], in_=dq_t[:], func=AF.Relu, scale=0.5, bias=kc_s[:, 0:1])
                    inc(s.activation(out=dq_r2[:], in_=dq_t[:], func=AF.Relu, bias=kc_s[:, 1:2]), "ac")
                on_s(_)
                mark(f"decact_{ip}_{half}")

                def _(v, ip=ip, b=b, half=half):
                    wof(v, f"decact_{ip}_{half}", "ac")
                    v.scalar_tensor_tensor(out=dq_m[:], in0=dq_t[:], scalar=0.5, in1=dq_r1[:],
                                           op0=OP.mult, op1=OP.add)
                    v.tensor_tensor(out=dq_m[:], in0=dq_m[:], in1=dq_r2[:], op=OP.add)
                    v.tensor_tensor(out=dq_m[:], in0=dq_m[:], in1=dq_sg[:], op=OP.mult)
                    if ip >= 1:
                        [v.wait_ge(SEM[_s], marks[f"decst_{ip-1}"][_s]) for _s in POOL["g"]]
                    inc(v.tensor_tensor(out=dq_w[half][:], in0=dq_m[:], in1=scl_b[b][:], op=OP.mult), "dv")
                on_v(_)
                mark(f"decw_{ip}_{half}")

            def _(g, ip=ip, part=part, ci=ci):
                wof(g, f"decw_{ip}_1", "dv")
                if part in ("g", "u"):
                    dst = wg_sh_t if part == "g" else wu_sh_t
                    for half in range(2):
                        d_ap = dst[:, 2 * ci + half, :, :].rearrange("j p f -> p j f")
                        s_ap = dq_w[half].ap().rearrange("p (j f) -> p j f", f=D)
                        dma(g, d_ap, s_ap, "g")
                else:
                    for half in range(2):
                        dma(g, wd_sh_t[2 * ci + half], dq_w[half][:], "g")
            on_g(_)
            mark(f"decst_{ip}")
        mark("dec_all")

        def _(g):
            wof(g, "dec_all", "g")
            g.wait_ge(SEM["cc"], C["cc"])
            inc(g.collective_compute("AllGather", OP.bypass, replica_groups=[list(range(NCORE))],
                                     ins=[wg_sh.opt()], outs=[wg_all.opt()]), "cc")
            mark("wag_g")
            g.wait_ge(SEM["cc"], C["cc"])
            inc(g.collective_compute("AllGather", OP.bypass, replica_groups=[list(range(NCORE))],
                                     ins=[wu_sh.opt()], outs=[wu_all.opt()]), "cc")
            mark("wag_u")
            g.wait_ge(SEM["cc"], C["cc"])
            inc(g.collective_compute("AllGather", OP.bypass, replica_groups=[list(range(NCORE))],
                                     ins=[wd_sh.opt()], outs=[wd_all.opt()]), "cc")
            mark("wag_d")
        on_g(_)
        p1s.close()

    def stream_stats(name, src_fn, sb1, sb2, sq, add_fn=None, park_fn=None, xb=None,
                     xbmark=None):
        for i in range(ET):
            buf = (sb1, sb2)[i % 2] if src_fn else None
            src = src_fn(i) if src_fn else None
            dep = None
            if src is not None:
                def _(sync, i=i, buf=buf, src=src):
                    w = 1 if add_fn is not None else 2
                    if i >= w:
                        sync.wait_ge(SEM["mm"], marks[f"{name}_mm_{i-w}"]["mm"])
                        if park_fn is not None:
                            [sync.wait_ge(SEM[_s], marks[f"{name}_pk_{i-w}"][_s]) for _s in POOL["g"]]
                    dma(sync, buf[:], src, "ld")
                    if add_fn is not None:
                        dma(sync, sq[:], add_fn(i), "ld")
                on_sync(_)
                mark(f"{name}_ld_{i}")
                dep = (f"{name}_ld_{i}", "ld")

                if add_fn is not None:
                    def _(v, i=i, buf=buf):
                        wof(v, f"{name}_ld_{i}", "ld")
                        if i >= 1:
                            v.wait_ge(SEM["ac"], marks[f"{name}_sq_{i-1}"]["ac"])
                        inc(v.tensor_tensor(out=buf[:], in0=buf[:], in1=sq[:], op=OP.add), "dv")
                    on_v(_)
                    mark(f"{name}_add_{i}")
                    dep = (f"{name}_add_{i}", "dv")

                if park_fn is not None:
                    def _(g, i=i, buf=buf, dep=dep):
                        wof(g, dep[0], dep[1])
                        dma(g, park_fn(i), buf[:], "g")
                    on_g(_)
                    mark(f"{name}_pk_{i}")

            def _(s, i=i, buf=buf, src=src, dep=dep):
                if dep is not None:
                    wof(s, dep[0], dep[1])
                elif i == 0 and xbmark is not None:
                    wof(s, xbmark[0], xbmark[1])
                if i >= 1:
                    s.wait_ge(SEM["mm"], marks[f"{name}_mm_{i-1}"]["mm"])
                a = buf[:] if src is not None else xb[:, T * i:T * i + T]
                inc(s.activation(out=sq[:], in_=a, func=AF.Square), "ac")
            on_s(_)
            mark(f"{name}_sq_{i}")

            def _(t, i=i, buf=buf, src=src, dep=dep):
                if dep is not None:
                    wof(t, dep[0], dep[1])
                wof(t, f"{name}_sq_{i}", "ac")
                last = None
                for c in range(TC):
                    sl = slice(CH * c, CH * c + CH)
                    a = buf[:, sl] if src is not None else xb[:, T * i + CH * c:T * i + CH * c + CH]
                    last = t.matmul(psA[0:1, sl], onfc_s[:], a, start=(i == 0), stop=(i == ET - 1))
                for c in range(TC):
                    sl = slice(CH * c, CH * c + CH)
                    last = t.matmul(psB[0:1, sl], onfc_s[:], sq[:, sl], start=(i == 0), stop=(i == ET - 1))
                inc(last, "mm")
            on_t(_)
            mark(f"{name}_mm_{i}")
        mark(f"{name}_stats")

    def ln_rows(name, kind="ln"):
        def _(v):
            wof(v, f"{name}_stats", "mm")
            v.tensor_scalar(out=R2[0:1, 0:T], in0=psA[0:1, 0:T], scalar1=1.0 / E, scalar2=None, op0=OP.mult)
            v.tensor_scalar(out=R1[0:1, 0:T], in0=psB[0:1, 0:T], scalar1=1.0 / E, scalar2=None, op0=OP.mult)
            v.tensor_tensor(out=R3[0:1, 0:T], in0=R2[0:1, 0:T], in1=R2[0:1, 0:T], op=OP.mult)
            inc(v.tensor_tensor(out=R3[0:1, 0:T], in0=R1[0:1, 0:T], in1=R3[0:1, 0:T], op=OP.subtract), "dv")
        on_v(_)
        mark(f"{name}_var")

        def _(sc):
            wof(sc, f"{name}_var", "dv")
            sc.activation(out=R1[0:1, 0:T], in_=R3[0:1, 0:T], func=AF.Ln, bias=kc_s[0:1, 3:4])
            inc(sc.activation(out=R1[0:1, 0:T], in_=R1[0:1, 0:T], func=AF.Exp, scale=-0.5), "ac")
        on_s(_)
        mark(f"{name}_rstd")

        if kind == "lnq":
            def _(v):
                wof(v, f"{name}_rstd", "ac")
                v.tensor_scalar(out=R4[0:1, 0:T], in0=R3[0:1, 0:T], scalar1=1e-5, scalar2=None, op0=OP.add)
                v.reciprocal(out=R4[0:1, 0:T], in_=R4[0:1, 0:T])
                inc(v.tensor_tensor(out=R4[0:1, 0:T], in0=R3[0:1, 0:T], in1=R4[0:1, 0:T], op=OP.mult), "dv")
            on_v(_)
            mark(f"{name}_msql")

            def _(sc):
                wof(sc, f"{name}_msql", "dv")
                sc.activation(out=R4[0:1, 0:T], in_=R4[0:1, 0:T], func=AF.Ln, bias=kc_s[0:1, 2:3])
                inc(sc.activation(out=R4[0:1, 0:T], in_=R4[0:1, 0:T], func=AF.Exp, scale=-0.5), "ac")
            on_s(_)
            mark(f"{name}_rsq")

            def _(v):
                wof(v, f"{name}_rsq", "ac")
                v.tensor_tensor(out=R3[0:1, 0:T], in0=R1[0:1, 0:T], in1=R4[0:1, 0:T], op=OP.mult)
                inc(v.tensor_tensor(out=R4[0:1, 0:T], in0=R2[0:1, 0:T], in1=R3[0:1, 0:T], op=OP.mult), "dv")
            on_v(_)
        else:
            def _(v):
                wof(v, f"{name}_rstd", "ac")
                inc(v.tensor_tensor(out=R2[0:1, 0:T], in0=R2[0:1, 0:T], in1=R1[0:1, 0:T], op=OP.mult), "dv")
            on_v(_)
        mark(f"{name}_rows")

    def emit_lnx(name, rowsmark, src_fn, dst, sb1, sb2, sq, wcol=None, arow=None, brow=None):
        arow = arow if arow is not None else R1
        brow = brow if brow is not None else R2
        for i in range(ET):
            buf = (sb1, sb2)[i % 2]

            def _(sync, i=i, buf=buf):
                if i >= 2:
                    sync.wait_ge(SEM["dv"], marks[f"{name}_o_{i-2}"]["dv"])
                    sync.wait_ge(SEM["ac"], marks[f"{name}_o_{i-2}"]["ac"])
                dma(sync, buf[:], src_fn(i), "ld")
            on_sync(_)
            mark(f"{name}_ld_{i}")

            def _(t, i=i):
                wof(t, rowsmark, "dv")
                wof(t, f"{name}_ld_{i}", "ld")
                if i >= 1:
                    t.wait_ge(SEM["dv"], marks[f"{name}_o_{i-1}"]["dv"])
                last = None
                for c in range(TC):
                    sl = slice(CH * c, CH * c + CH)
                    t.matmul(psA[:, sl], onfr_s[0:1, 0:D], arow[0:1, sl], start=True, stop=True)
                    last = t.matmul(psB[:, sl], onfr_s[0:1, 0:D], brow[0:1, sl], start=True, stop=True)
                inc(last, "mm")
            on_t(_)
            mark(f"{name}_bc_{i}")

            def _(v, i=i, buf=buf):
                wof(v, f"{name}_bc_{i}", "mm")
                for c in range(TC):
                    sl = slice(CH * c, CH * c + CH)
                    v.tensor_tensor(out=sq[:, sl], in0=buf[:, sl], in1=psA[:, sl], op=OP.mult)
                last = None
                for c in range(TC):
                    sl = slice(CH * c, CH * c + CH)
                    if wcol is None:
                        last = v.tensor_tensor(out=dst[:, T * i + CH * c:T * i + CH * c + CH],
                                               in0=sq[:, sl], in1=psB[:, sl], op=OP.subtract)
                    else:
                        last = v.tensor_tensor(out=buf[:, sl], in0=sq[:, sl], in1=psB[:, sl], op=OP.subtract)
                inc(last, "dv")
            on_v(_)
            mark(f"{name}_vo_{i}")

            if wcol is not None:
                def _(sc, i=i, buf=buf):
                    wof(sc, f"{name}_vo_{i}", "dv")
                    inc(sc.activation(out=dst[:, T * i:T * i + T], in_=buf[:], func=AF.Copy,
                                      scale=wcol[:, i:i + 1]), "ac")
                on_s(_)
                mark(f"{name}_so_{i}")

            def noop():
                pass
            marks[f"{name}_o_{i}"] = dict(marks[f"{name}_so_{i}"] if wcol is not None else marks[f"{name}_vo_{i}"])
        mark(f"{name}_done")

    def p2():
        qgate()
        p2s = ExitStack()
        P = p2s.enter_context
        NQ = P(nc.sbuf_tensor("NQ", [D, ET * T], BF16))
        AFl = P(nc.sbuf_tensor("AFl", [D, ET * T], BF16))

        pre = ExitStack()
        sb1 = pre.enter_context(nc.sbuf_tensor("p2s1", [D, T], F32))
        sb2 = pre.enter_context(nc.sbuf_tensor("p2s2", [D, T], F32))
        sq = pre.enter_context(nc.sbuf_tensor("p2sq", [D, T], F32))
        stream_stats("axs", lambda i: xT_t[i], sb1, sb2, sq)
        ln_rows("axs", kind="lnq")
        qgate()
        emit_lnx("anq", "axs_rows", lambda i: xT_t[i], NQ, sb1, sb2, sq, wcol=nqc_s, arow=R3, brow=R4)
        if STAGE == 0:
            for i in range(ET):
                def _(sc, i=i):
                    wof(sc, "anq_done", "dv" if False else "ac")
                    if i >= 1:
                        [sc.wait_ge(SEM[_s], marks[f"st0_{i-1}"][_s]) for _s in POOL["g"]]
                    inc(sc.copy(out=sb1[:], in_=NQ[:, T * i:T * i + T]), "ac")
                on_s(_)
                mark(f"st0c_{i}")

                def _(g, i=i):
                    wof(g, f"st0c_{i}", "ac")
                    dma(g, xpa_t[i], sb1[:], "g")
                on_g(_)
                mark(f"st0_{i}")
            pre.close()
            p2s.close()
            return
        pre.close()

        qgate()
        hp_s = ExitStack()
        P2 = hp_s.enter_context
        wst = P2(nc.sbuf_tensor("awst", [D, ET * D // 2], F32))
        wih = [P2(nc.sbuf_tensor(f"awih{b}", [D, ET * D], BF16)) for b in range(2)]
        pats = P2(nc.sbuf_tensor("apats", [D, MA // 2], F32))
        patb = [P2(nc.sbuf_tensor(f"apatb{b}", [D, MA], BF16)) for b in range(2)]
        projb = P2(nc.sbuf_tensor("aprojb", [D, T], BF16))
        qb = [P2(nc.sbuf_tensor(f"aqb{w}", [D, T], BF16)) for w in range(2)]
        kb = [P2(nc.sbuf_tensor(f"akb{w}", [D, MA], BF16)) for w in range(2)]
        vb = [P2(nc.sbuf_tensor(f"avb{w}", [D, MA], BF16)) for w in range(2)]
        pbuf = [P2(nc.sbuf_tensor(f"apb{w}", [D, CH], BF16)) for w in range(4)]
        rb = P2(nc.sbuf_tensor("arb", [D, CH], F32))
        svc = P2(nc.sbuf_tensor("asvc", [D, 2], F32))
        af1 = P2(nc.sbuf_tensor("aaf1", [D, CH], F32))

        for hp in range(H // 2):
            for hw in range(2):
                h = 2 * hp + hw

                for wh in range(2):
                    def _(sync, h=h, wh=wh):
                        if h + wh >= 1:
                            ph, pw = (h, 0) if wh == 1 else (h - 1, 1)
                            sync.wait_ge(SEM["dv"], marks[f"a_wcvt_{ph}_{pw}"]["dv"])
                        dma(sync, wst.ap().rearrange("p (i f) -> p i f", f=D),
                            winT[E // 2 * wh:E // 2 * (wh + 1), D * h:D * h + D].rearrange("(i p) f -> p i f", p=D), "ld")
                        dma(sync, pats[:], patT[D * h:D * h + D, MA // 2 * wh:MA // 2 * (wh + 1)], "ld")
                    on_sync(_)
                    mark(f"a_wldh_{h}_{wh}")

                    def _(v, h=h, hw=hw, wh=wh):
                        wof(v, f"a_wldh_{h}_{wh}", "ld")
                        if h >= 2 and wh == 0:
                            v.wait_ge(SEM["mm"], marks[f"a_kv_{h-2}"]["mm"])
                        v.tensor_copy(out=wih[hw][:, ET * D // 2 * wh:ET * D // 2 * (wh + 1)], in_=wst[:])
                        v.tensor_copy(out=patb[hw][:, MA // 2 * wh:MA // 2 * (wh + 1)], in_=pats[:])
                    on_v(_)
                    mark(f"a_wcvt_{h}_{wh}")
                mark(f"a_wcvt_{h}")

                def _(t, h=h, hw=hw, hp=hp):
                    wof(t, f"a_wcvt_{h}_1", "dv")
                    wof(t, "anq_done", "dv")
                    if h >= 1:
                        t.wait_ge(SEM["dv"], marks[f"a_kv_ev_{h-1}"]["dv"])
                    if hp >= 1 and hw == 0:
                        t.wait_ge(SEM["dv"], marks[f"aat_{hp-1}_{TC-1}_fb"]["dv"])
                    last = None
                    for i in range(ET):
                        for c in range(TC):
                            sl = slice(CH * c, CH * c + CH)
                            last = t.matmul(psA[:, sl], wih[hw][:, D * i:D * i + D],
                                            NQ[:, T * i + CH * c:T * i + CH * c + CH],
                                            start=(i == 0), stop=(i == ET - 1))
                    inc(last, "mm")
                on_t(_)
                mark(f"a_proj_{h}")

                def _(s, h=h):
                    wof(s, f"a_proj_{h}", "mm")
                    if h >= 1:
                        s.wait_ge(SEM["mm"], marks[f"a_q_{h-1}"]["mm"])
                    inc(s.copy(out=projb[:], in_=psA[:, 0:T]), "ac")
                on_s(_)
                mark(f"a_projev_{h}")

                def _(t, h=h):
                    wof(t, f"a_projev_{h}", "ac")
                    if h >= 1:
                        t.wait_ge(SEM["ac"], marks[f"a_svc_{h-1}"]["ac"])
                    last = None
                    for c in range(TC):
                        sl = slice(CH * c, CH * c + CH)
                        last = t.matmul(psB[:, sl], wqa_s[:], projb[:, sl], start=True, stop=True)
                    inc(last, "mm")
                on_t(_)
                mark(f"a_q_{h}")

                def _(s, h=h, hw=hw):
                    wof(s, f"a_q_{h}", "mm")
                    inc(s.activation(out=qb[hw][:], in_=psB[:, 0:T], func=AF.Copy, scale=sigb_s[:]), "ac")
                on_s(_)
                mark(f"a_qev_{h}")

                def _(t, h=h, hw=hw):
                    wof(t, f"a_qev_{h}", "ac")
                    t.matmul(psA[:, 0:CH], wka_s[:], patb[hw][:, 0:CH], start=True, stop=True)
                    t.matmul(psA[:, CH:MA], wka_s[:], patb[hw][:, CH:MA], start=True, stop=True)
                    last = None
                    for mt in range(8):
                        last = t.matmul(psA[:, MA + D * mt:MA + D * mt + D],
                                        patb[hw][:, D * mt:D * mt + D], wva_s[:], start=True, stop=True)
                    inc(last, "mm")
                on_t(_)
                mark(f"a_kv_{h}")

                def _(v, h=h, hw=hw):
                    wof(v, f"a_kv_{h}", "mm")
                    v.tensor_copy(out=kb[hw][:], in_=psA[:, 0:MA])
                    v.tensor_copy(out=vb[hw][:], in_=psA[:, MA:2 * MA])
                on_v(_)
                mark(f"a_kv_ev_{h}")

                def _(t, h=h, hw=hw):
                    wof(t, f"a_kv_ev_{h}", "dv")
                    last = None
                    for par in range(2):
                        for mt in range(8):
                            last = t.matmul(psB[64 * hw:64 * hw + 64, 1024 + par:1024 + par + 1],
                                            vb[hw][:, D * mt + 64 * par:D * mt + 64 * par + 64],
                                            onbc_s[:], start=(mt == 0), stop=(mt == 7),
                                            tile_position=(0, 64 * hw))
                    inc(last, "mm")
                on_t(_)
                mark(f"a_sv_{h}")

                def _(v, h=h, hw=hw):
                    wof(v, f"a_sv_{h}", "mm")
                    v.tensor_copy(out=svc[64 * hw:64 * hw + 64, 0:2], in_=psB[64 * hw:64 * hw + 64, 1024:1026])
                on_v(_)
                mark(f"a_svc_{h}")

            for tc in range(TC):
                tsl = slice(CH * tc, CH * tc + CH)
                tag = f"aat_{hp}_{tc}"
                steps = [(hw, mt) for hw in range(2) for mt in range(8)]
                ns = len(steps)
                ADL = 3
                SDEP = 3
                EDEP = 2
                PDEP = 4
                mm0, dv0, ac0 = C["mm"], C["dv"], C["ac"]
                SSL = [(psA, 0), (psA, CH), (psB, CH)]

                def posc(si):
                    return mm0 + si + 1 + max(0, si - ADL)

                def posd(j):
                    return mm0 + min(j + ADL + 1, ns) + j + 1

                def _(s, tag=tag, ns=ns, mm0=mm0, dv0=dv0, ac0=ac0, posc=posc):
                    for si in range(ns):
                        pst, off = SSL[si % SDEP]
                        ep = si % EDEP
                        s.wait_ge(SEM["mm"], posc(si))
                        s.wait_ge(SEM["dv"], dv0 + si - EDEP + 1 if si >= EDEP else dv0)
                        s.activation(out=psB[:, 1024 + CH * ep:1024 + CH * ep + CH],
                                     in_=pst[:, off:off + CH],
                                     func=AF.Exp).then_inc(SEM["ac"], 1)
                blk.scalar(_)

                def _(v, tag=tag, ns=ns, mm0=mm0, ac0=ac0, posd=posd):
                    for si in range(ns):
                        ep = si % EDEP
                        v.wait_ge(SEM["ac"], ac0 + si + 1)
                        v.wait_ge(SEM["mm"], posd(si - PDEP) if si >= PDEP else mm0)
                        v.tensor_scalar(out=pbuf[si % PDEP][:],
                                        in0=psB[:, 1024 + CH * ep:1024 + CH * ep + CH],
                                        scalar1=-1.0, scalar2=None,
                                        op0=OP.add).then_inc(SEM["dv"], 1)
                blk.vector(_)

                def _(t, steps=steps, tag=tag, tsl=tsl, hp=hp, tc=tc, ns=ns,
                      dv0=dv0, ac0=ac0):
                    for si in range(ns + ADL):
                        if si < ns:
                            hw, mt = steps[si]
                            pst, off = SSL[si % SDEP]
                            if si == 0:
                                wof(t, f"a_kv_ev_{2*hp+1}", "dv")
                                wof(t, f"a_svc_{2*hp+1}", "dv")
                                if tc > 0:
                                    wof(t, f"aat_{hp}_{tc-1}_fb", "dv")
                            t.wait_ge(SEM["ac"], ac0 + si - SDEP + 1 if si >= SDEP else ac0)
                            t.matmul(pst[:, off:off + CH], kb[hw][:, D * mt:D * mt + D],
                                     qb[hw][:, tsl], start=True,
                                     stop=True).then_inc(SEM["mm"], 1)
                        if si >= ADL:
                            j = si - ADL
                            jhw, jmt = steps[j]
                            jp = j % PDEP
                            t.wait_ge(SEM["dv"], dv0 + j + 1)
                            r = 32 * jhw
                            t.matmul(psA[r:r + 1, MA:MA + CH], onbc_s[:], pbuf[jp][:],
                                     start=(jmt == 0), stop=(jmt == 7))
                            t.matmul(psA[64 * jhw:64 * jhw + 64, 1536:2048],
                                     vb[jhw][:, D * jmt:D * jmt + 64], pbuf[jp][:],
                                     start=(jmt == 0), stop=(jmt == 7), tile_position=(0, 64 * jhw))
                            t.matmul(psB[64 * jhw:64 * jhw + 64, 0:CH],
                                     vb[jhw][:, D * jmt + 64:D * jmt + 128], pbuf[jp][:],
                                     start=(jmt == 0), stop=(jmt == 7),
                                     tile_position=(0, 64 * jhw)).then_inc(SEM["mm"], 1)
                on_t(_)
                C["mm"] += 2 * ns
                C["dv"] += ns
                C["ac"] += ns
                mark(f"{tag}_da_{ns-1}")

                def _(v, tag=tag):
                    wof(v, f"{tag}_da_15", "mm")
                    v.tensor_scalar(out=R4[0:1, 0:CH], in0=psA[0:1, MA:MA + CH], scalar1=float(MA), scalar2=None, op0=OP.add)
                    v.tensor_scalar(out=R4[0:1, CH:2 * CH], in0=psA[32:33, MA:MA + CH], scalar1=float(MA), scalar2=None, op0=OP.add)
                    v.reciprocal(out=R4[0:1, 0:CH], in_=R4[0:1, 0:CH])
                    inc(v.reciprocal(out=R4[0:1, CH:2 * CH], in_=R4[0:1, CH:2 * CH]), "dv")
                on_v(_)
                mark(f"{tag}_rd")

                def _(t, tag=tag):
                    wof(t, f"{tag}_rd", "dv")
                    t.matmul(psB[0:64, CH:2 * CH], onfr_s[0:1, 0:64], R4[0:1, 0:CH],
                             start=True, stop=True)
                    inc(t.matmul(psB[64:128, CH:2 * CH], onfr_s[0:1, 0:64], R4[0:1, CH:2 * CH],
                                 start=True, stop=True, tile_position=(0, 64)), "mm")
                on_t(_)
                mark(f"{tag}_rbc")

                def _(v, tag=tag):
                    wof(v, f"{tag}_rbc", "mm")
                    v.tensor_copy(out=rb[:], in_=psB[:, CH:2 * CH])
                on_v(_)
                mark(f"{tag}_rbs")

                def _(v, tag=tag, hp=hp, tc=tc):
                    v.tensor_scalar(out=af1[:], in0=psA[:, 1536:2048], scalar1=svc[:, 0:1], scalar2=None, op0=OP.add)
                    v.tensor_tensor(out=AFl[:, T * (2 * hp) + CH * tc:T * (2 * hp) + CH * tc + CH],
                                    in0=af1[:], in1=rb[:], op=OP.mult)
                    v.tensor_scalar(out=af1[:], in0=psB[:, 0:CH], scalar1=svc[:, 1:2], scalar2=None, op0=OP.add)
                    inc(v.tensor_tensor(out=AFl[:, T * (2 * hp + 1) + CH * tc:T * (2 * hp + 1) + CH * tc + CH],
                                        in0=af1[:], in1=rb[:], op=OP.mult), "dv")
                on_v(_)
                mark(f"{tag}_fb")
        mark("a_attn_done")
        hp_s.close()

        if STAGE == 15:
            qgate()
            co15 = ExitStack()
            c15 = co15.enter_context(nc.sbuf_tensor("c15", [D, T], F32))
            for i in range(ET):
                def _(sc, i=i):
                    if i == 0:
                        wof(sc, "a_attn_done", "dv")
                    if i >= 1:
                        [sc.wait_ge(SEM[_s], marks[f"st15_{i-1}"][_s]) for _s in POOL["g"]]
                    inc(sc.copy(out=c15[:], in_=AFl[:, T * i:T * i + T]), "ac")
                on_s(_)
                mark(f"st15c_{i}")

                def _(g, i=i):
                    wof(g, f"st15c_{i}", "ac")
                    dma(g, xpa_t[i], c15[:], "g")
                on_g(_)
                mark(f"st15_{i}")
            co15.close()
            p2s.close()
            return

        qgate()
        co = ExitStack()
        cb1 = co.enter_context(nc.sbuf_tensor("acb1", [D, T], F32))
        cb2 = co.enter_context(nc.sbuf_tensor("acb2", [D, T], F32))
        cw1 = co.enter_context(nc.sbuf_tensor("acw1", [D, T], F32))
        csq = co.enter_context(nc.sbuf_tensor("acsq", [D, T], F32))

        for i in range(ET):
            def _(s, i=i):
                if i == 0:
                    wof(s, "a_attn_done", "dv")
                if i >= 1:
                    s.wait_ge(SEM["mm"], marks[f"afs_mm_{i-1}"]["mm"])
                inc(s.activation(out=csq[:], in_=AFl[:, T * i:T * i + T], func=AF.Square), "ac")
            on_s(_)
            mark(f"afs_sq_{i}")

            def _(t, i=i):
                wof(t, f"afs_sq_{i}", "ac")
                if i == 0:
                    wof(t, "a_attn_done", "dv")
                last = None
                for c in range(TC):
                    sl = slice(CH * c, CH * c + CH)
                    last = t.matmul(psA[0:1, sl], onfc_s[:], csq[:, sl], start=(i == 0), stop=(i == ET - 1))
                inc(last, "mm")
            on_t(_)
            mark(f"afs_mm_{i}")
        mark("afs_stats")

        def _(s):
            wof(s, "afs_stats", "mm")
            s.activation(out=R3[0:1, 0:T], in_=psA[0:1, 0:T], func=AF.Ln, scale=1.0 / E, bias=kc_s[0:1, 2:3])
            inc(s.activation(out=R3[0:1, 0:T], in_=R3[0:1, 0:T], func=AF.Exp, scale=-0.5), "ac")
        on_s(_)
        mark("afs_rr")

        def _(v):
            wof(v, "afs_rr", "ac")
            v.tensor_scalar(out=R4[0:1, 0:T], in0=R1[0:1, 0:T], scalar1=1.0, scalar2=None, op0=OP.add)
            inc(v.tensor_tensor(out=R2[0:1, 0:T], in0=R2[0:1, 0:T], in1=R1[0:1, 0:T], op=OP.mult), "dv")
        on_v(_)
        mark("acmb_rows")

        for i in range(ET):
            buf = (cb1, cb2)[i % 2]

            def _(sync, i=i, buf=buf):
                if i >= 2:
                    [sync.wait_ge(SEM[_s], marks[f"acmb_pk_{i-2}"][_s]) for _s in POOL["g"]]
                dma(sync, buf[:], xT_t[i], "ld")
            on_sync(_)
            mark(f"acmb_ld_{i}")

            def _(t, i=i):
                wof(t, "acmb_rows", "dv")
                wof(t, f"acmb_ld_{i}", "ld")
                if i >= 1:
                    t.wait_ge(SEM["dv"], marks[f"acmb_o_{i-1}"]["dv"])
                    t.wait_ge(SEM["ac"], marks[f"acmb_w_{i-1}"]["ac"])
                last = None
                for c in range(TC):
                    sl = slice(CH * c, CH * c + CH)
                    t.matmul(psA[:, sl], onfr_s[0:1, 0:D], R3[0:1, sl], start=True, stop=True)
                    last = t.matmul(psB[:, sl], onfr_s[0:1, 0:D], R4[0:1, sl], start=True, stop=True)
                inc(last, "mm")
            on_t(_)
            mark(f"acmb_bc1_{i}")

            def _(v, i=i, buf=buf):
                wof(v, f"acmb_bc1_{i}", "mm")
                for c in range(TC):
                    sl = slice(CH * c, CH * c + CH)
                    v.tensor_tensor(out=csq[:, sl], in0=AFl[:, T * i + CH * c:T * i + CH * c + CH],
                                    in1=psA[:, sl], op=OP.mult)
                last = None
                for c in range(TC):
                    sl = slice(CH * c, CH * c + CH)
                    last = v.tensor_tensor(out=buf[:, sl], in0=buf[:, sl], in1=psB[:, sl], op=OP.mult)
                inc(last, "dv")
            on_v(_)
            mark(f"acmb_s1_{i}")

            def _(sc, i=i):
                wof(sc, f"acmb_s1_{i}", "dv")
                inc(sc.activation(out=cw1[:], in_=csq[:], func=AF.Copy, scale=nrc_s[:, i:i + 1]), "ac")
            on_s(_)
            mark(f"acmb_w_{i}")

            def _(t, i=i):
                wof(t, f"acmb_s1_{i}", "dv")
                last = None
                for c in range(TC):
                    sl = slice(CH * c, CH * c + CH)
                    last = t.matmul(psA[:, sl], onfr_s[0:1, 0:D], R2[0:1, sl], start=True, stop=True)
                inc(last, "mm")
            on_t(_)
            mark(f"acmb_bc2_{i}")

            def _(v, i=i, buf=buf):
                wof(v, f"acmb_bc2_{i}", "mm")
                wof(v, f"acmb_w_{i}", "ac")
                for c in range(TC):
                    sl = slice(CH * c, CH * c + CH)
                    v.tensor_tensor(out=buf[:, sl], in0=buf[:, sl], in1=psA[:, sl], op=OP.subtract)
                last = None
                for c in range(TC):
                    sl = slice(CH * c, CH * c + CH)
                    last = v.tensor_tensor(out=buf[:, sl], in0=buf[:, sl], in1=cw1[:, sl], op=OP.add)
                inc(last, "dv")
            on_v(_)
            mark(f"acmb_o_{i}")

            def _(g, i=i, buf=buf):
                wof(g, f"acmb_o_{i}", "dv")
                dma(g, xpa_t[i], buf[:], "g")
            on_g(_)
            mark(f"acmb_pk_{i}")
        mark("p2_done")
        co.close()
        p2s.close()

    def ffn(call):
        f = f"f{call}"

        qgate()
        lnb = ExitStack()
        LNX = lnb.enter_context(nc.sbuf_tensor(f"{f}lnx", [D, ET * T], BF16))
        pre = ExitStack()
        sb1 = pre.enter_context(nc.sbuf_tensor(f"{f}s1", [D, T], F32))
        sb2 = pre.enter_context(nc.sbuf_tensor(f"{f}s2", [D, T], F32))
        sq = pre.enter_context(nc.sbuf_tensor(f"{f}sq", [D, T], F32))
        if call == 1:
            stream_stats(f + "s", lambda i: xpa_t[i], sb1, sb2, sq)
        else:
            stream_stats(f + "s", lambda i: xpb_t[i], sb1, sb2, sq,
                         add_fn=lambda i: yp_t[i], park_fn=lambda i: xpa_t[i])
        ln_rows(f + "s", kind="ln")

        qgate()
        emit_lnx(f + "l", f + "s_rows", lambda i: xpa_t[i], LNX, sb1, sb2, sq)
        pre.close()

        qgate()
        pa = ExitStack()
        PA = pa.enter_context
        wgt = [PA(nc.sbuf_tensor(f"{f}wg{b}", [D, ET * D], BF16)) for b in range(2)]
        wut = [PA(nc.sbuf_tensor(f"{f}wu{b}", [D, ET * D], BF16)) for b in range(2)]
        g1 = PA(nc.sbuf_tensor(f"{f}g1", [D, MA], F32))
        g2 = PA(nc.sbuf_tensor(f"{f}g2", [D, MA], F32))
        ib = [PA(nc.sbuf_tensor(f"{f}ib{b}", [D, MA], BF16)) for b in range(2)]

        for j in range(NT):
            bj = j % 2

            def _(sync, j=j, bj=bj):
                if j == 0:
                    wof(sync, "wag_u", "cc")
                if j >= 2:
                    sync.wait_ge(SEM["mm"], marks[f"{f}a_mm_{j-2}_1"]["mm"])
                dma(sync, wgt[bj].ap().rearrange("p (i f) -> p i f", f=D),
                    wg_all_t[j].rearrange("i p f -> p i f"), "ld")
                dma(sync, wut[bj].ap().rearrange("p (i f) -> p i f", f=D),
                    wu_all_t[j].rearrange("i p f -> p i f"), "ld")
            on_sync(_)
            mark(f"{f}a_wld_{j}")

            for th in range(2):
                def _(t, j=j, bj=bj, th=th):
                    wof(t, f"{f}a_wld_{j}", "ld")
                    if j == 0 and th == 0:
                        wof(t, f + "l_done", "dv")
                    if j >= 1:
                        t.wait_ge(SEM["dv"], marks[f"{f}a_int_{j-1}_{th}"]["dv"])
                    last = None
                    for i in range(ET):
                        for c2 in range(2):
                            gsl = slice(MA * th + CH * c2, MA * th + CH * c2 + CH)
                            rhs_sl = slice(T * i + MA * th + CH * c2, T * i + MA * th + CH * c2 + CH)
                            t.matmul(psA[:, gsl], wgt[bj][:, D * i:D * i + D], LNX[:, rhs_sl],
                                     start=(i == 0), stop=(i == ET - 1))
                            last = t.matmul(psB[:, gsl], wut[bj][:, D * i:D * i + D], LNX[:, rhs_sl],
                                            start=(i == 0), stop=(i == ET - 1))
                    inc(last, "mm")
                on_t(_)
                mark(f"{f}a_mm_{j}_{th}")

                def _(s, j=j, th=th):
                    wof(s, f"{f}a_mm_{j}_{th}", "mm")
                    if j + th >= 1:
                        pj, pt = (j, 0) if th == 1 else (j - 1, 1)
                        s.wait_ge(SEM["dv"], marks[f"{f}a_int_{pj}_{pt}"]["dv"])
                    inc(s.activation(out=g1[:], in_=psA[:, MA * th:MA * th + MA], func=AF.Square), "ac")
                on_s(_)
                mark(f"{f}a_sq_{j}_{th}")

                def _(v, j=j, th=th):
                    wof(v, f"{f}a_sq_{j}_{th}", "ac")
                    v.tensor_scalar(out=g2[:], in0=g1[:], scalar1=C2P, scalar2=C1P, op0=OP.mult, op1=OP.add)
                    inc(v.tensor_tensor(out=g1[:], in0=psA[:, MA * th:MA * th + MA], in1=g2[:], op=OP.mult), "dv")
                on_v(_)
                mark(f"{f}a_gel1_{j}_{th}")

                def _(s, j=j, th=th):
                    wof(s, f"{f}a_gel1_{j}_{th}", "dv")
                    inc(s.activation(out=g2[:], in_=g1[:], func=AF.Sigmoid), "ac")
                on_s(_)
                mark(f"{f}a_sig_{j}_{th}")

                def _(v, j=j, th=th):
                    wof(v, f"{f}a_sig_{j}_{th}", "ac")
                    v.tensor_tensor(out=g1[:], in0=psA[:, MA * th:MA * th + MA], in1=g2[:], op=OP.mult)
                    if j >= 1:
                        [v.wait_ge(SEM[_s], marks[f"{f}a_ipk_{j-1}_{th}"][_s]) for _s in POOL["g"]]
                    inc(v.tensor_tensor(out=ib[th][:], in0=g1[:], in1=psB[:, MA * th:MA * th + MA], op=OP.mult), "dv")
                on_v(_)
                mark(f"{f}a_int_{j}_{th}")

                def _(g, j=j, th=th):
                    wof(g, f"{f}a_int_{j}_{th}", "dv")
                    dma(g, interp[j, :, MA * th:MA * th + MA], ib[th][:], "g")
                on_g(_)
                mark(f"{f}a_ipk_{j}_{th}")
        mark(f + "a_done")
        pa.close()
        lnb.close()

        qgate()
        pbx = ExitStack()
        XBl = pbx.enter_context(nc.sbuf_tensor(f"{f}XB", [D, ET * T], F32))
        pbi = ExitStack()
        IB = pbi.enter_context(nc.sbuf_tensor(f"{f}IB", [D, 8 * T], BF16))
        wdt = [pbi.enter_context(nc.sbuf_tensor(f"{f}wd{b}", [D, 8 * D], BF16)) for b in range(2)]

        for i in range(ET):
            def _(sync, i=i):
                dma(sync, XBl[:, T * i:T * i + T], xpa_t[i], "ld")
            on_sync(_)
            mark(f"{f}b_xr_{i}")

        for g_ in range(16):
            def _(sync, g_=g_):
                if g_ == 0:
                    wof(sync, "wag_d", "cc")
                    wof(sync, f + "a_done", "g")
                if g_ >= 1:
                    sync.wait_ge(SEM["mm"], marks[f"{f}b_mm_{g_-1}_{ET-1}"]["mm"])
                src = interp[8 * g_:8 * g_ + 8, :, :].rearrange("j p t -> p j t")
                dma(sync, IB.ap().rearrange("p (j t) -> p j t", t=T), src, "ld")
            on_sync(_)
            mark(f"{f}b_ib_{g_}")

            for i in range(ET):
                bi = i % 2

                def _(sync, g_=g_, i=i, bi=bi):
                    if g_ > 0 or i >= 2:
                        pg, pi = (g_, i - 2) if i >= 2 else (g_ - 1, ET - 2 + i)
                        sync.wait_ge(SEM["mm"], marks[f"{f}b_mm_{pg}_{pi}"]["mm"])
                    ap = wd_all_t[8 * g_:8 * g_ + 8, :, D * i:D * i + D].rearrange("jj p f -> p jj f")
                    dma(sync, wdt[bi].ap().rearrange("p (jj f) -> p jj f", f=D), ap, "ld")
                on_sync(_)
                mark(f"{f}b_wd_{g_}_{i}")

                def _(t, g_=g_, i=i, bi=bi):
                    wof(t, f"{f}b_wd_{g_}_{i}", "ld")
                    wof(t, f"{f}b_ib_{g_}", "ld")
                    ps = (psA, psB)[i % 2]
                    if g_ > 0 or i >= 2:
                        pg, pi = (g_, i - 2) if i >= 2 else (g_ - 1, ET - 2 + i)
                        t.wait_ge(SEM["dv"], marks[f"{f}b_add_{pg}_{pi}"]["dv"])
                    last = None
                    for jj in range(8):
                        for c in range(TC):
                            sl = slice(CH * c, CH * c + CH)
                            last = t.matmul(ps[:, sl], wdt[bi][:, D * jj:D * jj + D],
                                            IB[:, T * jj + CH * c:T * jj + CH * c + CH],
                                            start=(jj == 0), stop=(jj == 7))
                    inc(last, "mm")
                on_t(_)
                mark(f"{f}b_mm_{g_}_{i}")

                def _(v, g_=g_, i=i):
                    wof(v, f"{f}b_mm_{g_}_{i}", "mm")
                    if g_ == 0:
                        wof(v, f"{f}b_xr_{i}", "ld")
                    ps = (psA, psB)[i % 2]
                    inc(v.tensor_tensor(out=XBl[:, T * i:T * i + T], in0=XBl[:, T * i:T * i + T],
                                        in1=ps[:, 0:T], op=OP.add), "dv")
                on_v(_)
                mark(f"{f}b_add_{g_}_{i}")
        mark(f + "b_done")
        pbi.close()
        return XBl, pbx

    def p4(XBl, pbx):
        pre = ExitStack()
        sq0 = pre.enter_context(nc.sbuf_tensor("b4sq", [D, T], F32))
        for i in range(ET):
            def _(g, i=i):
                if i == 0:
                    wof(g, "f1b_done", "dv")
                dma(g, xpb_t[i], XBl[:, T * i:T * i + T], "g")
            on_g(_)
            mark(f"b4pk_{i}")
        mark("b4pk_done")
        stream_stats("b4s", None, None, None, sq0, xb=XBl, xbmark=("f1b_done", "dv"))
        ln_rows("b4s", kind="ln")
        pre.close()

        def _(v):
            wof(v, "b4pk_done", "g")
            inc(v.tensor_scalar(out=R4[0:1, 0:4], in0=R4[0:1, 0:4], scalar1=1.0, scalar2=None, op0=OP.mult), "dv")
        on_v(_)
        mark("b4_xbfree")
        pbx.close()

        p4s = ExitStack()
        P4 = p4s.enter_context
        LNX = P4(nc.sbuf_tensor("blnx", [D, ET * T], BF16))
        pp4 = ExitStack()
        sb1 = pp4.enter_context(nc.sbuf_tensor("b4s1", [D, T], F32))
        sb2 = pp4.enter_context(nc.sbuf_tensor("b4s2", [D, T], F32))
        sqq = pp4.enter_context(nc.sbuf_tensor("b4sq2", [D, T], F32))

        def lnx_src(i):
            return xpb_t[i]

        def _(sync):
            wof(sync, "b4_xbfree", "dv")
        on_sync(_)
        qgate()
        emit_lnx("b4l", "b4s_rows", lnx_src, LNX, sb1, sb2, sqq)

        qgate()
        kvs = ExitStack()
        KX = kvs.enter_context
        wkb_b = KX(nc.sbuf_tensor("bwkb", [D, ET * KV * D], BF16))
        kvw = KX(nc.sbuf_tensor("bkvw", [D, T], BF16))

        for wq4 in range(4):
            def _(sync, wq4=wq4):
                if wq4 >= 1:
                    sync.wait_ge(SEM["ac"], marks[f"bkv_wcvt_{wq4-1}"]["ac"])
                dma(sync, sb1.ap().rearrange("p (i f) -> p i f", f=KV * D),
                    wkb[E // 4 * wq4:E // 4 * (wq4 + 1), :].rearrange("(i p) f -> p i f", p=D), "ld")
            on_sync(_)
            mark(f"bkv_wld_{wq4}")

            def _(s, wq4=wq4):
                wof(s, f"bkv_wld_{wq4}", "ld")
                if wq4 == 0:
                    wof(s, "b4l_done", "dv")
                inc(s.copy(out=wkb_b[:, 2048 * wq4:2048 * (wq4 + 1)], in_=sb1[:]), "ac")
            on_s(_)
            mark(f"bkv_wcvt_{wq4}")
        mark("bkv_wcvt")

        for kvh in range(KV):
            def _(t, kvh=kvh):
                wof(t, "bkv_wcvt", "ac")
                if kvh >= 1:
                    t.wait_ge(SEM["ac"], marks[f"bkv_km_{kvh-1}"]["ac"])
                last = None
                for i in range(ET):
                    for c in range(TC):
                        sl = slice(CH * c, CH * c + CH)
                        last = t.matmul(psA[:, sl], wkb_b[:, KV * D * i + D * kvh:KV * D * i + D * kvh + D],
                                        LNX[:, T * i + CH * c:T * i + CH * c + CH],
                                        start=(i == 0), stop=(i == ET - 1))
                inc(last, "mm")
            on_t(_)
            mark(f"bkv_kmm_{kvh}")

            def _(s, kvh=kvh):
                wof(s, f"bkv_kmm_{kvh}", "mm")
                inc(s.activation(out=sqq[:], in_=psA[:, 0:T], func=AF.Square), "ac")
            on_s(_)
            mark(f"bkv_ksq_{kvh}")

            def _(t, kvh=kvh):
                wof(t, f"bkv_ksq_{kvh}", "ac")
                last = None
                for c in range(TC):
                    sl = slice(CH * c, CH * c + CH)
                    last = t.matmul(psB[0:1, sl], onfc_s[:], sqq[:, sl], start=True, stop=True)
                inc(last, "mm")
            on_t(_)
            mark(f"bkv_ksmm_{kvh}")

            def _(s, kvh=kvh):
                wof(s, f"bkv_ksmm_{kvh}", "mm")
                s.activation(out=R1[0:1, 0:T], in_=psB[0:1, 0:T], func=AF.Ln, scale=1.0 / D, bias=kc_s[0:1, 2:3])
                inc(s.activation(out=R1[0:1, 0:T], in_=R1[0:1, 0:T], func=AF.Exp, scale=-0.5), "ac")
            on_s(_)
            mark(f"bkv_krs_{kvh}")

            for c in range(TC):
                def _(t, kvh=kvh, c=c):
                    wof(t, f"bkv_krs_{kvh}", "ac")
                    if c >= 1:
                        t.wait_ge(SEM["dv"], marks[f"bkv_kn_{kvh}_{c-1}"]["dv"])
                    sl = slice(CH * c, CH * c + CH)
                    inc(t.matmul(psB[:, 512:1024], onfr_s[0:1, 0:D], R1[0:1, sl], start=True, stop=True), "mm")
                on_t(_)
                mark(f"bkv_kbc_{kvh}_{c}")

                def _(s, kvh=kvh, c=c):
                    wof(s, f"bkv_kbc_{kvh}_{c}", "mm")
                    inc(s.copy(out=sqq[:, 0:CH], in_=psB[:, 512:1024]), "ac")
                on_s(_)
                mark(f"bkv_kbs_{kvh}_{c}")

                def _(v, kvh=kvh, c=c):
                    wof(v, f"bkv_kbs_{kvh}_{c}", "ac")
                    if c == 0 and kvh >= 1:
                        wof(v, f"bkv_kpk_{kvh-1}", "g")
                    sl = slice(CH * c, CH * c + CH)
                    v.tensor_tensor(out=sqq[:, CH:2 * CH], in0=psA[:, sl], in1=sqq[:, 0:CH], op=OP.mult)
                    inc(v.tensor_scalar(out=kvw[:, sl], in0=sqq[:, CH:2 * CH], scalar1=gk_s[:], scalar2=None,
                                        op0=OP.mult), "dv")
                on_v(_)
                mark(f"bkv_kn_{kvh}_{c}")

            def _(g, kvh=kvh):
                wof(g, f"bkv_kn_{kvh}_{TC-1}", "dv")
                dma(g, kin_t[kvh], kvw[:], "g")
            on_g(_)
            mark(f"bkv_kpk_{kvh}")

            def _(s, kvh=kvh):
                wof(s, f"bkv_kn_{kvh}_{TC-1}", "dv")
                inc(s.activation(out=sqq[:], in_=kvw[:], func=AF.Square), "ac")
            on_s(_)
            mark(f"bkv_kev_{kvh}")

            def _(t, kvh=kvh):
                wof(t, f"bkv_kev_{kvh}", "ac")
                last = None
                for c in range(TC):
                    sl = slice(CH * c, CH * c + CH)
                    last = t.matmul(psB[32:33, sl], onfc_s[:], sqq[:, sl], start=True, stop=True)
                inc(last, "mm")
            on_t(_)
            mark(f"bkv_kn2_{kvh}")

            def _(v, kvh=kvh):
                wof(v, f"bkv_kn2_{kvh}", "mm")
                inc(v.tensor_reduce(out=R3[0:1, 0:1], in_=psB[32:33, 0:T], axis=mybir.AxisListType.X,
                                    op=OP.max), "dv")
            on_v(_)
            mark(f"bkv_kmx_{kvh}")

            def _(s, kvh=kvh):
                wof(s, f"bkv_kmx_{kvh}", "dv")
                s.activation(out=R3[0:1, 1:2], in_=R3[0:1, 0:1], func=AF.Ln, bias=kc_s[0:1, 4:5])
                inc(s.activation(out=km_s[0:1, kvh:kvh + 1], in_=R3[0:1, 1:2], func=AF.Exp, scale=0.5), "ac")
            on_s(_)
            mark(f"bkv_km_{kvh}")
        kvs.close()

        qgate()
        vvs = ExitStack()
        wvb_b = vvs.enter_context(nc.sbuf_tensor("bwvb", [D, ET * KV * D], BF16))
        vst = vvs.enter_context(nc.sbuf_tensor("bvst", [D, T], BF16))

        for wq4 in range(4):
            def _(sync, wq4=wq4):
                if wq4 >= 1:
                    sync.wait_ge(SEM["ac"], marks[f"bvv_wcvt_{wq4-1}"]["ac"])
                dma(sync, sb2.ap().rearrange("p (i f) -> p i f", f=KV * D),
                    wvb[E // 4 * wq4:E // 4 * (wq4 + 1), :].rearrange("(i p) f -> p i f", p=D), "ld")
            on_sync(_)
            mark(f"bvv_wld_{wq4}")

            def _(s, wq4=wq4):
                wof(s, f"bvv_wld_{wq4}", "ld")
                if wq4 == 0:
                    wof(s, f"bkv_km_{KV-1}", "ac")
                inc(s.copy(out=wvb_b[:, 2048 * wq4:2048 * (wq4 + 1)], in_=sb2[:]), "ac")
            on_s(_)
            mark(f"bvv_wcvt_{wq4}")
        mark("bvv_wcvt")

        for mt in range(ET):
            def _(t, mt=mt):
                wof(t, "bvv_wcvt", "ac")
                if mt >= 2:
                    t.wait_ge(SEM["ac"], marks[f"bvv_ev_{mt-2}"]["ac"])
                ps = (psA, psB)[mt % 2]
                last = None
                for i in range(ET):
                    last = t.matmul(ps[:, 0:KV * D], LNX[:, T * i + D * mt:T * i + D * mt + D],
                                    wvb_b[:, KV * D * i:KV * D * i + KV * D],
                                    start=(i == 0), stop=(i == ET - 1))
                inc(last, "mm")
            on_t(_)
            mark(f"bvv_mm_{mt}")

            def _(s, mt=mt):
                wof(s, f"bvv_mm_{mt}", "mm")
                if mt % 4 == 0 and mt >= 4:
                    [s.wait_ge(SEM[_s], marks[f"bvv_pk_{mt//4-1}"][_s]) for _s in POOL["g"]]
                ps = (psA, psB)[mt % 2]
                inc(s.copy(out=vst[:, CH * (mt % 4):CH * (mt % 4) + CH], in_=ps[:, 0:KV * D]), "ac")
            on_s(_)
            mark(f"bvv_ev_{mt}")

            if mt % 4 == 3:
                def _(g, mt=mt):
                    wof(g, f"bvv_ev_{mt}", "ac")
                    dst = vag_in[D * (mt - 3):D * (mt + 1), :].rearrange("(m p) f -> p m f", p=D)
                    dma(g, dst, vst.ap().rearrange("p (m f) -> p m f", f=KV * D), "g")
                on_g(_)
                mark(f"bvv_pk_{mt//4}")
        mark("bkv_done")
        vvs.close()
        pp4.close()

        def _(g):
            wof(g, "bkv_done", "g")
            g.wait_ge(SEM["cc"], C["cc"])
            inc(g.collective_compute("AllGather", OP.bypass,
                                     replica_groups=[[0, 1], [2, 3], [4, 5], [6, 7]],
                                     ins=[kag_in.opt()], outs=[kag_out.opt()]), "cc")
            g.wait_ge(SEM["cc"], C["cc"])
            inc(g.collective_compute("AllGather", OP.bypass,
                                     replica_groups=[[0, 1], [2, 3], [4, 5], [6, 7]],
                                     ins=[vag_in.opt()], outs=[vag_out.opt()]), "cc")
        on_g(_)
        mark("kvag")

        qgate()
        at = ExitStack()
        AT = at.enter_context
        mtri_s = AT(nc.sbuf_tensor("mtri_s", [D, 4 * CH], F32))
        AO = AT(nc.sbuf_tensor("bAO", [D, H * HT], BF16))
        wqs = AT(nc.sbuf_tensor("bwqs", [D, ET * D], F32))
        wqh = [AT(nc.sbuf_tensor(f"bwqh{b}", [D, ET * D], BF16)) for b in range(2)]
        qbb = AT(nc.sbuf_tensor("bqb", [D, HT], BF16))
        klo = AT(nc.sbuf_tensor("bklo", [D, T], BF16))
        kpr = AT(nc.sbuf_tensor("bkpr", [D, T], BF16))
        vlo = AT(nc.sbuf_tensor("bvlo", [D, ET * D], BF16))
        vpr = AT(nc.sbuf_tensor("bvpr", [D, ET * D], BF16))
        clo = AT(nc.sbuf_tensor("bclo", [D, CH], F32))
        cpr = AT(nc.sbuf_tensor("bcpr", [D, CH], F32))
        sub = [AT(nc.sbuf_tensor(f"bsub{b}", [D, CH], F32)) for b in range(4)]
        pbu = [AT(nc.sbuf_tensor(f"bpb{b}", [D, CH], BF16)) for b in range(4)]
        rbb = AT(nc.sbuf_tensor("brb", [D, CH], F32))
        sqh = AT(nc.sbuf_tensor("bsqh", [D, HT], F32))
        yts = [AT(nc.sbuf_tensor(f"byts{b}", [D, HT], F32)) for b in range(2)]

        def _(sync):
            dma(sync, mtri_s[:], mtri, "ld")
        on_sync(_)
        mark("bat_mtld")

        def _(v):
            wof(v, "bat_mtld", "ld")
            inc(v.tensor_scalar(out=R4[0:1, 4:8], in0=R4[0:1, 4:8], scalar1=1.0, scalar2=None, op0=OP.mult), "dv")
        on_v(_)
        mark("bat_mtok")

        for th in range(2):
            hsl = slice(HT * th, HT * th + HT)
            for h in range(H):
                kvh = h // 4
                bh = h % 2
                ht = f"b{th}_{h}"
                if h % 4 == 0:
                    def _(sync, kvh=kvh, th=th, h=h):
                        wof(sync, "kvag", "cc")
                        if th > 0 or h > 0:
                            pth, ph = (th, h - 1) if h > 0 else (0, H - 1)
                            sync.wait_ge(SEM["mm"], marks[f"b{pth}_{ph}_alast"]["mm"])
                        dma(sync, klo[:], kin_t[kvh], "ld")
                        dma(sync, kpr[:], kout_t[0, kvh], "ld")
                        dma(sync, vlo.ap().rearrange("p (m f) -> p m f", f=D),
                            vag_in[:, D * kvh:D * kvh + D].rearrange("(m p) f -> p m f", p=D), "ld")
                        dma(sync, vpr.ap().rearrange("p (m f) -> p m f", f=D),
                            vag_out[0:T, D * kvh:D * kvh + D].rearrange("(m p) f -> p m f", p=D), "ld")
                    on_sync(_)
                    mark(f"b{th}_kvld_{kvh}")

                def _(sync, h=h, ht=ht, th=th):
                    if h >= 1:
                        sync.wait_ge(SEM["dv"], marks[f"b{th}_{h-1}_wcv"]["dv"])
                    elif th > 0:
                        sync.wait_ge(SEM["dv"], marks[f"bop{th-1}_{ET-1}_wcv"]["dv"])
                    dma(sync, wqs.ap().rearrange("p (i f) -> p i f", f=D),
                        wqb[:, D * h:D * h + D].rearrange("(i p) f -> p i f", p=D), "ld")
                on_sync(_)
                mark(f"{ht}_wld")

                def _(v, ht=ht, bh=bh, th=th, h=h):
                    wof(v, f"{ht}_wld", "ld")
                    if h >= 2:
                        v.wait_ge(SEM["mm"], marks[f"b{th}_{h-2}_qmm"]["mm"])
                    elif th > 0:
                        v.wait_ge(SEM["mm"], marks[f"bop{th-1}_{ET-2+h}_mm"]["mm"])
                    v.tensor_copy(out=wqh[bh][:], in_=wqs[:])
                on_v(_)
                mark(f"{ht}_wcv")

                def _(t, ht=ht, bh=bh, th=th, h=h, hsl=hsl):
                    wof(t, f"{ht}_wcv", "dv")
                    if th == 0 and h == 0:
                        wof(t, "bkv_done", "g")
                    else:
                        pth, ph = (th, h - 1) if h >= 1 else (0, H - 1)
                        t.wait_ge(SEM["dv"], marks[f"b{pth}_{ph}_qn_1"]["dv"])
                    if th > 0 and h == 0:
                        t.wait_ge(SEM["ac"], marks[f"bop{th-1}_{ET-1}_ev"]["ac"])
                    last = None
                    for i in range(ET):
                        for c2 in range(2):
                            qsl = slice(CH * c2, CH * c2 + CH)
                            last = t.matmul(psB[:, qsl], wqh[bh][:, D * i:D * i + D],
                                            LNX[:, T * i + HT * th + CH * c2:T * i + HT * th + CH * c2 + CH],
                                            start=(i == 0), stop=(i == ET - 1))
                    inc(last, "mm")
                on_t(_)
                mark(f"{ht}_qmm")

                def _(s, ht=ht):
                    wof(s, f"{ht}_qmm", "mm")
                    inc(s.activation(out=sqh[:], in_=psB[:, 0:HT], func=AF.Square), "ac")
                on_s(_)
                mark(f"{ht}_qsq")

                def _(t, ht=ht, th=th, h=h):
                    wof(t, f"{ht}_qsq", "ac")
                    if th > 0 or h > 0:
                        pth, ph = (th, h - 1) if h >= 1 else (0, H - 1)
                        t.wait_ge(SEM["dv"], marks[f"b{pth}_{ph}_cr"]["dv"])
                    last = None
                    for c2 in range(2):
                        qsl = slice(CH * c2, CH * c2 + CH)
                        last = t.matmul(psA[0:1, qsl], onfc_s[:], sqh[:, qsl], start=True, stop=True)
                    inc(last, "mm")
                on_t(_)
                mark(f"{ht}_qs2")

                def _(s, ht=ht):
                    wof(s, f"{ht}_qs2", "mm")
                    s.activation(out=R1[0:1, 0:HT], in_=psA[0:1, 0:HT], func=AF.Ln, scale=1.0 / D, bias=kc_s[0:1, 2:3])
                    inc(s.activation(out=R1[0:1, 0:HT], in_=R1[0:1, 0:HT], func=AF.Exp, scale=-0.5), "ac")
                on_s(_)
                mark(f"{ht}_qrs")

                for c2 in range(2):
                    def _(t, ht=ht, c2=c2):
                        wof(t, f"{ht}_qrs", "ac")
                        if c2 >= 1:
                            t.wait_ge(SEM["dv"], marks[f"{ht}_qn_{c2-1}"]["dv"])
                        qsl = slice(CH * c2, CH * c2 + CH)
                        inc(t.matmul(psA[:, 512:1024], onfr_s[0:1, 0:D], R1[0:1, qsl], start=True, stop=True), "mm")
                    on_t(_)
                    mark(f"{ht}_qbc_{c2}")

                    def _(v, ht=ht, c2=c2):
                        wof(v, f"{ht}_qbc_{c2}", "mm")
                        v.tensor_copy(out=sqh[:, 0:CH], in_=psA[:, 512:1024])
                    on_v(_)
                    mark(f"{ht}_qbs_{c2}")

                    def _(v, ht=ht, c2=c2):
                        wof(v, f"{ht}_qbs_{c2}", "dv")
                        qsl = slice(CH * c2, CH * c2 + CH)
                        v.tensor_tensor(out=sqh[:, CH:2 * CH] if False else sqh[:, CH * c2:CH * c2 + CH],
                                        in0=psB[:, qsl], in1=sqh[:, 0:CH], op=OP.mult) if False else None
                        v.tensor_tensor(out=rbb[:], in0=psB[:, qsl], in1=sqh[:, 0:CH], op=OP.mult)
                        inc(v.tensor_scalar(out=qbb[:, qsl], in0=rbb[:], scalar1=gqsc_s[:], scalar2=None,
                                            op0=OP.mult), "dv")
                    on_v(_)
                    mark(f"{ht}_qn_{c2}")

                def _(s, ht=ht):
                    wof(s, f"{ht}_qn_1", "dv")
                    inc(s.activation(out=sqh[:], in_=qbb[:], func=AF.Square), "ac")
                on_s(_)
                mark(f"{ht}_cq1")

                def _(t, ht=ht):
                    wof(t, f"{ht}_cq1", "ac")
                    last = None
                    for c2 in range(2):
                        qsl = slice(CH * c2, CH * c2 + CH)
                        last = t.matmul(psA[32:33, qsl], onfc_s[:], sqh[:, qsl], start=True, stop=True)
                    inc(last, "mm")
                on_t(_)
                mark(f"{ht}_cq2")

                def _(s, ht=ht):
                    wof(s, f"{ht}_cq2", "mm")
                    s.activation(out=R3[0:1, 0:HT], in_=psA[32:33, 0:HT], func=AF.Ln, bias=kc_s[0:1, 4:5])
                    inc(s.activation(out=R3[0:1, 0:HT], in_=R3[0:1, 0:HT], func=AF.Exp, scale=0.5), "ac")
                on_s(_)
                mark(f"{ht}_cq3")

                def _(v, ht=ht, kvh=kvh):
                    wof(v, f"{ht}_cq3", "ac")
                    v.tensor_scalar(out=R3[0:1, 0:HT], in0=R3[0:1, 0:HT], scalar1=km_s[0:1, kvh:kvh + 1],
                                    scalar2=None, op0=OP.mult)
                    inc(v.tensor_scalar(out=R3[0:1, HT:2 * HT], in0=R3[0:1, 0:HT], scalar1=pm_s[0:1, 0:1],
                                        scalar2=None, op0=OP.add), "dv")
                on_v(_)
                mark(f"{ht}_cr")

                for tcl in range(2):
                    tc = 2 * th + tcl
                    tsl = slice(CH * tcl, CH * tcl + CH)
                    tag = f"{ht}_{tcl}"

                    def _(t, ht=ht, tag=tag, tcl=tcl):
                        if tcl == 0:
                            wof(t, f"{ht}_cr", "dv")
                        else:
                            wof(t, f"{ht}_0_aoev", "dv")
                        t.matmul(psA[:, 1024:1536], onfr_s[0:1, 0:D], R3[0:1, CH * tcl:CH * tcl + CH],
                                 start=True, stop=True)
                        inc(t.matmul(psA[:, 1536:2048], onfr_s[0:1, 0:D], R3[0:1, HT + CH * tcl:HT + CH * tcl + CH],
                                     start=True, stop=True), "mm")
                    on_t(_)
                    mark(f"{tag}_cbc")

                    def _(v, tag=tag):
                        wof(v, f"{tag}_cbc", "mm")
                        v.tensor_copy(out=clo[:], in_=psA[:, 1024:1536])
                        v.tensor_copy(out=cpr[:], in_=psA[:, 1536:2048])
                    on_v(_)
                    mark(f"{tag}_cbs")

                    msteps = [("p", mt) for mt in range(ET)] + [("l", mt) for mt in range(4 * tc + 4)]
                    ns = len(msteps)
                    ADL = 3
                    ADEP = 4
                    mm0, dv0, ac0 = C["mm"], C["dv"], C["ac"]
                    diags = [(src == "l" and mt >= 4 * tc) for src, mt in msteps]
                    cum = []
                    tot = 0
                    for dgf in diags:
                        tot += 2 if dgf else 1
                        cum.append(tot)

                    def posc(si):
                        return mm0 + si + 1 + max(0, si - ADL)

                    def posd(j):
                        return mm0 + min(j + ADL + 1, ns) + j + 1

                    def _(s, ns=ns, mm0=mm0, dv0=dv0, cum=cum, posd=posd):
                        for si in range(ns):
                            pp = si % ADEP
                            s.wait_ge(SEM["dv"], dv0 + cum[si])
                            s.wait_ge(SEM["mm"], posd(si - ADEP) if si >= ADEP else mm0)
                            s.activation(out=pbu[pp][:], in_=sub[pp][:],
                                         func=AF.Exp).then_inc(SEM["ac"], 1)
                    blk.scalar(_)

                    def _(v, msteps=msteps, tag=tag, tc=tc, ns=ns, mm0=mm0, dv0=dv0,
                          ac0=ac0, cum=cum, diags=diags, posc=posc):
                        for si, (src, mt) in enumerate(msteps):
                            pp = si % ADEP
                            if si == 0:
                                wof(v, f"{tag}_cbs", "dv")
                            v.wait_ge(SEM["mm"], posc(si))
                            v.wait_ge(SEM["ac"], ac0 + si - ADEP + 1 if si >= ADEP else ac0)
                            ce = cpr if src == "p" else clo
                            i1 = v.tensor_tensor(out=sub[pp][:], in0=psA[:, CH * pp:CH * pp + CH],
                                                 in1=ce[:], op=OP.subtract)
                            i1.then_inc(SEM["dv"], 1)
                            if diags[si]:
                                r = mt - 4 * tc
                                v.wait_ge(SEM["dv"], dv0 + cum[si] - 1)
                                v.tensor_tensor(out=sub[pp][:], in0=sub[pp][:],
                                                in1=mtri_s[:, CH * r:CH * r + CH],
                                                op=OP.add).then_inc(SEM["dv"], 1)
                    blk.vector(_)

                    def _(t, msteps=msteps, tag=tag, tsl=tsl, ns=ns, dv0=dv0, ac0=ac0,
                          cum=cum):
                        for si in range(ns + ADL):
                            if si < ns:
                                src, mt = msteps[si]
                                pp = si % ADEP
                                if si == 0:
                                    wof(t, f"{tag}_cbs", "dv")
                                t.wait_ge(SEM["dv"], dv0 + cum[si - ADEP] if si >= ADEP else dv0)
                                kbuf = kpr if src == "p" else klo
                                t.matmul(psA[:, CH * pp:CH * pp + CH], kbuf[:, D * mt:D * mt + D],
                                         qbb[:, tsl], start=True, stop=True).then_inc(SEM["mm"], 1)
                            if si >= ADL:
                                j = si - ADL
                                jsrc, jmt = msteps[j]
                                jp = j % ADEP
                                t.wait_ge(SEM["ac"], ac0 + j + 1)
                                vbuf = vpr if jsrc == "p" else vlo
                                t.matmul(psB[0:1, 1024:1536], onbc_s[:], pbu[jp][:],
                                         start=(j == 0), stop=(j == ns - 1))
                                t.matmul(psB[:, 1536:2048], vbuf[:, D * jmt:D * jmt + D], pbu[jp][:],
                                         start=(j == 0), stop=(j == ns - 1)).then_inc(SEM["mm"], 1)
                    on_t(_)
                    C["mm"] += 2 * ns
                    C["dv"] += cum[-1]
                    C["ac"] += ns
                    mark(f"{tag}_da_{ns-1}")

                    def _(v, tag=tag, ns=ns):
                        wof(v, f"{tag}_da_{ns-1}", "mm")
                        v.tensor_scalar(out=R4[0:1, 0:CH], in0=psB[0:1, 1024:1536], scalar1=1.0,
                                        scalar2=None, op0=OP.mult)
                        inc(v.reciprocal(out=R4[0:1, 0:CH], in_=R4[0:1, 0:CH]), "dv")
                    on_v(_)
                    mark(f"{tag}_rd")

                    def _(t, tag=tag):
                        wof(t, f"{tag}_rd", "dv")
                        inc(t.matmul(psA[:, 1024:1536], onfr_s[0:1, 0:D], R4[0:1, 0:CH], start=True, stop=True), "mm")
                    on_t(_)
                    mark(f"{tag}_rbc")

                    def _(v, tag=tag):
                        wof(v, f"{tag}_rbc", "mm")
                        v.tensor_copy(out=rbb[:], in_=psA[:, 1024:1536])
                    on_v(_)
                    mark(f"{tag}_rbs")

                    def _(v, tag=tag, h=h, tcl=tcl, th=th):
                        if th > 0 and h == 0 and tcl == 0:
                            v.wait_ge(SEM["mm"], marks[f"bop{th-1}_{ET-1}_mm"]["mm"])
                        inc(v.tensor_tensor(out=AO[:, HT * h + CH * tcl:HT * h + CH * tcl + CH],
                                            in0=psB[:, 1536:2048], in1=rbb[:], op=OP.mult), "dv")
                    on_v(_)
                    mark(f"{tag}_aoev")
                mark(f"b{th}_{h}_alast")
            mark(f"bat{th}_done")

            for i in range(ET):
                bi = i % 2
                ot = f"bop{th}_{i}"

                def _(sync, i=i, ot=ot, th=th):
                    if i >= 1:
                        sync.wait_ge(SEM["dv"], marks[f"bop{th}_{i-1}_wcv"]["dv"])
                    else:
                        sync.wait_ge(SEM["dv"], marks[f"b{th}_{H-1}_wcv"]["dv"])
                    dma(sync, wqs.ap().rearrange("p (i f) -> p i f", f=D),
                        wob[:, D * i:D * i + D].rearrange("(k p) f -> p k f", p=D), "ld")
                on_sync(_)
                mark(f"{ot}_wld")

                def _(v, ot=ot, bi=bi, th=th, i=i):
                    wof(v, f"{ot}_wld", "ld")
                    if th > 0 or i >= 2:
                        pth, pi = (th, i - 2) if i >= 2 else (0, ET - 2 + i)
                        v.wait_ge(SEM["mm"], marks[f"bop{pth}_{pi}_mm"]["mm"])
                    v.tensor_copy(out=wqh[bi][:], in_=wqs[:])
                on_v(_)
                mark(f"{ot}_wcv")

                def _(t, ot=ot, bi=bi, th=th, i=i):
                    wof(t, f"{ot}_wcv", "dv")
                    if i == 0:
                        wof(t, f"bat{th}_done", "dv")
                    ps = (psA, psB)[i % 2]
                    if th > 0 or i >= 2:
                        pth, pi = (th, i - 2) if i >= 2 else (0, ET - 2 + i)
                        t.wait_ge(SEM["ac"], marks[f"bop{pth}_{pi}_ev"]["ac"])
                    last = None
                    for k in range(ET):
                        for c2 in range(2):
                            qsl = slice(CH * c2, CH * c2 + CH)
                            last = t.matmul(ps[:, qsl], wqh[bi][:, D * k:D * k + D],
                                            AO[:, HT * k + CH * c2:HT * k + CH * c2 + CH],
                                            start=(k == 0), stop=(k == ET - 1))
                    inc(last, "mm")
                on_t(_)
                mark(f"{ot}_mm")

                def _(s, ot=ot, bi=bi, th=th, i=i):
                    wof(s, f"{ot}_mm", "mm")
                    if th > 0 or i >= 2:
                        pth, pi = (th, i - 2) if i >= 2 else (0, ET - 2 + i)
                        [s.wait_ge(SEM[_s], marks[f"bop{pth}_{pi}_pk"][_s]) for _s in POOL["g"]]
                    ps = (psA, psB)[i % 2]
                    inc(s.copy(out=yts[bi][:], in_=ps[:, 0:HT]), "ac")
                on_s(_)
                mark(f"{ot}_ev")

                def _(g, ot=ot, i=i, th=th, bi=bi):
                    wof(g, f"{ot}_ev", "ac")
                    dma(g, yp_t[i][:, HT * th:HT * th + HT], yts[bi][:], "g")
                on_g(_)
                mark(f"{ot}_pk")
        mark("p4_done")
        at.close()
        p4s.close()

    def p6(xsrc_sbuf=None, dram_src=None):
        if xsrc_sbuf is not None:
            def _(v):
                inc(v.tensor_scalar(out=R4[0:1, 0:4], in0=R4[0:1, 0:4], scalar1=1.0,
                                    scalar2=None, op0=OP.mult), "dv")
            on_v(_)
            mark("p6_flush")

            def _(sync):
                wof(sync, "p6_flush", "dv")
                for i in range(ET):
                    dma(sync, yT_t[i], xsrc_sbuf[:, T * i:T * i + T], "st")
                for _s in POOL["st"]:
                    sync.wait_ge(SEM[_s], C[_s])
            on_sync(_)
        else:
            def _(sync):
                for q in ("ld", "g", "mm", "dv", "ac"):
                    for _s in POOL[q]:
                        sync.wait_ge(SEM[_s], C[_s])
                dma(sync, yT, dram_src, "st")
                for _s in POOL["st"]:
                    sync.wait_ge(SEM[_s], C[_s])
            on_sync(_)

    p0()
    if STAGE >= 2 and STAGE != 15:
        p1()
    p2()
    if STAGE in (0, 15):
        p6(dram_src=xparkA)
    elif STAGE == 1:
        p6(dram_src=xparkA)
    elif STAGE == 2:
        XBl, pbx = ffn(1)
        p6(xsrc_sbuf=XBl)
    else:
        XBl, pbx = ffn(1)
        p4(XBl, pbx)
        if STAGE == 3:
            qgate()
            st3 = ExitStack()
            s1 = st3.enter_context(nc.sbuf_tensor("st3a", [D, T], F32))
            s2 = st3.enter_context(nc.sbuf_tensor("st3b", [D, T], F32))
            s3 = st3.enter_context(nc.sbuf_tensor("st3c", [D, T], F32))
            stream_stats("x3s", lambda i: xpb_t[i], s1, s2, s3,
                         add_fn=lambda i: yp_t[i], park_fn=lambda i: xpa_t[i])
            st3.close()

            def _(sync):
                wof(sync, "x3s_stats", "mm")
                for _s in POOL["g"]:
                    sync.wait_ge(SEM[_s], C[_s])
                dma(sync, yT, xparkA, "st")
                for _s in POOL["st"]:
                    sync.wait_ge(SEM[_s], C[_s])
            on_sync(_)
        else:
            XB2, pbx2 = ffn(2)
            p6(xsrc_sbuf=XB2)

    return nc


_PE = perm256(E)
_PN = perm256(NI)


def _stage_inputs(inputs, core):
    ii = inputs
    p, hf = core // 2, core % 2
    tok = slice(T * hf, T * hf + T)
    f32 = np.float32

    x = np.asarray(ii["L0_attn_a_query_input"], f32)
    d = {}
    d["xT"] = np.ascontiguousarray(x[p, tok, :].T[_PE, :])
    d["winT"] = np.ascontiguousarray(np.asarray(ii["L0_attn_a_W_in"], f32).T[_PE, :])
    d["wqa"] = np.ascontiguousarray(np.asarray(ii["L0_attn_a_W_q"], f32).T)
    d["wka"] = np.ascontiguousarray(np.asarray(ii["L0_attn_a_W_k"], f32).T)
    dperm = np.concatenate([np.arange(0, D, 2), np.arange(1, D, 2)])
    d["wva"] = np.ascontiguousarray(np.asarray(ii["L0_attn_a_W_v"], f32).T[:, dperm])
    d["patT"] = np.ascontiguousarray(
        np.asarray(ii["L0_attn_a_stored_patterns"], f32).transpose(0, 2, 1).reshape(H * D, MA))
    d["nqw_c"] = np.ascontiguousarray(np.asarray(ii["L0_attn_a_norm_query_w"], f32)[_PE].reshape(ET, D).T)
    d["nrw_c"] = np.ascontiguousarray(np.asarray(ii["L0_attn_a_norm_retrieved_w"], f32)[_PE].reshape(ET, D).T)
    d["wqb"] = np.ascontiguousarray(np.asarray(ii["L0_attn_b_Wq"], f32)[_PE, :])
    d["wkb"] = np.ascontiguousarray(np.asarray(ii["L0_attn_b_Wk"], f32)[_PE, :])
    d["wvb"] = np.ascontiguousarray(np.asarray(ii["L0_attn_b_Wv"], f32)[_PE, :])
    d["wob"] = np.ascontiguousarray(np.asarray(ii["L0_attn_b_Wo"], f32)[:, _PE])
    d["gqv"] = np.ascontiguousarray(np.asarray(ii["L0_attn_b_gq"], f32)[:, None])
    d["gkv"] = np.ascontiguousarray(np.asarray(ii["L0_attn_b_gk"], f32)[:, None])

    def codes_u8(a):
        a = np.ascontiguousarray(np.asarray(a, np.int32))
        return a.view(np.uint8).reshape(a.shape[0], a.shape[1], 4)[:, :, 0]

    nsl = slice(T * core, T * core + T)
    gT = codes_u8(ii["L0_ffn_gate_W_packed"]).T[:, _PN]
    uT = codes_u8(ii["L0_ffn_up_W_packed"]).T[:, _PN]
    dT = codes_u8(ii["L0_ffn_down_W_packed"]).T[:, _PE]
    d["gcodes"] = np.ascontiguousarray(gT[:, nsl])
    d["ucodes"] = np.ascontiguousarray(uT[:, nsl])
    d["dcodes"] = np.ascontiguousarray(dT[1024 * core:1024 * core + 1024, :])

    def scl_rep(s, permc, rowsl, colsl):
        sT = np.asarray(s, f32).T[:, permc]
        return np.ascontiguousarray(np.repeat(sT, 16, axis=0)[rowsl, colsl])

    d["gscl"] = scl_rep(ii["L0_ffn_gate_scales"], _PN, slice(None), nsl)
    d["uscl"] = scl_rep(ii["L0_ffn_up_scales"], _PN, slice(None), nsl)
    d["dscl"] = scl_rep(ii["L0_ffn_down_scales"], _PE, slice(1024 * core, 1024 * core + 1024), slice(None))

    kcv = np.zeros((D, 8), f32)
    kcv[:, 0] = -2.0
    kcv[:, 1] = -6.0
    kcv[:, 2] = 1e-6
    kcv[:, 3] = 1e-5
    kcv[:, 4] = 1e-20
    d["kc"] = kcv
    d["beta_in"] = np.asarray(ii["L0_attn_a_beta"], f32).reshape(1, 1)
    d["pmask"] = np.full((1, 1), 1e30 if hf == 0 else 0.0, f32)
    mt = np.zeros((D, 4 * CH), f32)
    mrow = np.arange(D)[:, None]
    scol = np.arange(CH)[None, :]
    for r in range(4):
        mt[:, CH * r:CH * r + CH] = np.where(mrow <= scol - D * r, 0.0, -1e30)
    d["mtri"] = mt
    d["onesf_r"] = np.ones((D, D), f32)
    d["onesf_c"] = np.ones((D, 1), f32)
    d["onesb_c"] = np.ones((D, 1), ml_dtypes.bfloat16)
    return d


def kernel(**inputs):
    nc = build_program()
    in_maps = [_stage_inputs(inputs, c) for c in range(NCORE)]
    res = run_bass_kernel_spmd(nc, in_maps, core_ids=list(range(NCORE)))
    out = np.empty((4, 4096, E), np.float32)
    inv = np.empty(E, np.int64)
    inv[_PE] = np.arange(E)
    for c in range(NCORE):
        shard = np.asarray(res.results[c]["yT"], np.float32)
        p, hf = c // 2, c % 2
        out[p, T * hf:T * hf + T, :] = shard[inv, :].T
    return out
